# version 21
# baseline (speedup 1.0000x reference)
import numpy as np
import ml_dtypes
import concourse.bass as bass
import concourse.mybir as mybir
from concourse.bass_utils import run_bass_kernel_spmd

F32 = mybir.dt.float32
BF16 = mybir.dt.bfloat16
I32 = mybir.dt.int32
AF = mybir.ActivationFunctionType
ALU = mybir.AluOpType
AX = mybir.AxisListType

D = 0.99
SEG = 2048
PRE = 3072
GRP = 256
NG = SEG // GRP
NPT = PRE // 128
DM = 1024
NDM = DM // 128
NH, DH = 16, 64
NP = NH // 2
DFF = 4096
NDH = DFF // 128
EPS = 1e-5
MAGIC = 0x5f3759df
D256 = float(D ** 256)


class Prog:
    def __init__(self):
        self.ops = []
        self._prev_psums = ()
        self._opened = set()

    def psum_scope(self, prev_tokens):
        self._prev_psums = tuple(prev_tokens)
        self._opened = set()

    def add(self, eng, fn, reads=(), writes=(), dma=False):
        writes = list(writes)
        for t in list(writes):
            if t.startswith("p.") and t not in self._opened:
                self._opened.add(t)
                writes += self._prev_psums
        self.ops.append(dict(eng=eng, fn=fn, reads=tuple(reads),
                             writes=tuple(writes), dma=dma, idx=len(self.ops)))

    def emit(self, nc, block, sems, serial=False):
        ops = self.ops
        NSLOT = 8
        dcount = {}
        gcount = 0
        for op in ops:
            if op["dma"]:
                k = dcount.get(op["eng"], 0)
                dcount[op["eng"]] = k + 1
                op["slot"] = k % NSLOT
                op["semval"] = 16 * (k // NSLOT + 1)
                op["inc"] = 16
            elif op["eng"] == "gpsimd" and op["fn"] is not None:
                k = gcount
                gcount += 1
                op["slot"] = k % NSLOT
                op["semval"] = k // NSLOT + 1
                op["inc"] = 1
        last_w = {}
        readers = {}
        for op in ops:
            deps = set()
            if serial and op["idx"] > 0:
                deps.add(op["idx"] - 1)
            for t in op["reads"]:
                if t in last_w:
                    deps.add(last_w[t])
            for t in op["writes"]:
                if t in last_w:
                    deps.add(last_w[t])
                for r in readers.get(t, ()):
                    deps.add(r)
            deps.discard(op["idx"])
            fd = set()
            for d in deps:
                dop = ops[d]
                if (not dop["dma"] and not op["dma"]
                        and dop["eng"] == "tensor" and op["eng"] == "tensor"):
                    continue
                fd.add(d)
            bysrc = {}
            for d in fd:
                dop = ops[d]
                if dop["dma"] or dop["eng"] == "gpsimd":
                    key = (dop["eng"], dop["dma"], dop.get("slot"))
                else:
                    key = (dop["eng"],)
                if key not in bysrc or d > bysrc[key]:
                    bysrc[key] = d
            op["deps"] = sorted(bysrc.values())
            for t in op["reads"]:
                readers.setdefault(t, []).append(op["idx"])
            for t in op["writes"]:
                last_w[t] = op["idx"]
                readers[t] = []
        need_inc = set()
        for op in ops:
            for d in op["deps"]:
                if not ops[d]["dma"]:
                    need_inc.add(d)
        ccount = {}
        for op in ops:
            if op["dma"] or op["eng"] == "gpsimd":
                pass
            elif op["idx"] in need_inc:
                ccount[op["eng"]] = ccount.get(op["eng"], 0) + 1
                op["semval"] = ccount[op["eng"]]
                op["inc"] = 1
            else:
                op["semval"] = None
                op["inc"] = 0
        hw = {}

        def semkey(dop):
            if dop["dma"]:
                return "dma_%s_%d" % (dop["eng"], dop["slot"])
            if dop["eng"] == "gpsimd":
                return "gps_%d" % dop["slot"]
            return "c_" + dop["eng"]

        def make_section(name):
            def section(e):
                for op in ops:
                    if op["eng"] != name:
                        continue
                    for d in op["deps"]:
                        dop = ops[d]
                        sk = semkey(dop)
                        val = dop["semval"]
                        assert val is not None
                        if hw.get((name, sk), -1) >= val:
                            continue
                        e.wait_ge(sems[sk], val)
                        hw[(name, sk)] = val
                    if op["fn"] is not None:
                        if op["dma"] or op["eng"] == "gpsimd":
                            sk = semkey(op)
                            prev = op["semval"] - (16 if op["dma"] else 1)
                            if prev > 0 and hw.get((name, sk), -1) < prev:
                                e.wait_ge(sems[sk], prev)
                                hw[(name, sk)] = prev
                        ins = op["fn"](e)
                        if op["inc"]:
                            ins.then_inc(sems[semkey(op)], op["inc"])
                    else:
                        assert op["inc"] == 0
            return section

        for name in ["tensor", "vector", "scalar", "gpsimd", "sync"]:
            getattr(block, name)(make_section(name))



def _host_consts():
    c = {}
    c["ident"] = np.eye(128, dtype=ml_dtypes.bfloat16)
    c["ocf"] = np.ones((128, 1), dtype=np.float32)
    c["ocb"] = np.ones((128, 1), dtype=ml_dtypes.bfloat16)
    c["orb"] = np.ones((1, 128), dtype=ml_dtypes.bfloat16)
    c["orf"] = np.ones((1, 128), dtype=np.float32)
    mask = np.ones((128, 256), dtype=np.float32)
    tri = (np.arange(128)[None, :] >= np.arange(128)[:, None]).astype(np.float32)
    mask[:, 0:128] = tri
    c["mask"] = mask
    i = np.arange(GRP, dtype=np.float64)
    c["qdec"] = np.tile((D ** i / 8.0).astype(np.float32), (128, 1))
    c["kdec"] = np.tile((D ** (-i)).astype(np.float32), (128, 1))
    p = np.arange(128, dtype=np.float64)
    kus = np.stack([D ** (256 - (128 * s + p)) for s in range(2)], axis=1)
    c["kus"] = kus.astype(np.float32)
    return c



def build_nc(dbg=False, skip_p0=False):
    nc = bass.Bass("TRN2", target_bir_lowering=False)

    xseg = nc.dram_tensor("xseg", [128, (SEG // 128) * DM], F32, kind="ExternalInput")
    xpre = nc.dram_tensor("xpre", [128, NPT * DM], F32, kind="ExternalInput")
    cpre = nc.dram_tensor("cpre", [128, NPT], F32, kind="ExternalInput")
    csum = nc.dram_tensor("csum", [1, 1], F32, kind="ExternalInput")
    Wq = nc.dram_tensor("Wq", [128, NDM * DM], F32, kind="ExternalInput")
    Wk = nc.dram_tensor("Wk", [128, NDM * DM], F32, kind="ExternalInput")
    Wv = nc.dram_tensor("Wv", [128, NDM * DM], F32, kind="ExternalInput")
    Wg = nc.dram_tensor("Wg", [128, NDM * DM], F32, kind="ExternalInput")
    Wo = nc.dram_tensor("Wo", [128, NDM * DM], F32, kind="ExternalInput")
    w1 = nc.dram_tensor("w1", [128, NDM * DFF], F32, kind="ExternalInput")
    w2 = nc.dram_tensor("w2", [128, NDH * DM], F32, kind="ExternalInput")
    b_v = nc.dram_tensor("b_v", [1, DM], F32, kind="ExternalInput")
    b_k = nc.dram_tensor("b_k", [1, DM], F32, kind="ExternalInput")
    b_o = nc.dram_tensor("b_o", [1, DM], F32, kind="ExternalInput")
    b_2 = nc.dram_tensor("b_2", [1, DM], F32, kind="ExternalInput")
    bqT = nc.dram_tensor("bqT", [128, NDM], F32, kind="ExternalInput")
    bkT = nc.dram_tensor("bkT", [128, NDM], F32, kind="ExternalInput")
    bgT = nc.dram_tensor("bgT", [128, NDM], F32, kind="ExternalInput")
    b1T = nc.dram_tensor("b1T", [128, NDH], F32, kind="ExternalInput")
    gmT = nc.dram_tensor("gmT", [128, NDM], F32, kind="ExternalInput")
    btT = nc.dram_tensor("btT", [128, NDM], F32, kind="ExternalInput")
    out = nc.dram_tensor("out", [SEG, DM], F32, kind="ExternalOutput")
    s_scr = nc.dram_tensor("s_scr", [SEG, DM], F32, kind="Internal")
    w1b = nc.dram_tensor("w1b", [128, NDM * DFF], BF16, kind="Internal")
    w2b = nc.dram_tensor("w2b", [128, NDH * DM], BF16, kind="Internal")
    if dbg:
        d_S0 = nc.dram_tensor("d_S0", [128, 512], F32, kind="ExternalOutput")
        d_S1 = nc.dram_tensor("d_S1", [128, 512], F32, kind="ExternalOutput")
        d_xnT = nc.dram_tensor("d_xnT", [128, 16384], BF16, kind="ExternalOutput")
        d_Qs = nc.dram_tensor("d_Qs", [128, 2048], BF16, kind="ExternalOutput")
        d_Ks = nc.dram_tensor("d_Ks", [128, 2048], BF16, kind="ExternalOutput")
        d_V = nc.dram_tensor("d_V", [128, 2048], BF16, kind="ExternalOutput")
        d_Ku = nc.dram_tensor("d_Ku", [128, 2048], BF16, kind="ExternalOutput")
        d_Gt = nc.dram_tensor("d_Gt", [128, 2048], F32, kind="ExternalOutput")
        d_rsb = nc.dram_tensor("d_rsb", [128, 2048], F32, kind="ExternalOutput")
        d_As = nc.dram_tensor("d_As", [128, 2048], BF16, kind="ExternalOutput")
        d_s = nc.dram_tensor("d_s", [SEG, DM], F32, kind="ExternalOutput")
        d_hT = nc.dram_tensor("d_hT", [128, 4096], BF16, kind="ExternalOutput")
        d_Pm = nc.dram_tensor("d_Pm", [128, 256], BF16, kind="ExternalOutput")
        d_Wq = nc.dram_tensor("d_Wq", [128, 8192], BF16, kind="ExternalOutput")
        d_PJ = nc.dram_tensor("d_PJ", [128, 256], F32, kind="ExternalOutput")
        d_xnb = nc.dram_tensor("d_xnb", [128, 1024], BF16, kind="ExternalOutput")
        d_Wk = nc.dram_tensor("d_Wk", [128, 8192], BF16, kind="ExternalOutput")

    hc = _host_consts()
    ident_d = nc.inline_tensor(hc["ident"], name="identc")
    ocf_d = nc.inline_tensor(hc["ocf"], name="ocfc")
    ocb_d = nc.inline_tensor(hc["ocb"], name="ocbc")
    orb_d = nc.inline_tensor(hc["orb"], name="orbc")
    orf_d = nc.inline_tensor(hc["orf"], name="orfc")
    mask_d = nc.inline_tensor(hc["mask"], name="maskc")
    qdec_d = nc.inline_tensor(hc["qdec"], name="qdecc")
    kdec_d = nc.inline_tensor(hc["kdec"], name="kdecc")
    kus_d = nc.inline_tensor(hc["kus"], name="kusc")

    p = Prog()
    A = p.add

    from contextlib import ExitStack
    _st = ExitStack()
    def SB(name, shape, dtype):
        return _st.enter_context(nc.sbuf_tensor(name, shape, dtype))
    if True:
        WB1 = SB("WB1", [128, 32768], BF16)
        WB2 = SB("WB2", [128, 32768], BF16)
        xnT = SB("xnT", [128, 16384], BF16)
        xg01 = SB("xg01", [128, 2048], F32)
        xg23 = SB("xg23", [128, 2048], F32)
        xg0 = xg01[:, 0:1024]
        xg1 = xg01[:, 1024:2048]
        state = SB("state", [128, 512], F32)
        sb16 = SB("sb16", [128, 512], BF16)
        xnb = SB("xnb", [128, 1024], BF16)
        tA2F = SB("tA2F", [128, 256], F32)
        tB2F = SB("tB2F", [128, 256], F32)
        identS = SB("identS", [128, 128], BF16)
        ocfS = SB("ocfS", [128, 1], F32)
        orbS = SB("orbS", [1, 128], BF16)
        orfS = SB("orfS", [1, 128], F32)
        maskS = SB("maskS", [128, 256], F32)
        qdecS = SB("qdecS", [128, GRP], F32)
        kdecS = SB("kdecS", [128, GRP], F32)
        kusS = SB("kusS", [128, 2], F32)
        cpreS = SB("cpreS", [128, NPT], F32)
        cpreB = SB("cpreB", [128, NPT], BF16)
        csumS = SB("csumS", [1, 1], F32)
        bvB = SB("bvB", [1, DM], BF16)
        bkB = SB("bkB", [1, DM], BF16)
        boB = SB("boB", [1, DM], BF16)
        b2B = SB("b2B", [1, DM], BF16)
        bqTs = SB("bqTs", [128, NDM], F32)
        bkTs = SB("bkTs", [128, NDM], F32)
        bgTs = SB("bgTs", [128, NDM], F32)
        b1Ts = SB("b1Ts", [128, NDH], F32)
        gmTs = SB("gmTs", [128, NDM], F32)
        btTs = SB("btTs", [128, NDM], F32)
        mcolB = SB("mcolB", [128, NDM], BF16)
        vpe = SB("vpe", [128, 4], F32)
        nrY = SB("nrY", [128, 4], F32)
        nrT = SB("nrT", [128, 4], F32)
        bnst = SB("bnst", [128, 24], F32)
        mvc = SB("mvc", [128, 4], F32)
    if True:
        def wb1(c0, n):
            return WB1[:, c0:c0 + n]
        def wb2(c0, n):
            return WB2[:, c0:c0 + n]
        WkS = lambda c0, n: WB1[:, 0 + c0: 0 + c0 + n]
        WvS = lambda c0, n: WB1[:, 8192 + c0: 8192 + c0 + n]
        MS = lambda c0, n: WB1[:, 16384 + c0: 16384 + c0 + n]
        TS = lambda c0, n: WB1[:, 24576 + c0: 24576 + c0 + n]
        WqS = lambda c0, n: WB1[:, 16384 + c0: 16384 + c0 + n]
        WgS = lambda c0, n: WB1[:, 24576 + c0: 24576 + c0 + n]
        WoS = lambda c0, n: WB2[:, 0 + c0: 0 + c0 + n]
        XH = lambda c0, n: WB2[:, 8192 + c0: 8192 + c0 + n]
        zB = WB2[0:1, 8192 + 1024 * 6:8192 + 1024 * 6 + 1024]
        bvsB = WB2[0:1, 8192 + 1024 * 7:8192 + 1024 * 7 + 1024]
        mstage = WB2[:, 8192 + 1024 * 8:8192 + 1024 * 8 + 1024]
        mrowB = WB2[0:1, 8192 + 1024 * 9:8192 + 1024 * 9 + 1024]
        QsS = WB2[:, 8192:10240]
        KsS = WB2[:, 10240:12288]
        VS = WB2[:, 12288:14336]
        KuS = WB2[:, 14336:16384]
        AsS = WB2[:, 16384:18432]
        Pm = [WB2[:, 18432:18688], WB2[:, 18688:18944],
              WB2[:, 18944:19072], WB2[:, 19072:19200]]
        SPARE = WB2[:, 19200:32768].bitcast(F32)
        GtS = SPARE[:, 0:2048]
        rsbS = SPARE[:, 2048:4096]
        sstgS = SPARE[:, 4096:5120]
        sqsbS = SPARE[:, 5120:5376]
        tAS = SPARE[:, 5376:5632]
        tBS = SPARE[:, 5632:5888]
        tA2S = tA2F[:, :]
        tB2S = tB2F[:, :]
        vR = SPARE[0:1, 5888:6144]
        yR = SPARE[0:1, 6144:6400]
        nmrR = SPARE[0:1, 6400:6656]
        nmuR = tAS[0:1, 0:256]
        tR = sqsbS[0:1, 0:256]
        WB2_A_TOKENS = ("b.Qs", "b.Ks", "b.V", "b.Ku", "b.As", "b.Pm0", "b.Pm1",
                        "b.Gt", "b.rsb", "b.sstg", "b.sqsb", "b.tA", "b.tB")

        w1S = lambda c0, n: WB1[:, c0:c0 + n]
        w2S = lambda c0, n: WB2[:, c0:c0 + n]

        for nm, dst, src in [
            ("c.ident", identS, ident_d), ("c.ocf", ocfS, ocf_d),
            ("c.orb", orbS, orb_d),
            ("c.orf", orfS, orf_d), ("c.mask", maskS, mask_d),
            ("c.qdec", qdecS, qdec_d), ("c.kdec", kdecS, kdec_d),
            ("c.kus", kusS, kus_d), ("r.cpre", cpreS, cpre),
            ("r.csum", csumS, csum),
            ("r.bqT", bqTs, bqT), ("r.bkT", bkTs, bkT),
            ("r.bgT", bgTs, bgT), ("r.b1T", b1Ts, b1T),
            ("r.gmT", gmTs, gmT), ("r.btT", btTs, btT),
        ]:
            A("sync", lambda e, d=dst, s=src: e.dma_start(out=d[:, :], in_=s[:, :]),
              writes=[nm], dma=True)
        for nm, dst, src in [
            ("r.cpreB", cpreB, cpre), ("r.bvB", bvB, b_v), ("r.bkB", bkB, b_k),
            ("r.boB", boB, b_o), ("r.b2B", b2B, b_2),
        ]:
            A("gpsimd", lambda e, d=dst, s=src: e.dma_start(out=d[:, :], in_=s[:, :]),
              writes=[nm], dma=True)

        def ln_stats(xbuf, xtok, col):
            cs = (col, col + 1)
            sx = col // 2
            A("scalar", lambda e: e.activation(
                xnb[:, :], xbuf[:, :], AF.Copy,
                accum_out=nsum[:, cs[0]:cs[1]]),
              reads=[xtok], writes=["b.xnb", "b.nsum.%d" % sx])
            A("vector", lambda e: e.tensor_scalar(
                nmu[:, cs[0]:cs[1]], nsum[:, cs[0]:cs[1]], -1.0 / DM, None,
                ALU.mult),
              reads=["b.nsum.%d" % sx], writes=["b.nmu.%d" % sx])
            A("scalar", lambda e: e.activation(
                xnb[:, :], xbuf[:, :], AF.Square, bias=nmu[:, cs[0]:cs[1]],
                accum_out=v1c[:, cs[0]:cs[1]]),
              reads=[xtok, "b.nmu.%d" % sx], writes=["b.xnb", "b.v1c.%d" % sx])

        def ln_nr(c0, ncols):
            cs = (c0, c0 + ncols)
            sx = c0 // 2
            A("vector", lambda e: e.tensor_scalar(
                vpe[:, cs[0]:cs[1]], v1c[:, cs[0]:cs[1]], 1.0 / DM, EPS,
                ALU.mult, ALU.add),
              reads=["b.v1c.%d" % sx], writes=["b.vpe.%d" % sx])
            ln_nr_core(c0, ncols)

        def ln_nr_core(c0, ncols):
            cs = (c0, c0 + ncols)
            sx = c0 // 2
            A("vector", lambda e: e.tensor_scalar(
                nrY.bitcast(I32)[:, cs[0]:cs[1]], vpe.bitcast(I32)[:, cs[0]:cs[1]],
                1, None, ALU.logical_shift_right),
              reads=["b.vpe.%d" % sx], writes=["b.nrY.%d" % sx])
            A("vector", lambda e: e.tensor_scalar(
                nrY.bitcast(I32)[:, cs[0]:cs[1]], nrY.bitcast(I32)[:, cs[0]:cs[1]],
                -1, None, ALU.bitwise_xor),
              reads=["b.nrY.%d" % sx], writes=["b.nrY.%d" % sx])
            A("vector", lambda e: e.tensor_scalar(
                nrY.bitcast(I32)[:, cs[0]:cs[1]], nrY.bitcast(I32)[:, cs[0]:cs[1]],
                MAGIC + 1, None, ALU.add),
              reads=["b.nrY.%d" % sx], writes=["b.nrY.%d" % sx])
            for _ in range(2):
                A("vector", lambda e: e.tensor_tensor(
                    nrT[:, cs[0]:cs[1]], nrY[:, cs[0]:cs[1]], nrY[:, cs[0]:cs[1]],
                    ALU.mult), reads=["b.nrY.%d" % sx], writes=["b.nrT.%d" % sx])
                A("vector", lambda e: e.tensor_tensor(
                    nrT[:, cs[0]:cs[1]], nrT[:, cs[0]:cs[1]], vpe[:, cs[0]:cs[1]],
                    ALU.mult),
                  reads=["b.nrT.%d" % sx, "b.vpe.%d" % sx],
                  writes=["b.nrT.%d" % sx])
                A("vector", lambda e: e.tensor_scalar(
                    nrT[:, cs[0]:cs[1]], nrT[:, cs[0]:cs[1]], -0.5, 1.5,
                    ALU.mult, ALU.add),
                  reads=["b.nrT.%d" % sx], writes=["b.nrT.%d" % sx])
                A("vector", lambda e: e.tensor_tensor(
                    nrY[:, cs[0]:cs[1]], nrY[:, cs[0]:cs[1]], nrT[:, cs[0]:cs[1]],
                    ALU.mult),
                  reads=["b.nrY.%d" % sx, "b.nrT.%d" % sx],
                  writes=["b.nrY.%d" % sx])

        def ln_finish(xbuf, xtok, col, out_ap, out_tok, scale_col=None, extra=()):
            cs = (col, col + 1)
            sx = col // 2
            rstd_tok = "b.nrY.%d" % sx
            rstd_ap = nrY
            if scale_col is not None:
                ap, rtok = scale_col
                A("vector", lambda e: e.tensor_tensor(
                    nrT[:, cs[0]:cs[1]], nrY[:, cs[0]:cs[1]], ap, ALU.mult),
                  reads=["b.nrY.%d" % sx, rtok], writes=["b.nrT.%d" % sx])
                rstd_tok = "b.nrT.%d" % sx
                rstd_ap = nrT
            A("vector", lambda e: e.tensor_scalar(
                out_ap, xbuf[:, :], nmu[:, cs[0]:cs[1]], rstd_ap[:, cs[0]:cs[1]],
                ALU.add, ALU.mult),
              reads=[xtok, "b.nmu.%d" % sx, rstd_tok] + list(extra),
              writes=[out_tok])

        def ln_bn(xbuf, xtok, t2):
            A("vector", lambda e: e.bn_stats(
                bnst[:, 12 * t2:12 * t2 + 6], xbuf[:, 0:512]),
              reads=[xtok], writes=["b.bn.%d" % t2])
            A("vector", lambda e: e.bn_stats(
                bnst[:, 12 * t2 + 6:12 * t2 + 12], xbuf[:, 512:1024]),
              reads=[xtok], writes=["b.bn.%d" % t2])
            A("vector", lambda e: e.bn_aggr(
                mvc[:, 2 * t2:2 * t2 + 2], bnst[:, 12 * t2:12 * t2 + 12]),
              reads=["b.bn.%d" % t2], writes=["b.mv.%d" % t2])

        def ln_bn_vpe(c0):
            mvv = mvc.rearrange("p (t two) -> p t two", two=2)[:, :, 1:2]
            A("vector", lambda e: e.tensor_scalar(
                vpe[:, c0:c0 + 2], mvv, EPS, None, ALU.add),
              reads=["b.mv.0", "b.mv.1"], writes=["b.vpe.%d" % (c0 // 2)])

        def ln_bn_fin(xbuf, xtok, t2, col, out_ap, out_tok, scale_col=None,
                      extra=()):
            cs = (col, col + 1)
            sx = col // 2
            rstd_tok = "b.nrY.%d" % sx
            rstd_ap = nrY
            if scale_col is not None:
                ap, rtok = scale_col
                A("vector", lambda e: e.tensor_tensor(
                    nrT[:, cs[0]:cs[1]], nrY[:, cs[0]:cs[1]], ap, ALU.mult),
                  reads=["b.nrY.%d" % sx, rtok], writes=["b.nrT.%d" % sx])
                rstd_tok = "b.nrT.%d" % sx
                rstd_ap = nrT
            A("vector", lambda e, rstd_ap=rstd_ap: e.tensor_scalar(
                out_ap, xbuf[:, :], mvc[:, 2 * t2:2 * t2 + 1],
                rstd_ap[:, cs[0]:cs[1]], ALU.subtract, ALU.mult),
              reads=[xtok, "b.mv.%d" % t2, rstd_tok] + list(extra),
              writes=[out_tok])

        _p0 = ExitStack()
        pM0 = _p0.enter_context(nc.psum_tensor("pM0", [128, 1024], F32))
        pM1 = _p0.enter_context(nc.psum_tensor("pM1", [128, 1024], F32))
        pMR = _p0.enter_context(nc.psum_tensor("pMR", [1, 1024], F32))
        pTR = _p0.enter_context(nc.psum_tensor("pTR", [128, 512], BF16))
        pSP = _p0.enter_context(nc.psum_tensor("pSP", [128, 64], F32))
        if True:
            def stageE_pre(g):
                A("sync", lambda e, g=g: e.dma_start(
                    out=xg01[:, :], in_=xseg[:, 2048 * g:2048 * g + 2048]),
                  writes=["xg0", "xg1"], dma=True)
                for t in range(2):
                    xb = xg0 if t == 0 else xg1
                    ln_bn(xb, "xg%d" % t, t)
                ln_bn_vpe(0)
                ln_nr_core(0, 2)

            def stageE_fin(g, trp=None):
                gc = GRP * g
                trp = trp if trp is not None else (
                    (pTR2[:, 0:512], "p.SCA"), (pSCBb[:, 0:512], "p.SCB"))
                for t in range(2):
                    xb = xg0 if t == 0 else xg1
                    xtok = "xg%d" % t
                    ln_bn_fin(xb, xtok, t, t, xnb[:, :], "b.xnb")
                    if dbg and g == 0 and t == 0:
                        A("sync", lambda e: e.dma_start(out=d_xnb[:, :], in_=xnb[:, :]),
                          reads=["b.xnb"], writes=["d.out"], dma=True)
                    for c4 in range(2):
                        pT, pTtok = trp[(2 * t + c4) % 2]
                        for bb in range(4):
                            A("tensor", lambda e, c4=c4, bb=bb, pT=pT: e.transpose(
                                pT[:, 128 * bb:128 * bb + 128],
                                xnb[:, 512 * c4 + 128 * bb:512 * c4 + 128 * bb + 128],
                                identS[:, :]),
                              reads=["b.xnb", "c.ident"], writes=[pTtok])
                        A("scalar", lambda e, c4=c4, gc=gc, t=t, pT=pT: e.copy(
                            xnT[:, :].rearrange("p (t n) -> p t n", n=2048)
                               [:, 4 * c4:4 * c4 + 4, gc + 128 * t:gc + 128 * t + 128],
                            pT[:, 0:512].rearrange("p (t n) -> p t n", n=128)),
                          reads=[pTtok], writes=["xnT"])

            def stageE(g, trp=None):
                stageE_pre(g)
                stageE_fin(g, trp=trp)

            def stageM1(g):
                gc = GRP * g
                for it in range(2):
                    for dm in range(NDM):
                        for ns in range(2):
                            A("tensor", lambda e, it=it, dm=dm, ns=ns, gc=gc: e.matmul(
                                pPJ[:, 512 * ns:512 * ns + 512],
                                xnT[:, 2048 * dm + gc + 128 * it:2048 * dm + gc + 128 * it + 128],
                                WvS(1024 * dm + 512 * ns, 512),
                                start=(dm == 0), stop=False, skip_group_check=True),
                              reads=["xnT", "w.v"], writes=["p.PJ0", "p.PJ1"])
                    for ns in range(2):
                        A("tensor", lambda e, ns=ns: e.matmul(
                            pPJ[:, 512 * ns:512 * ns + 512],
                            orbS[0:1, :], bvB[0:1, 512 * ns:512 * ns + 512],
                            start=False, stop=True, skip_group_check=True),
                          reads=["c.orb", "r.bvB"], writes=["p.PJ0", "p.PJ1"])
                    A("scalar", lambda e, it=it: e.copy(
                        VS[:, 1024 * it:1024 * it + 1024], pPJ[:, :]),
                      reads=["p.PJ0", "p.PJ1"], writes=["b.V"])
                    for dm in range(NDM):
                        for ns in range(2):
                            A("tensor", lambda e, it=it, dm=dm, ns=ns, gc=gc: e.matmul(
                                pPJ[:, 512 * ns:512 * ns + 512],
                                xnT[:, 2048 * dm + gc + 128 * it:2048 * dm + gc + 128 * it + 128],
                                WkS(1024 * dm + 512 * ns, 512),
                                start=(dm == 0), stop=False, skip_group_check=True),
                              reads=["xnT", "w.k"], writes=["p.PJ0", "p.PJ1"])
                    for ns in range(2):
                        A("tensor", lambda e, ns=ns: e.matmul(
                            pPJ[:, 512 * ns:512 * ns + 512],
                            orbS[0:1, :], bkB[0:1, 512 * ns:512 * ns + 512],
                            start=False, stop=True, skip_group_check=True),
                          reads=["c.orb", "r.bkB"], writes=["p.PJ0", "p.PJ1"])
                    A("vector", lambda e, it=it: e.tensor_scalar(
                        KuS[:, 1024 * it:1024 * it + 1024], pPJ[:, :],
                        kusS[:, it:it + 1], None, ALU.mult),
                      reads=["p.PJ0", "p.PJ1", "c.kus"], writes=["b.Ku"])
                for which, wsl, wtok in (("q", WqS, "w.q"), ("k", WkS, "w.k")):
                    for dt in range(NDM):
                        half = (dt % 2) * 512
                        ptok = "p.PJ%d" % (dt % 2)
                        for dm in range(NDM):
                            A("tensor", lambda e, wsl=wsl, dt=dt, dm=dm, half=half, gc=gc: e.matmul(
                                pPJ[:, half:half + GRP],
                                wsl(1024 * dm + 128 * dt, 128),
                                xnT[:, 2048 * dm + gc:2048 * dm + gc + GRP],
                                start=(dm == 0), stop=(dm == NDM - 1),
                                skip_group_check=True),
                              reads=[wtok, "xnT"], writes=[ptok])
                        if which == "q":
                            if dbg and g == 0 and dt == 0:
                                A("scalar", lambda e: e.copy(sqsbS[:, :], pPJ[:, 0:256]),
                                  reads=[ptok], writes=["b.sqsb"])
                                A("sync", lambda e: e.dma_start(out=d_PJ[:, :], in_=sqsbS[:, :]),
                                  reads=["b.sqsb"], writes=["d.out"], dma=True)
                            A("vector", lambda e, dt=dt, half=half: e.scalar_tensor_tensor(
                                QsS[:, GRP * dt:GRP * dt + GRP],
                                pPJ[:, half:half + GRP],
                                bqTs[:, dt:dt + 1], qdecS[:, :],
                                ALU.add, ALU.mult),
                              reads=[ptok, "r.bqT", "c.qdec"], writes=["b.Qs"])
                        elif which == "k":
                            A("vector", lambda e, dt=dt, half=half: e.scalar_tensor_tensor(
                                KsS[:, GRP * dt:GRP * dt + GRP],
                                pPJ[:, half:half + GRP],
                                bkTs[:, dt:dt + 1], kdecS[:, :],
                                ALU.add, ALU.mult),
                              reads=[ptok, "r.bkT", "c.kdec"], writes=["b.Ks"])


            def stageM2(g):
                gc = GRP * g
                A("scalar", lambda e: e.copy(sb16[:, :], state[:, :]),
                  reads=["b.state"], writes=["b.sb16"])
                for which, wsl, wtok in (("g", WgS, "w.g"),):
                    for dt in range(NDM):
                        half = (dt % 2) * 512
                        ptok = "p.PJ%d" % (dt % 2)
                        for dm in range(NDM):
                            A("tensor", lambda e, wsl=wsl, dt=dt, dm=dm, half=half, gc=gc: e.matmul(
                                pPJ[:, half:half + GRP],
                                wsl(1024 * dm + 128 * dt, 128),
                                xnT[:, 2048 * dm + gc:2048 * dm + gc + GRP],
                                start=(dm == 0), stop=(dm == NDM - 1),
                                skip_group_check=True),
                              reads=[wtok, "xnT"], writes=[ptok])
                        A("scalar", lambda e, dt=dt, half=half: e.activation(
                            GtS[:, GRP * dt:GRP * dt + GRP],
                            pPJ[:, half:half + GRP],
                            AF.Sigmoid, bias=bgTs[:, dt:dt + 1]),
                          reads=[ptok, "r.bgT"], writes=["b.Gt.%d" % dt])
                PmO = [xnb[:, 0:256], xnb[:, 256:512],
                       xnb[:, 512:640], xnb[:, 640:768]]

                def m2_jb(hp):
                    if hp % 2 == 0:
                        jb0b, jb0t = pSCB, "p.SCB"
                        jb1b, jb1t, jb1c = pSCA, "p.SCA", 256
                        pm, pmt, extw = Pm, "b.Pm", []
                    else:
                        jb0b, jb0t = pLATE, "p.LATE"
                        jb1b, jb1t, jb1c = pS, "p.S", 0
                        pm, pmt, extw = PmO, "b.PmX", ["b.xnb"]
                    for hh in range(2):
                        r0 = 64 * hh
                        jb0ap = jb0b[:, 256 * hh:256 * hh + 256]
                        jb1ap = jb1b[:, jb1c + 128 * hh:jb1c + 128 * hh + 128]
                        A("tensor", lambda e, hp=hp, r0=r0, jb0ap=jb0ap: e.matmul(
                            jb0ap,
                            KsS[r0:r0 + 64, GRP * hp:GRP * hp + 128],
                            QsS[r0:r0 + 64, GRP * hp:GRP * hp + GRP],
                            start=True, stop=True, skip_group_check=True),
                          reads=["b.Ks", "b.Qs"], writes=[jb0t])
                        A("tensor", lambda e, hp=hp, r0=r0, jb1ap=jb1ap: e.matmul(
                            jb1ap,
                            KsS[r0:r0 + 64, GRP * hp + 128:GRP * hp + 256],
                            QsS[r0:r0 + 64, GRP * hp + 128:GRP * hp + 256],
                            start=True, stop=True, skip_group_check=True),
                          reads=["b.Ks", "b.Qs"], writes=[jb1t])
                        A("vector", lambda e, hh=hh, jb0ap=jb0ap, pm=pm: e.tensor_tensor(
                            pm[hh][:, 0:256], jb0ap, maskS[:, 0:256],
                            ALU.mult),
                          reads=[jb0t, "c.mask"],
                          writes=["%s0%d" % (pmt, hh)] + extw)
                        A("vector", lambda e, hh=hh, jb1ap=jb1ap, pm=pm: e.tensor_tensor(
                            pm[2 + hh][:, 0:128], jb1ap, maskS[:, 0:128],
                            ALU.mult),
                          reads=[jb1t, "c.mask"],
                          writes=["%s1%d" % (pmt, hh)] + extw)

                def m2_ret(hp):
                    pm, pmt = (Pm, "b.Pm") if hp % 2 == 0 else (PmO, "b.PmX")
                    extr = [] if hp % 2 == 0 else ["b.xnb"]
                    for hh in range(2):
                        h = 2 * hp + hh
                        ro = 64 * hh
                        A("tensor", lambda e, h=h, ro=ro, hh=hh, pm=pm: e.matmul(
                            pRT[ro:ro + 64, 0:256] if ro == 0 else pRT[64:128, 0:256],
                            VS[:, 64 * h:64 * h + 64],
                            pm[hh][:, 0:256],
                            start=True, stop=False, skip_group_check=True,
                            tile_position=(0, ro)),
                          reads=["b.V", "%s0%d" % (pmt, hh)] + extr,
                          writes=["p.RT"])
                        A("tensor", lambda e, h=h, ro=ro, hh=hh, pm=pm: e.matmul(
                            pRT[ro:ro + 64, 128:256] if ro == 0 else pRT[64:128, 128:256],
                            VS[:, 1024 + 64 * h:1024 + 64 * h + 64],
                            pm[2 + hh][:, 0:128],
                            start=False, stop=False, skip_group_check=True,
                            tile_position=(0, ro)),
                          reads=["b.V", "%s1%d" % (pmt, hh)] + extr,
                          writes=["p.RT"])
                        A("tensor", lambda e, h=h, ro=ro, hp=hp: e.matmul(
                            pRT[ro:ro + 64, 0:256] if ro == 0 else pRT[64:128, 0:256],
                            sb16[ro:ro + 64, 64 * hp:64 * hp + 64],
                            QsS[ro:ro + 64, GRP * hp:GRP * hp + GRP],
                            start=False, stop=(ro == 64), skip_group_check=True,
                            tile_position=(ro, ro)),
                          reads=["b.sb16", "b.Qs"], writes=["p.RT"])
                    A("scalar", lambda e, hp=hp: e.copy(
                        rsbS[:, GRP * hp:GRP * hp + GRP], pRT[:, 0:256]),
                      reads=["p.RT"], writes=["b.rsb.%d" % hp])
                    for hh in range(2):
                        h = 2 * hp + hh
                        ro = 64 * hh
                        for it in range(2):
                            A("tensor", lambda e, h=h, ro=ro, it=it, hp=hp: e.matmul(
                                pSU[ro:ro + 64, 64 * hp:64 * hp + 64] if ro == 0
                                else pSU[64:128, 64 * hp:64 * hp + 64],
                                KuS[:, 1024 * it + 64 * h:1024 * it + 64 * h + 64],
                                VS[:, 1024 * it + 64 * h:1024 * it + 64 * h + 64],
                                start=(it == 0), stop=(it == 1),
                                skip_group_check=True, tile_position=(0, ro)),
                              reads=["b.Ku", "b.V"], writes=["p.SU"])

                m2_jb(0)
                for hp in range(NP):
                    if hp + 1 < NP:
                        m2_jb(hp + 1)
                    m2_ret(hp)
                if dbg and g == 0:
                    A("sync", lambda e: e.dma_start(
                        out=d_Pm[:, :], in_=Pm[0][:, 0:256]),
                      reads=["b.Pm00"], writes=["d.out"], dma=True)
                A("vector", lambda e: e.scalar_tensor_tensor(
                    state[:, :], state[:, :], D256, pSU[:, :], ALU.mult, ALU.add),
                  reads=["b.state", "p.SU"], writes=["b.state"])
                if dbg and g == 0:
                    A("sync", lambda e: e.dma_start(out=d_Wq[:, :], in_=WqS(0, 8192)),
                      reads=["w.q"], writes=["d.out"], dma=True)
                    A("sync", lambda e: e.dma_start(out=d_Wk[:, :], in_=WkS(0, 8192)),
                      reads=["w.k"], writes=["d.out"], dma=True)
                    for dd, sb_, tok in [(d_Qs, QsS, "b.Qs"), (d_Ks, KsS, "b.Ks"),
                                         (d_V, VS, "b.V"), (d_Ku, KuS, "b.Ku"),
                                         (d_Gt, GtS, "b.Gt"), (d_rsb, rsbS, "b.rsb"),
                                         (d_S1, state, "b.state")]:
                        A("sync", lambda e, dd=dd, sb_=sb_: e.dma_start(
                            out=dd[:, :], in_=sb_[:, 0:dd.shape[1]]),
                          reads=[tok], writes=["d.out"], dma=True)

            def stageS(g):
                gc = GRP * g
                for dt in range(NDM):
                    A("tensor", lambda e, dt=dt: e.matmul(
                        pLATE[0:1, 0:256], ocfS[:, :],
                        rsbS[:, GRP * dt:GRP * dt + GRP],
                        start=(dt == 0), stop=(dt == NDM - 1),
                        skip_group_check=True),
                      reads=["c.ocf", "b.rsb.%d" % dt], writes=["p.LATE"])
                A("vector", lambda e: e.tensor_scalar(
                    nmuR, pLATE[0:1, 0:256], -1.0 / DM, None, ALU.mult),
                  reads=["p.LATE"], writes=["b.tA0"])
                AsF = AsS.bitcast(F32)
                for dt in range(NDM):
                    j = dt % 4
                    sqap = AsF[:, 256 * j:256 * j + 256]
                    sqtok = ["b.As.%d" % (2 * j), "b.As.%d" % (2 * j + 1)]
                    A("scalar", lambda e, dt=dt, sqap=sqap: e.activation(
                        sqap, rsbS[:, GRP * dt:GRP * dt + GRP], AF.Square),
                      reads=["b.rsb.%d" % dt], writes=sqtok)
                    A("tensor", lambda e, dt=dt, sqap=sqap: e.matmul(
                        pLATE[0:1, 256:512], ocfS[:, :], sqap,
                        start=(dt == 0), stop=(dt == NDM - 1),
                        skip_group_check=True),
                      reads=["c.ocf"] + sqtok, writes=["p.LATE"])
                A("vector", lambda e: e.tensor_tensor(
                    tR, nmuR, nmuR, ALU.mult),
                  reads=["b.tA0"], writes=["b.sqsb"])
                A("vector", lambda e: e.scalar_tensor_tensor(
                    vR, pLATE[0:1, 256:512], 1.0 / DM, tR,
                    ALU.mult, ALU.subtract),
                  reads=["p.LATE", "b.sqsb"], writes=["b.vR"])
                A("vector", lambda e: e.tensor_scalar(
                    vR, vR, EPS, None, ALU.add),
                  reads=["b.vR"], writes=["b.vR"])
                A("vector", lambda e: e.tensor_scalar(
                    yR.bitcast(I32), vR.bitcast(I32), 1, None,
                    ALU.logical_shift_right), reads=["b.vR"], writes=["b.yR"])
                A("vector", lambda e: e.tensor_scalar(
                    yR.bitcast(I32), yR.bitcast(I32), -1,
                    None, ALU.bitwise_xor), reads=["b.yR"], writes=["b.yR"])
                A("vector", lambda e: e.tensor_scalar(
                    yR.bitcast(I32), yR.bitcast(I32), MAGIC + 1,
                    None, ALU.add), reads=["b.yR"], writes=["b.yR"])
                for _ in range(2):
                    A("vector", lambda e: e.tensor_tensor(
                        tR, yR, yR, ALU.mult),
                      reads=["b.yR"], writes=["b.sqsb"])
                    A("vector", lambda e: e.tensor_tensor(
                        tR, tR, vR, ALU.mult),
                      reads=["b.sqsb", "b.vR"], writes=["b.sqsb"])
                    A("vector", lambda e: e.tensor_scalar(
                        tR, tR, -0.5, 1.5, ALU.mult, ALU.add),
                      reads=["b.sqsb"], writes=["b.sqsb"])
                    A("vector", lambda e: e.tensor_tensor(
                        yR, yR, tR, ALU.mult),
                      reads=["b.yR", "b.sqsb"], writes=["b.yR"])
                A("vector", lambda e: e.tensor_tensor(
                    nmrR, nmuR, yR, ALU.mult),
                  reads=["b.tA0", "b.yR"], writes=["b.nmrR"])

            def stageL(g):
                gc = GRP * g
                A("tensor", lambda e: e.matmul(
                    pLATE[:, 0:256], orfS[0:1, :], yR,
                    start=True, stop=True, skip_group_check=True),
                  reads=["c.orf", "b.yR"], writes=["p.LATE"])
                A("tensor", lambda e: e.matmul(
                    pLATE[:, 256:512], orfS[0:1, :], nmrR,
                    start=True, stop=True, skip_group_check=True),
                  reads=["c.orf", "b.nmrR"], writes=["p.LATE"])
                for dt in range(NDM):
                    tA_ = tAS if dt % 2 == 0 else tA2S
                    tB_ = tBS if dt % 2 == 0 else tB2S
                    tat = "b.tA%d" % (dt % 2)
                    tbt = "b.tB%d" % (dt % 2)
                    A("vector", lambda e, dt=dt, tA_=tA_: e.tensor_tensor(
                        tA_[:, :], rsbS[:, GRP * dt:GRP * dt + GRP], pLATE[:, 0:256],
                        ALU.mult), reads=["b.rsb.%d" % dt, "p.LATE"], writes=[tat])
                    A("vector", lambda e, dt=dt, tA_=tA_, tB_=tB_: e.tensor_tensor(
                        tB_[:, :], tA_[:, :], pLATE[:, 256:512], ALU.add),
                      reads=[tat, "p.LATE"], writes=[tbt])
                    A("vector", lambda e, dt=dt, tA_=tA_, tB_=tB_: e.tensor_scalar(
                        tA_[:, :], tB_[:, :], gmTs[:, dt:dt + 1], btTs[:, dt:dt + 1],
                        ALU.mult, ALU.add),
                      reads=[tbt, "r.gmT", "r.btT"], writes=[tat])
                    A("gpsimd", lambda e, dt=dt, tA_=tA_: e.tensor_tensor(
                        AsS[:, GRP * dt:GRP * dt + GRP], tA_[:, :],
                        GtS[:, GRP * dt:GRP * dt + GRP], ALU.mult),
                      reads=[tat, "b.Gt.%d" % dt], writes=["b.As.%d" % dt])
                if dbg and g == 0:
                    A("sync", lambda e: e.dma_start(out=d_As[:, :], in_=AsS[:, :]),
                      reads=["b.As.%d" % d_ for d_ in range(NDM)], writes=["d.out"], dma=True)
                for it in range(2):
                    for ns in range(2):
                        for dt in range(NDM):
                            A("tensor", lambda e, it=it, dt=dt, ns=ns: e.matmul(
                                pS[:, :],
                                AsS[:, GRP * dt + 128 * it:GRP * dt + 128 * it + 128],
                                WoS(1024 * dt + 512 * ns, 512),
                                start=(dt == 0), stop=False, skip_group_check=True),
                              reads=["b.As.%d" % dt, "w.o"], writes=["p.S"])
                        A("tensor", lambda e, ns=ns: e.matmul(
                            pS[:, :],
                            orbS[0:1, :], boB[0:1, 512 * ns:512 * ns + 512],
                            start=False, stop=True, skip_group_check=True),
                          reads=["c.orb", "r.boB"], writes=["p.S"])
                        A("scalar", lambda e, ns=ns: e.copy(
                            sstgS[:, 512 * ns:512 * ns + 512], pS[:, :]),
                          reads=["p.S"], writes=["b.sstg"])
                    A("gpsimd", lambda e, gc=gc, it=it: e.dma_start(
                        out=s_scr[gc + 128 * it:gc + 128 * it + 128, :],
                        in_=sstgS[:, :]),
                      reads=["b.sstg"], writes=["s_scr.%d.%d" % (g, it)], dma=True)

            if skip_p0:
                A("vector", lambda e: e.memset(state[:, :], 0.0),
                  writes=["b.state"])
            if not skip_p0:
                for o in range(3):
                    for jp in range(4):
                        pr = 4 * o + jp
                        jt0 = 2 * pr
                        buf = xg01 if pr % 2 == 0 else xg23
                        btok = ["xg0", "xg1"] if pr % 2 == 0 else ["xg2", "xg3"]
                        c0 = jt0 % 4
                        A("sync", lambda e, jt0=jt0, buf=buf: e.dma_start(
                            out=buf[:, :],
                            in_=xpre[:, 1024 * jt0:1024 * jt0 + 2048]),
                          writes=btok, dma=True)
                        for t2 in range(2):
                            ln_bn(buf[:, 1024 * t2:1024 * t2 + 1024],
                                  btok[t2], t2)
                        ln_bn_vpe(c0)
                        ln_nr_core(c0, 2)
                        for t2 in range(2):
                            jt = jt0 + t2
                            ln_bn_fin(buf[:, 1024 * t2:1024 * t2 + 1024],
                                      btok[t2], t2, c0 + t2,
                                      XH(1024 * jt, 1024), "xnh.%d" % jt,
                                      scale_col=(cpreS[:, jt:jt + 1], "r.cpre"))
                            for ns in range(2):
                                A("tensor", lambda e, jt=jt, ns=ns: e.matmul(
                                    pMR[0:1, 512 * ns:512 * ns + 512],
                                    cpreB[:, jt:jt + 1],
                                    XH(1024 * jt + 512 * ns, 512),
                                    start=(jt == 0), stop=(jt == NPT - 1),
                                    skip_group_check=True),
                                  reads=["r.cpreB", "xnh.%d" % jt],
                                  writes=["p.MR"])
                    if o == 0:
                        A("gpsimd", lambda e: e.dma_start(
                            out=WkS(0, 8192), in_=Wk.ap()),
                          writes=["w.k"], dma=True)
                        A("gpsimd", lambda e: e.dma_start(
                            out=WvS(0, 8192), in_=Wv.ap()),
                          writes=["w.v"], dma=True)
                    for at in range(NDM):
                        pM = pM0 if at % 2 == 0 else pM1
                        ptok = "p.M%d" % (at % 2)
                        if o > 0:
                            for ns in range(2):
                                A("tensor", lambda e, at=at, ns=ns, pM=pM: e.matmul(
                                    pM[:, 512 * ns:512 * ns + 512],
                                    identS[:, :],
                                    MS(1024 * at + 512 * ns, 512),
                                    start=True, stop=False,
                                    skip_group_check=True),
                                  reads=["b.M.%d" % at, "c.ident"],
                                  writes=[ptok])
                        for j in range(8):
                            jt = 8 * o + j
                            for ns in range(2):
                                A("tensor", lambda e, jt=jt, at=at, ns=ns, pM=pM, o=o: e.matmul(
                                    pM[:, 512 * ns:512 * ns + 512],
                                    XH(1024 * jt + 128 * at, 128),
                                    XH(1024 * jt + 512 * ns, 512),
                                    start=(jt % 8 == 0 and o == 0),
                                    stop=(jt % 8 == 7),
                                    skip_group_check=True),
                                  reads=["xnh.%d" % jt], writes=[ptok])
                        A("scalar", lambda e, at=at, pM=pM: e.copy(
                            MS(1024 * at, 1024), pM[:, :]),
                          reads=[ptok], writes=["b.M.%d" % at])
                A("vector", lambda e: e.tensor_copy(mrowB, pMR[0:1, :]),
                  reads=["p.MR"], writes=["r.mrow", "xnh.9"])
                A("vector", lambda e: e.tensor_copy(mstage[0:1, 0:1024], pMR[0:1, :]),
                  reads=["p.MR"], writes=["r.mst", "xnh.8"])
                for t in range(NDM):
                    A("tensor", lambda e, t=t: e.transpose(
                        pTR[:, 0:128], mstage[:, 128 * t:128 * t + 128],
                        identS[:, :]),
                      reads=["r.mst", "xnh.8", "c.ident"], writes=["p.TR"])
                    A("scalar", lambda e, t=t: e.copy(
                        mcolB[:, t:t + 1], pTR[:, 0:1]),
                      reads=["p.TR"], writes=["r.mcol"])
                for at in range(NDM):
                    pM = pM0 if at % 2 == 0 else pM1
                    ptok = "p.M%d" % (at % 2)
                    for bt in range(NDM):
                        for ns in range(2):
                            A("tensor", lambda e, at=at, bt=bt, ns=ns, pM=pM: e.matmul(
                                pM[:, 512 * ns:512 * ns + 512],
                                MS(1024 * bt + 128 * at, 128),
                                WvS(1024 * bt + 512 * ns, 512),
                                start=(bt == 0), stop=False,
                                skip_group_check=True),
                              reads=["b.M.%d" % bt, "w.v"], writes=[ptok])
                    for ns in range(2):
                        A("tensor", lambda e, at=at, ns=ns, pM=pM: e.matmul(
                            pM[:, 512 * ns:512 * ns + 512],
                            mrowB[0:1, 128 * at:128 * at + 128],
                            bvB[0:1, 512 * ns:512 * ns + 512],
                            start=False, stop=True, skip_group_check=True),
                          reads=["r.mrow", "r.bvB"], writes=[ptok])
                    A("scalar", lambda e, at=at, pM=pM: e.copy(
                        TS(1024 * at, 1024), pM[:, :]),
                      reads=[ptok], writes=["b.T"])
            A("gpsimd", lambda e: e.dma_start(out=WqS(0, 8192), in_=Wq.ap()),
              writes=["w.q"] + ["b.M.%d" % at_ for at_ in range(NDM)],
              dma=True)
            stageE(0, trp=((pTR, "p.TR"), (pTR, "p.TR")))
            if not skip_p0:
                A("vector", lambda e: e.tensor_scalar(
                    bvsB, bvB[0:1, :], csumS[0:1, 0:1], None, ALU.mult),
                  reads=["r.bvB", "r.csum"], writes=["r.bvs", "xnh.7"])
                for at in range(NDM):
                    for ns in range(2):
                        A("tensor", lambda e, at=at, ns=ns: e.matmul(
                            pMR[0:1, 512 * ns:512 * ns + 512],
                            mcolB[:, at:at + 1],
                            WvS(1024 * at + 512 * ns, 512),
                            start=(at == 0), stop=False, skip_group_check=True),
                          reads=["r.mcol", "w.v"], writes=["p.MR"])
                for ns in range(2):
                    A("tensor", lambda e, ns=ns: e.matmul(
                        pMR[0:1, 512 * ns:512 * ns + 512],
                        orbS[0:1, 0:1], bvsB[0:1, 512 * ns:512 * ns + 512],
                        start=False, stop=True, skip_group_check=True),
                      reads=["c.orb", "r.bvs"], writes=["p.MR"])
                A("vector", lambda e: e.tensor_copy(zB, pMR[0:1, :]),
                  reads=["p.MR"], writes=["r.zB", "xnh.6"])
            stageE(1, trp=((pTR, "p.TR"), (pTR, "p.TR")))
            if not skip_p0:
                for hp in range(NP):
                    for hh in range(2):
                        h = 2 * hp + hh
                        ro = 64 * hh
                        for at in range(NDM):
                            A("tensor", lambda e, h=h, ro=ro, at=at: e.matmul(
                                pSP[ro:ro + 64, 0:64] if ro == 0 else pSP[64:128, 0:64],
                                WkS(1024 * at + 64 * h, 64),
                                TS(1024 * at + 64 * h, 64),
                                start=(at == 0), stop=False,
                                skip_group_check=True, tile_position=(0, ro)),
                              reads=["w.k", "b.T"], writes=["p.SP"])
                        A("tensor", lambda e, h=h, ro=ro: e.matmul(
                            pSP[ro:ro + 64, 0:64] if ro == 0 else pSP[64:128, 0:64],
                            bkB[0:1, 64 * h:64 * h + 64],
                            zB[0:1, 64 * h:64 * h + 64],
                            start=False, stop=(ro == 64), skip_group_check=True,
                            tile_position=(0, ro)),
                          reads=["r.bkB", "r.zB"], writes=["p.SP"])
                    A("vector", lambda e, hp=hp: e.tensor_copy(
                        state[:, 64 * hp:64 * hp + 64], pSP[:, 0:64]),
                      reads=["p.SP"], writes=["b.state"])
            if skip_p0:
                A("gpsimd", lambda e: e.dma_start(out=WkS(0, 8192), in_=Wk.ap()),
                  writes=["w.k"], dma=True)
                A("gpsimd", lambda e: e.dma_start(out=WvS(0, 8192), in_=Wv.ap()),
                  writes=["w.v"], dma=True)
            A("gpsimd", lambda e: e.dma_start(out=WgS(0, 8192), in_=Wg.ap()),
              writes=["w.g", "b.T"], dma=True)
            A("gpsimd", lambda e: e.dma_start(out=WoS(0, 8192), in_=Wo.ap()),
              writes=["w.o"], dma=True)
            A("gpsimd", lambda e: e.dma_start(out=w1b[:, :], in_=w1.ap()),
              writes=["w.1hbm"], dma=True)
            A("gpsimd", lambda e: e.dma_start(out=w2b[:, :], in_=w2.ap()),
              writes=["w.2hbm"], dma=True)
            if dbg:
                A("sync", lambda e: e.dma_start(out=d_S0[:, :], in_=state[:, :]),
                  reads=["b.state"], writes=["d.out"], dma=True)
        _p0.close()

        p.psum_scope(["p.M0", "p.M1", "p.MR", "p.TR", "p.SP"])
        _pa = ExitStack()
        pPJ = _pa.enter_context(nc.psum_tensor("pPJ", [128, 1024], F32))
        pSCA = _pa.enter_context(nc.psum_tensor("pSCA", [128, 512], F32))
        pSCB = _pa.enter_context(nc.psum_tensor("pSCB", [128, 512], F32))
        pRT = _pa.enter_context(nc.psum_tensor("pRT", [128, 512], F32))
        pSU = _pa.enter_context(nc.psum_tensor("pSU", [128, 512], F32))
        pLATE = _pa.enter_context(nc.psum_tensor("pLATE", [128, 512], F32))
        pS = _pa.enter_context(nc.psum_tensor("pS", [128, 512], F32))
        pTR2 = pSCA.bitcast(BF16)
        pSCBb = pSCB.bitcast(BF16)
        if True:
            stageM1(0)
            for g in range(NG):
                stageM2(g)
                if g + 2 < NG:
                    stageE_pre(g + 2)
                stageS(g)
                if g + 1 < NG:
                    stageM1(g + 1)
                if g + 2 < NG:
                    stageE_fin(g + 2)
                stageL(g)
        _pa.close()
        if dbg:
            A("sync", lambda e: e.dma_start(out=d_xnT[:, :], in_=xnT[:, :]),
              reads=["xnT"], writes=["d.out"], dma=True)
            A("sync", lambda e: e.dma_start(out=d_s[:, :], in_=s_scr[:, :]),
              reads=["s_scr.%d.%d" % (g_, i_) for g_ in range(NG) for i_ in range(2)],
              writes=["d.out"], dma=True)
        p.psum_scope(["p.PJ0", "p.PJ1", "p.SCA", "p.SCB", "p.RT", "p.SU",
                      "p.LATE", "p.S"])
        A("sync", lambda e: e.dma_start(out=w1S(0, 32768), in_=w1b.ap()),
          reads=["w.1hbm"],
          writes=["w.1", "w.k", "w.v", "w.q", "w.g"]
                 + ["b.M.%d" % at_ for at_ in range(NDM)] + ["b.T"], dma=True)
        A("scalar", lambda e: e.dma_start(out=w2S(0, 32768), in_=w2b.ap()),
          reads=["w.2hbm"],
          writes=("w.2", "w.o", "r.mst", "r.mrow", "r.zB", "r.bvs")
                 + tuple("xnh.%d" % j_ for j_ in range(NPT)) + WB2_A_TOKENS,
          dma=True)

        _p3 = ExitStack()
        hT = _p3.enter_context(nc.sbuf_tensor("hT", [128, 4096], BF16))
        sld0 = _p3.enter_context(nc.sbuf_tensor("sld", [128, 1024], F32))
        sld1 = sld0
        scrA = xg23[:, 1024:2048]
        scrB = xg23[:, 0:1024]
        pH0 = _p3.enter_context(nc.psum_tensor("pH0", [128, 1024], F32))
        pH1 = _p3.enter_context(nc.psum_tensor("pH1", [128, 1024], F32))
        pY0 = _p3.enter_context(nc.psum_tensor("pY0", [128, 1024], F32))
        pY1 = _p3.enter_context(nc.psum_tensor("pY1", [128, 1024], F32))
        if True:
            nout = 0
            for g in range(NG):
                gc = GRP * g
                for half in range(2):
                    for q in range(4):
                        pH = pH0 if q % 2 == 0 else pH1
                        htok = "p.H%d" % (q % 2)
                        for sub in range(4):
                            dht = 16 * half + 4 * q + sub
                            for dm in range(NDM):
                                A("tensor", lambda e, dht=dht, dm=dm, sub=sub, pH=pH, gc=gc: e.matmul(
                                    pH[:, 256 * sub:256 * sub + 256],
                                    w1S(4096 * dm + 128 * dht, 128),
                                    xnT[:, 2048 * dm + gc:2048 * dm + gc + GRP],
                                    start=(dm == 0),
                                    stop=(dm == NDM - 1), skip_group_check=True),
                                  reads=["w.1", "xnT"], writes=[htok])
                        for sub in range(4):
                            dht = 16 * half + 4 * q + sub
                            lt = 4 * q + sub
                            A("scalar", lambda e, dht=dht, lt=lt, sub=sub, pH=pH: e.activation(
                                hT[:, 256 * lt:256 * lt + 256],
                                pH[:, 256 * sub:256 * sub + 256],
                                AF.Gelu, bias=b1Ts[:, dht:dht + 1]),
                              reads=[htok, "r.b1T"], writes=["b.hT.%d" % lt])
                    if dbg and g == 0 and half == 0:
                        A("sync", lambda e: e.dma_start(out=d_hT[:, :], in_=hT[:, :]),
                          reads=["b.hT.%d" % l_ for l_ in range(16)], writes=["d.out"], dma=True)
                    for it in range(2):
                        pY = pY0 if it % 2 == 0 else pY1
                        ytok = "p.Y%d" % (it % 2)
                        for lt in range(16):
                            dht = 16 * half + lt
                            for ns in range(2):
                                A("tensor", lambda e, lt=lt, dht=dht, ns=ns, it=it, pY=pY, half=half: e.matmul(
                                    pY[:, 512 * ns:512 * ns + 512],
                                    hT[:, 256 * lt + 128 * it:256 * lt + 128 * it + 128],
                                    w2S(1024 * dht + 512 * ns, 512),
                                    start=(lt == 0 and half == 0), stop=False,
                                    skip_group_check=True),
                                  reads=["b.hT.%d" % lt, "w.2"], writes=[ytok])
                for it in range(2):
                    pY = pY0 if it % 2 == 0 else pY1
                    ytok = "p.Y%d" % (it % 2)
                    xb = xg0 if it % 2 == 0 else xg1
                    xtok = "xg%d" % (it % 2)
                    sb = sld0 if it % 2 == 0 else sld1
                    stok = "b.sld"
                    A("sync", lambda e, g=g, it=it, xb=xb: e.dma_start(
                        out=xb[:, :],
                        in_=xseg[:, 1024 * (2 * g + it):1024 * (2 * g + it) + 1024]),
                      writes=[xtok], dma=True)
                    A("sync", lambda e, gc=gc, it=it, sb=sb: e.dma_start(
                        out=sb[:, :], in_=s_scr[gc + 128 * it:gc + 128 * it + 128, :]),
                      reads=["s_scr.%d.%d" % (g, it)], writes=[stok], dma=True)
                    for ns in range(2):
                        A("tensor", lambda e, ns=ns, pY=pY: e.matmul(
                            pY[:, 512 * ns:512 * ns + 512],
                            orbS[0:1, :], b2B[0:1, 512 * ns:512 * ns + 512],
                            start=False, stop=True, skip_group_check=True),
                          reads=["c.orb", "r.b2B"], writes=[ytok])
                    A("scalar", lambda e, pY=pY: e.activation(
                        scrA[:, :], pY[:, :], AF.Tanh, scale=0.5),
                      reads=[ytok], writes=["b.scrA"])
                    A("vector", lambda e, pY=pY: e.scalar_tensor_tensor(
                        scrB[:, :], pY[:, :], 0.5, scrA[:, :], ALU.mult, ALU.mult),
                      reads=[ytok, "b.scrA"], writes=["b.scrB"])
                    A("vector", lambda e, pY=pY: e.scalar_tensor_tensor(
                        scrA[:, :], pY[:, :], 0.5, scrB[:, :], ALU.mult, ALU.add),
                      reads=[ytok, "b.scrB"], writes=["b.scrA"])
                    A("gpsimd", lambda e, sb=sb: e.tensor_tensor(
                        scrB[:, :], sb[:, :], scrA[:, :], ALU.mult),
                      reads=[stok, "b.scrA"], writes=["b.scrB"])
                    A("gpsimd", lambda e, sb=sb, xb=xb: e.tensor_tensor(
                        sb[:, :], xb[:, :], scrB[:, :], ALU.add),
                      reads=[xtok, "b.scrB"], writes=[stok])
                    A("gpsimd", lambda e, gc=gc, it=it, sb=sb: e.dma_start(
                        out=out[gc + 128 * it:gc + 128 * it + 128, :], in_=sb[:, :]),
                      reads=[stok], writes=["d.out.%d" % nout], dma=True)
                    nout += 1
            A("scalar", None, reads=["d.out.%d" % i for i in range(nout)])

        _p3.close()
        def SEM(nm):
            return _st.enter_context(nc.semaphore(nm))
        sems = {"c_tensor": SEM("sem_t"), "c_vector": SEM("sem_v"),
                "c_scalar": SEM("sem_s"), "c_gpsimd": SEM("sem_g"),
                "c_sync": SEM("sem_sy")}
        for qq in ("sync", "gpsimd", "scalar"):
            for sl in range(8):
                sems["dma_%s_%d" % (qq, sl)] = SEM("dq_%s_%d" % (qq, sl))
        for sl in range(8):
            sems["gps_%d" % sl] = SEM("gpc_%d" % sl)
        with nc.Block() as block:
            p.emit(nc, block, sems, serial=bool(globals().get("SERIAL", False)))
        _st.close()
    return nc



def _pm(a, nt, n):
    return np.ascontiguousarray(
        np.asarray(a, np.float32).reshape(nt, 128, n)
        .transpose(1, 0, 2).reshape(128, nt * n))


def shared_inputs(inputs):
    f32 = np.float32

    def row(v):
        return np.ascontiguousarray(np.asarray(v, f32).reshape(1, -1))

    def colT(v, nt):
        return np.ascontiguousarray(np.asarray(v, f32).reshape(nt, 128).T)

    return dict(
        Wq=_pm(inputs["L0_attn_Wq"], NDM, DM),
        Wk=_pm(inputs["L0_attn_Wk"], NDM, DM),
        Wv=_pm(inputs["L0_attn_Wv"], NDM, DM),
        Wg=_pm(inputs["L0_attn_Wg"], NDM, DM),
        Wo=_pm(inputs["L0_attn_Wo"], NDM, DM),
        w1=_pm(inputs["L0_ffn_w1"], NDM, DFF),
        w2=_pm(inputs["L0_ffn_w2"], NDH, DM),
        b_v=row(inputs["L0_attn_bv"]), b_k=row(inputs["L0_attn_bk"]),
        b_o=row(inputs["L0_attn_bo"]), b_2=row(inputs["L0_ffn_b2"]),
        bqT=colT(inputs["L0_attn_bq"], NDM), bkT=colT(inputs["L0_attn_bk"], NDM),
        bgT=colT(inputs["L0_attn_bg"], NDM), b1T=colT(inputs["L0_ffn_b1"], NDH),
        gmT=colT(inputs["L0_attn_gamma"], NDM), btT=colT(inputs["L0_attn_beta"], NDM),
    )


def core_inputs(inputs, core, shared=None):
    f32 = np.float32
    x = np.asarray(inputs["L0_attn_x"])
    b, seg = core // 4, core % 4
    t0 = SEG * seg
    xseg = np.ascontiguousarray(x[b, t0:t0 + SEG], f32)
    xpre = np.zeros((PRE, DM), f32)
    lo = max(0, t0 - PRE)
    if t0 > 0:
        xpre[PRE - (t0 - lo):] = x[b, lo:t0]
    w = np.arange(PRE, dtype=np.float64)
    c = D ** (PRE - w)
    c[w < PRE - t0] = 0.0
    csqrt = np.sqrt(c)
    cpre_l = np.ascontiguousarray(csqrt.reshape(NPT, 128).T.astype(f32))
    csum_l = np.array([[c.sum()]], dtype=f32)
    m = dict(shared if shared is not None else shared_inputs(inputs))
    m.update(xseg=_pm(xseg, SEG // 128, DM), xpre=_pm(xpre, NPT, DM),
             cpre=cpre_l, csum=csum_l)
    return m


def kernel(**inputs):
    shared = shared_inputs(inputs)
    in_maps = [core_inputs(inputs, core, shared) for core in range(8)]

    nc = build_nc()
    res = run_bass_kernel_spmd(nc, in_maps, core_ids=list(range(8)))
    shards = [res.results[i]["out"] for i in range(8)]
    full = np.concatenate(shards, axis=0).reshape(2, 8192, 1024)
    return np.ascontiguousarray(full.astype(np.float32))


if __name__ == "__main__":
    rng = np.random.default_rng(0)
    ins = {
        "L0_attn_x": rng.standard_normal((2, 8192, 1024)).astype(np.float32),
    }
    for nm in ["Wq", "Wk", "Wv", "Wg", "Wo"]:
        ins["L0_attn_" + nm] = (rng.standard_normal((1024, 1024)) * 0.02).astype(np.float32)
    for nm in ["bq", "bk", "bv", "bg", "bo", "gamma", "beta"]:
        ins["L0_attn_" + nm] = rng.standard_normal(1024).astype(np.float32) * 0.02
    ins["L0_attn_gamma"] += 1.0
    ins["L0_ffn_w1"] = (rng.standard_normal((1024, 4096)) / 32).astype(np.float32)
    ins["L0_ffn_b1"] = (rng.standard_normal(4096) * 0.02).astype(np.float32)
    ins["L0_ffn_w2"] = (rng.standard_normal((4096, 1024)) / 64).astype(np.float32)
    ins["L0_ffn_b2"] = (rng.standard_normal(1024) * 0.02).astype(np.float32)
    o = kernel(**ins)
    print("kernel ran", o.shape, o.dtype)


# version 22
# speedup vs baseline: 1.2056x; 1.2056x over previous
import numpy as np
import ml_dtypes
import concourse.bass as bass
import concourse.mybir as mybir
from concourse.bass_utils import run_bass_kernel_spmd

F32 = mybir.dt.float32
BF16 = mybir.dt.bfloat16
I32 = mybir.dt.int32
AF = mybir.ActivationFunctionType
ALU = mybir.AluOpType
AX = mybir.AxisListType

D = 0.99
SEG = 2048
PRE = 3072
GRP = 256
NG = SEG // GRP
NPT = PRE // 128
DM = 1024
NDM = DM // 128
NH, DH = 16, 64
NP = NH // 2
DFF = 4096
NDH = DFF // 128
EPS = 1e-5
MAGIC = 0x5f3759df
D256 = float(D ** 256)


class Prog:
    def __init__(self):
        self.ops = []
        self._prev_psums = ()
        self._opened = set()

    def psum_scope(self, prev_tokens):
        self._prev_psums = tuple(prev_tokens)
        self._opened = set()

    def add(self, eng, fn, reads=(), writes=(), dma=False):
        writes = list(writes)
        for t in list(writes):
            if t.startswith("p.") and t not in self._opened:
                self._opened.add(t)
                writes += self._prev_psums
        self.ops.append(dict(eng=eng, fn=fn, reads=tuple(reads),
                             writes=tuple(writes), dma=dma, idx=len(self.ops)))

    def emit(self, nc, block, sems, serial=False):
        ops = self.ops
        NSLOT = 8
        dcount = {}
        gcount = 0
        for op in ops:
            if op["dma"]:
                k = dcount.get(op["eng"], 0)
                dcount[op["eng"]] = k + 1
                op["slot"] = k % NSLOT
                op["semval"] = 16 * (k // NSLOT + 1)
                op["inc"] = 16
            elif op["eng"] == "gpsimd" and op["fn"] is not None:
                k = gcount
                gcount += 1
                op["slot"] = k % NSLOT
                op["semval"] = k // NSLOT + 1
                op["inc"] = 1
        last_w = {}
        readers = {}
        for op in ops:
            deps = set()
            if serial and op["idx"] > 0:
                deps.add(op["idx"] - 1)
            for t in op["reads"]:
                if t in last_w:
                    deps.add(last_w[t])
            for t in op["writes"]:
                if t in last_w:
                    deps.add(last_w[t])
                for r in readers.get(t, ()):
                    deps.add(r)
            deps.discard(op["idx"])
            fd = set()
            for d in deps:
                dop = ops[d]
                if (not dop["dma"] and not op["dma"]
                        and dop["eng"] == "tensor" and op["eng"] == "tensor"):
                    continue
                fd.add(d)
            bysrc = {}
            for d in fd:
                dop = ops[d]
                if dop["dma"] or dop["eng"] == "gpsimd":
                    key = (dop["eng"], dop["dma"], dop.get("slot"))
                else:
                    key = (dop["eng"],)
                if key not in bysrc or d > bysrc[key]:
                    bysrc[key] = d
            op["deps"] = sorted(bysrc.values())
            for t in op["reads"]:
                readers.setdefault(t, []).append(op["idx"])
            for t in op["writes"]:
                last_w[t] = op["idx"]
                readers[t] = []
        need_inc = set()
        for op in ops:
            for d in op["deps"]:
                if not ops[d]["dma"]:
                    need_inc.add(d)
        ccount = {}
        for op in ops:
            if op["dma"] or op["eng"] == "gpsimd":
                pass
            elif op["idx"] in need_inc:
                ccount[op["eng"]] = ccount.get(op["eng"], 0) + 1
                op["semval"] = ccount[op["eng"]]
                op["inc"] = 1
            else:
                op["semval"] = None
                op["inc"] = 0
        hw = {}

        def semkey(dop):
            if dop["dma"]:
                return "dma_%s_%d" % (dop["eng"], dop["slot"])
            if dop["eng"] == "gpsimd":
                return "gps_%d" % dop["slot"]
            return "c_" + dop["eng"]

        def make_section(name):
            def section(e):
                for op in ops:
                    if op["eng"] != name:
                        continue
                    for d in op["deps"]:
                        dop = ops[d]
                        sk = semkey(dop)
                        val = dop["semval"]
                        assert val is not None
                        if hw.get((name, sk), -1) >= val:
                            continue
                        e.wait_ge(sems[sk], val)
                        hw[(name, sk)] = val
                    if op["fn"] is not None:
                        if op["dma"] or op["eng"] == "gpsimd":
                            sk = semkey(op)
                            prev = op["semval"] - (16 if op["dma"] else 1)
                            if prev > 0 and hw.get((name, sk), -1) < prev:
                                e.wait_ge(sems[sk], prev)
                                hw[(name, sk)] = prev
                        ins = op["fn"](e)
                        if op["inc"]:
                            ins.then_inc(sems[semkey(op)], op["inc"])
                    else:
                        assert op["inc"] == 0
            return section

        for name in ["tensor", "vector", "scalar", "gpsimd", "sync"]:
            getattr(block, name)(make_section(name))



def _host_consts():
    c = {}
    c["ident"] = np.eye(128, dtype=ml_dtypes.bfloat16)
    c["ocf"] = np.ones((128, 1), dtype=np.float32)
    c["ocb"] = np.ones((128, 1), dtype=ml_dtypes.bfloat16)
    c["orb"] = np.ones((1, 128), dtype=ml_dtypes.bfloat16)
    c["orf"] = np.ones((1, 128), dtype=np.float32)
    mask = np.ones((128, 256), dtype=np.float32)
    tri = (np.arange(128)[None, :] >= np.arange(128)[:, None]).astype(np.float32)
    mask[:, 0:128] = tri
    c["mask"] = mask
    i = np.arange(GRP, dtype=np.float64)
    c["qdec"] = np.tile((D ** i / 8.0).astype(np.float32), (128, 1))
    c["kdec"] = np.tile((D ** (-i)).astype(np.float32), (128, 1))
    p = np.arange(128, dtype=np.float64)
    kus = np.stack([D ** (256 - (128 * s + p)) for s in range(2)], axis=1)
    c["kus"] = kus.astype(np.float32)
    return c



def build_nc(dbg=False, skip_p0=False):
    nc = bass.Bass("TRN2", target_bir_lowering=False)

    xseg = nc.dram_tensor("xseg", [128, (SEG // 128) * DM], F32, kind="ExternalInput")
    xpre = nc.dram_tensor("xpre", [128, NPT * DM], F32, kind="ExternalInput")
    cpre = nc.dram_tensor("cpre", [128, NPT], F32, kind="ExternalInput")
    csum = nc.dram_tensor("csum", [1, 1], F32, kind="ExternalInput")
    Wq = nc.dram_tensor("Wq", [128, NDM * DM], F32, kind="ExternalInput")
    Wk = nc.dram_tensor("Wk", [128, NDM * DM], F32, kind="ExternalInput")
    Wv = nc.dram_tensor("Wv", [128, NDM * DM], F32, kind="ExternalInput")
    Wg = nc.dram_tensor("Wg", [128, NDM * DM], F32, kind="ExternalInput")
    Wo = nc.dram_tensor("Wo", [128, NDM * DM], F32, kind="ExternalInput")
    w1 = nc.dram_tensor("w1", [128, NDM * DFF], F32, kind="ExternalInput")
    w2 = nc.dram_tensor("w2", [128, NDH * DM], F32, kind="ExternalInput")
    b_v = nc.dram_tensor("b_v", [1, DM], F32, kind="ExternalInput")
    b_k = nc.dram_tensor("b_k", [1, DM], F32, kind="ExternalInput")
    b_o = nc.dram_tensor("b_o", [1, DM], F32, kind="ExternalInput")
    b_2 = nc.dram_tensor("b_2", [1, DM], F32, kind="ExternalInput")
    bqT = nc.dram_tensor("bqT", [128, NDM], F32, kind="ExternalInput")
    bkT = nc.dram_tensor("bkT", [128, NDM], F32, kind="ExternalInput")
    bgT = nc.dram_tensor("bgT", [128, NDM], F32, kind="ExternalInput")
    b1T = nc.dram_tensor("b1T", [128, NDH], F32, kind="ExternalInput")
    gmT = nc.dram_tensor("gmT", [128, NDM], F32, kind="ExternalInput")
    btT = nc.dram_tensor("btT", [128, NDM], F32, kind="ExternalInput")
    out = nc.dram_tensor("out", [SEG, DM], F32, kind="ExternalOutput")
    s_scr = nc.dram_tensor("s_scr", [SEG, DM], F32, kind="Internal")
    w1b = nc.dram_tensor("w1b", [128, NDM * DFF], BF16, kind="Internal")
    w2b = nc.dram_tensor("w2b", [128, NDH * DM], BF16, kind="Internal")
    if dbg:
        d_S0 = nc.dram_tensor("d_S0", [128, 512], F32, kind="ExternalOutput")
        d_S1 = nc.dram_tensor("d_S1", [128, 512], F32, kind="ExternalOutput")
        d_xnT = nc.dram_tensor("d_xnT", [128, 16384], BF16, kind="ExternalOutput")
        d_Qs = nc.dram_tensor("d_Qs", [128, 2048], BF16, kind="ExternalOutput")
        d_Ks = nc.dram_tensor("d_Ks", [128, 2048], BF16, kind="ExternalOutput")
        d_V = nc.dram_tensor("d_V", [128, 2048], BF16, kind="ExternalOutput")
        d_Ku = nc.dram_tensor("d_Ku", [128, 2048], BF16, kind="ExternalOutput")
        d_Gt = nc.dram_tensor("d_Gt", [128, 2048], F32, kind="ExternalOutput")
        d_rsb = nc.dram_tensor("d_rsb", [128, 2048], F32, kind="ExternalOutput")
        d_As = nc.dram_tensor("d_As", [128, 2048], BF16, kind="ExternalOutput")
        d_s = nc.dram_tensor("d_s", [SEG, DM], F32, kind="ExternalOutput")
        d_hT = nc.dram_tensor("d_hT", [128, 4096], BF16, kind="ExternalOutput")
        d_Pm = nc.dram_tensor("d_Pm", [128, 256], BF16, kind="ExternalOutput")
        d_Wq = nc.dram_tensor("d_Wq", [128, 8192], BF16, kind="ExternalOutput")
        d_PJ = nc.dram_tensor("d_PJ", [128, 256], F32, kind="ExternalOutput")
        d_xnb = nc.dram_tensor("d_xnb", [128, 1024], BF16, kind="ExternalOutput")
        d_Wk = nc.dram_tensor("d_Wk", [128, 8192], BF16, kind="ExternalOutput")

    hc = _host_consts()
    ident_d = nc.inline_tensor(hc["ident"], name="identc")
    ocf_d = nc.inline_tensor(hc["ocf"], name="ocfc")
    ocb_d = nc.inline_tensor(hc["ocb"], name="ocbc")
    orb_d = nc.inline_tensor(hc["orb"], name="orbc")
    orf_d = nc.inline_tensor(hc["orf"], name="orfc")
    mask_d = nc.inline_tensor(hc["mask"], name="maskc")
    qdec_d = nc.inline_tensor(hc["qdec"], name="qdecc")
    kdec_d = nc.inline_tensor(hc["kdec"], name="kdecc")
    kus_d = nc.inline_tensor(hc["kus"], name="kusc")

    p = Prog()
    A = p.add

    from contextlib import ExitStack
    _st = ExitStack()
    def SB(name, shape, dtype):
        return _st.enter_context(nc.sbuf_tensor(name, shape, dtype))
    if True:
        WB1 = SB("WB1", [128, 32768], BF16)
        WB2 = SB("WB2", [128, 32768], BF16)
        xnT = SB("xnT", [128, 16384], BF16)
        xg01 = SB("xg01", [128, 2048], F32)
        xg23 = SB("xg23", [128, 2048], F32)
        xg0 = xg01[:, 0:1024]
        xg1 = xg01[:, 1024:2048]
        state = SB("state", [128, 512], F32)
        sb16 = SB("sb16", [128, 512], BF16)
        xnb = SB("xnb", [128, 1024], BF16)
        tA2F = SB("tA2F", [128, 256], F32)
        tB2F = SB("tB2F", [128, 256], F32)
        identS = SB("identS", [128, 128], BF16)
        ocfS = SB("ocfS", [128, 1], F32)
        orbS = SB("orbS", [1, 128], BF16)
        orfS = SB("orfS", [1, 128], F32)
        maskS = SB("maskS", [128, 256], F32)
        qdecS = SB("qdecS", [128, GRP], F32)
        kdecS = SB("kdecS", [128, GRP], F32)
        kusS = SB("kusS", [128, 2], F32)
        cpreS = SB("cpreS", [128, NPT], F32)
        cpreB = SB("cpreB", [128, NPT], BF16)
        csumS = SB("csumS", [1, 1], F32)
        bvB = SB("bvB", [1, DM], BF16)
        bkB = SB("bkB", [1, DM], BF16)
        boB = SB("boB", [1, DM], BF16)
        b2B = SB("b2B", [1, DM], BF16)
        bqTs = SB("bqTs", [128, NDM], F32)
        bkTs = SB("bkTs", [128, NDM], F32)
        bgTs = SB("bgTs", [128, NDM], F32)
        b1Ts = SB("b1Ts", [128, NDH], F32)
        gmTs = SB("gmTs", [128, NDM], F32)
        btTs = SB("btTs", [128, NDM], F32)
        mcolB = SB("mcolB", [128, NDM], BF16)
        vpe = SB("vpe", [128, 4], F32)
        nrY = SB("nrY", [128, 4], F32)
        nrT = SB("nrT", [128, 4], F32)
        bnst = SB("bnst", [128, 24], F32)
        mvc = SB("mvc", [128, 4], F32)
    if True:
        def wb1(c0, n):
            return WB1[:, c0:c0 + n]
        def wb2(c0, n):
            return WB2[:, c0:c0 + n]
        WkS = lambda c0, n: WB1[:, 0 + c0: 0 + c0 + n]
        WvS = lambda c0, n: WB1[:, 8192 + c0: 8192 + c0 + n]
        MS = lambda c0, n: WB1[:, 16384 + c0: 16384 + c0 + n]
        TS = lambda c0, n: WB1[:, 24576 + c0: 24576 + c0 + n]
        WqS = lambda c0, n: WB1[:, 16384 + c0: 16384 + c0 + n]
        WgS = lambda c0, n: WB1[:, 24576 + c0: 24576 + c0 + n]
        WoS = lambda c0, n: WB2[:, 0 + c0: 0 + c0 + n]
        XH = lambda c0, n: WB2[:, 8192 + c0: 8192 + c0 + n]
        zB = WB2[0:1, 8192 + 1024 * 6:8192 + 1024 * 6 + 1024]
        bvsB = WB2[0:1, 8192 + 1024 * 7:8192 + 1024 * 7 + 1024]
        mstage = WB2[:, 8192 + 1024 * 8:8192 + 1024 * 8 + 1024]
        mrowB = WB2[0:1, 8192 + 1024 * 9:8192 + 1024 * 9 + 1024]
        QsS = WB2[:, 8192:10240]
        KsS = WB2[:, 10240:12288]
        VS = WB2[:, 12288:14336]
        KuS = WB2[:, 14336:16384]
        AsS = WB2[:, 16384:18432]
        Pm = [WB2[:, 18432:18688], WB2[:, 18688:18944],
              WB2[:, 18944:19072], WB2[:, 19072:19200]]
        SPARE = WB2[:, 19200:32768].bitcast(F32)
        GtS = SPARE[:, 0:2048]
        rsbS = SPARE[:, 2048:4096]
        sstgS = SPARE[:, 4096:5120]
        sqsbS = SPARE[:, 5120:5376]
        tAS = SPARE[:, 5376:5632]
        tBS = SPARE[:, 5632:5888]
        tA2S = tA2F[:, :]
        tB2S = tB2F[:, :]
        vR = SPARE[0:1, 5888:6144]
        yR = SPARE[0:1, 6144:6400]
        nmrR = SPARE[0:1, 6400:6656]
        nmuR = tAS[0:1, 0:256]
        tR = sqsbS[0:1, 0:256]
        WB2_A_TOKENS = ("b.Qs", "b.Ks", "b.V", "b.Ku", "b.As", "b.Pm0", "b.Pm1",
                        "b.Gt", "b.rsb", "b.sstg", "b.sqsb", "b.tA", "b.tB")

        w1S = lambda c0, n: WB1[:, c0:c0 + n]

        def w2S(c0, n):
            b, off = divmod(c0, 1024)
            if b < 8:
                pb = 8192 + 1024 * b
            elif b < 24:
                pb = 16384 + 1024 * (b - 8)
            else:
                pb = 1024 * (b - 24)
            return WB2[:, pb + off:pb + off + n]

        for nm, dst, src in [
            ("c.ident", identS, ident_d), ("c.ocf", ocfS, ocf_d),
            ("c.orb", orbS, orb_d),
            ("c.orf", orfS, orf_d), ("c.mask", maskS, mask_d),
            ("c.qdec", qdecS, qdec_d), ("c.kdec", kdecS, kdec_d),
            ("c.kus", kusS, kus_d), ("r.cpre", cpreS, cpre),
            ("r.csum", csumS, csum),
            ("r.bqT", bqTs, bqT), ("r.bkT", bkTs, bkT),
            ("r.bgT", bgTs, bgT), ("r.b1T", b1Ts, b1T),
            ("r.gmT", gmTs, gmT), ("r.btT", btTs, btT),
        ]:
            A("sync", lambda e, d=dst, s=src: e.dma_start(out=d[:, :], in_=s[:, :]),
              writes=[nm], dma=True)
        for nm, dst, src in [
            ("r.cpreB", cpreB, cpre), ("r.bvB", bvB, b_v), ("r.bkB", bkB, b_k),
            ("r.boB", boB, b_o), ("r.b2B", b2B, b_2),
        ]:
            A("gpsimd", lambda e, d=dst, s=src: e.dma_start(out=d[:, :], in_=s[:, :]),
              writes=[nm], dma=True)

        def ln_stats(xbuf, xtok, col):
            cs = (col, col + 1)
            sx = col // 2
            A("scalar", lambda e: e.activation(
                xnb[:, :], xbuf[:, :], AF.Copy,
                accum_out=nsum[:, cs[0]:cs[1]]),
              reads=[xtok], writes=["b.xnb", "b.nsum.%d" % sx])
            A("vector", lambda e: e.tensor_scalar(
                nmu[:, cs[0]:cs[1]], nsum[:, cs[0]:cs[1]], -1.0 / DM, None,
                ALU.mult),
              reads=["b.nsum.%d" % sx], writes=["b.nmu.%d" % sx])
            A("scalar", lambda e: e.activation(
                xnb[:, :], xbuf[:, :], AF.Square, bias=nmu[:, cs[0]:cs[1]],
                accum_out=v1c[:, cs[0]:cs[1]]),
              reads=[xtok, "b.nmu.%d" % sx], writes=["b.xnb", "b.v1c.%d" % sx])

        def ln_nr(c0, ncols):
            cs = (c0, c0 + ncols)
            sx = c0 // 2
            A("vector", lambda e: e.tensor_scalar(
                vpe[:, cs[0]:cs[1]], v1c[:, cs[0]:cs[1]], 1.0 / DM, EPS,
                ALU.mult, ALU.add),
              reads=["b.v1c.%d" % sx], writes=["b.vpe.%d" % sx])
            ln_nr_core(c0, ncols)

        def ln_nr_core(c0, ncols):
            cs = (c0, c0 + ncols)
            sx = c0 // 2
            A("vector", lambda e: e.tensor_scalar(
                nrY.bitcast(I32)[:, cs[0]:cs[1]], vpe.bitcast(I32)[:, cs[0]:cs[1]],
                1, None, ALU.logical_shift_right),
              reads=["b.vpe.%d" % sx], writes=["b.nrY.%d" % sx])
            A("vector", lambda e: e.tensor_scalar(
                nrY.bitcast(I32)[:, cs[0]:cs[1]], nrY.bitcast(I32)[:, cs[0]:cs[1]],
                -1, None, ALU.bitwise_xor),
              reads=["b.nrY.%d" % sx], writes=["b.nrY.%d" % sx])
            A("vector", lambda e: e.tensor_scalar(
                nrY.bitcast(I32)[:, cs[0]:cs[1]], nrY.bitcast(I32)[:, cs[0]:cs[1]],
                MAGIC + 1, None, ALU.add),
              reads=["b.nrY.%d" % sx], writes=["b.nrY.%d" % sx])
            for _ in range(2):
                A("vector", lambda e: e.tensor_tensor(
                    nrT[:, cs[0]:cs[1]], nrY[:, cs[0]:cs[1]], nrY[:, cs[0]:cs[1]],
                    ALU.mult), reads=["b.nrY.%d" % sx], writes=["b.nrT.%d" % sx])
                A("vector", lambda e: e.tensor_tensor(
                    nrT[:, cs[0]:cs[1]], nrT[:, cs[0]:cs[1]], vpe[:, cs[0]:cs[1]],
                    ALU.mult),
                  reads=["b.nrT.%d" % sx, "b.vpe.%d" % sx],
                  writes=["b.nrT.%d" % sx])
                A("vector", lambda e: e.tensor_scalar(
                    nrT[:, cs[0]:cs[1]], nrT[:, cs[0]:cs[1]], -0.5, 1.5,
                    ALU.mult, ALU.add),
                  reads=["b.nrT.%d" % sx], writes=["b.nrT.%d" % sx])
                A("vector", lambda e: e.tensor_tensor(
                    nrY[:, cs[0]:cs[1]], nrY[:, cs[0]:cs[1]], nrT[:, cs[0]:cs[1]],
                    ALU.mult),
                  reads=["b.nrY.%d" % sx, "b.nrT.%d" % sx],
                  writes=["b.nrY.%d" % sx])

        def ln_finish(xbuf, xtok, col, out_ap, out_tok, scale_col=None, extra=()):
            cs = (col, col + 1)
            sx = col // 2
            rstd_tok = "b.nrY.%d" % sx
            rstd_ap = nrY
            if scale_col is not None:
                ap, rtok = scale_col
                A("vector", lambda e: e.tensor_tensor(
                    nrT[:, cs[0]:cs[1]], nrY[:, cs[0]:cs[1]], ap, ALU.mult),
                  reads=["b.nrY.%d" % sx, rtok], writes=["b.nrT.%d" % sx])
                rstd_tok = "b.nrT.%d" % sx
                rstd_ap = nrT
            A("vector", lambda e: e.tensor_scalar(
                out_ap, xbuf[:, :], nmu[:, cs[0]:cs[1]], rstd_ap[:, cs[0]:cs[1]],
                ALU.add, ALU.mult),
              reads=[xtok, "b.nmu.%d" % sx, rstd_tok] + list(extra),
              writes=[out_tok])

        def ln_bn(xbuf, xtok, t2):
            A("vector", lambda e: e.bn_stats(
                bnst[:, 12 * t2:12 * t2 + 6], xbuf[:, 0:512]),
              reads=[xtok], writes=["b.bn.%d" % t2])
            A("vector", lambda e: e.bn_stats(
                bnst[:, 12 * t2 + 6:12 * t2 + 12], xbuf[:, 512:1024]),
              reads=[xtok], writes=["b.bn.%d" % t2])
            A("vector", lambda e: e.bn_aggr(
                mvc[:, 2 * t2:2 * t2 + 2], bnst[:, 12 * t2:12 * t2 + 12]),
              reads=["b.bn.%d" % t2], writes=["b.mv.%d" % t2])

        def ln_bn_vpe(c0):
            mvv = mvc.rearrange("p (t two) -> p t two", two=2)[:, :, 1:2]
            A("vector", lambda e: e.tensor_scalar(
                vpe[:, c0:c0 + 2], mvv, EPS, None, ALU.add),
              reads=["b.mv.0", "b.mv.1"], writes=["b.vpe.%d" % (c0 // 2)])

        def ln_bn_fin(xbuf, xtok, t2, col, out_ap, out_tok, scale_col=None,
                      extra=()):
            cs = (col, col + 1)
            sx = col // 2
            rstd_tok = "b.nrY.%d" % sx
            rstd_ap = nrY
            if scale_col is not None:
                ap, rtok = scale_col
                A("vector", lambda e: e.tensor_tensor(
                    nrT[:, cs[0]:cs[1]], nrY[:, cs[0]:cs[1]], ap, ALU.mult),
                  reads=["b.nrY.%d" % sx, rtok], writes=["b.nrT.%d" % sx])
                rstd_tok = "b.nrT.%d" % sx
                rstd_ap = nrT
            A("vector", lambda e, rstd_ap=rstd_ap: e.tensor_scalar(
                out_ap, xbuf[:, :], mvc[:, 2 * t2:2 * t2 + 1],
                rstd_ap[:, cs[0]:cs[1]], ALU.subtract, ALU.mult),
              reads=[xtok, "b.mv.%d" % t2, rstd_tok] + list(extra),
              writes=[out_tok])

        _p0 = ExitStack()
        pM0 = _p0.enter_context(nc.psum_tensor("pM0", [128, 1024], F32))
        pM1 = _p0.enter_context(nc.psum_tensor("pM1", [128, 1024], F32))
        pMR = _p0.enter_context(nc.psum_tensor("pMR", [1, 1024], F32))
        pTR = _p0.enter_context(nc.psum_tensor("pTR", [128, 512], BF16))
        pSP = _p0.enter_context(nc.psum_tensor("pSP", [128, 64], F32))
        if True:
            def stageE_pre(g):
                A("sync", lambda e, g=g: e.dma_start(
                    out=xg01[:, :], in_=xseg[:, 2048 * g:2048 * g + 2048]),
                  writes=["xg0", "xg1"], dma=True)
                for t in range(2):
                    xb = xg0 if t == 0 else xg1
                    ln_bn(xb, "xg%d" % t, t)
                ln_bn_vpe(0)
                ln_nr_core(0, 2)

            def stageE_fin(g, trp=None):
                gc = GRP * g
                trp = trp if trp is not None else (
                    (pTR2[:, 0:512], "p.SCA"), (pSCBb[:, 0:512], "p.SCB"))
                for t in range(2):
                    xb = xg0 if t == 0 else xg1
                    xtok = "xg%d" % t
                    ln_bn_fin(xb, xtok, t, t, xnb[:, :], "b.xnb")
                    if dbg and g == 0 and t == 0:
                        A("sync", lambda e: e.dma_start(out=d_xnb[:, :], in_=xnb[:, :]),
                          reads=["b.xnb"], writes=["d.out"], dma=True)
                    for c4 in range(2):
                        pT, pTtok = trp[(2 * t + c4) % 2]
                        for bb in range(4):
                            A("tensor", lambda e, c4=c4, bb=bb, pT=pT: e.transpose(
                                pT[:, 128 * bb:128 * bb + 128],
                                xnb[:, 512 * c4 + 128 * bb:512 * c4 + 128 * bb + 128],
                                identS[:, :]),
                              reads=["b.xnb", "c.ident"], writes=[pTtok])
                        A("scalar", lambda e, c4=c4, gc=gc, t=t, pT=pT: e.copy(
                            xnT[:, :].rearrange("p (t n) -> p t n", n=2048)
                               [:, 4 * c4:4 * c4 + 4, gc + 128 * t:gc + 128 * t + 128],
                            pT[:, 0:512].rearrange("p (t n) -> p t n", n=128)),
                          reads=[pTtok], writes=["xnT"])

            def stageE(g, trp=None):
                stageE_pre(g)
                stageE_fin(g, trp=trp)

            def stageM1(g):
                gc = GRP * g
                for it in range(2):
                    for dm in range(NDM):
                        for ns in range(2):
                            A("tensor", lambda e, it=it, dm=dm, ns=ns, gc=gc: e.matmul(
                                pPJ[:, 512 * ns:512 * ns + 512],
                                xnT[:, 2048 * dm + gc + 128 * it:2048 * dm + gc + 128 * it + 128],
                                WvS(1024 * dm + 512 * ns, 512),
                                start=(dm == 0), stop=False, skip_group_check=True),
                              reads=["xnT", "w.v"], writes=["p.PJ0", "p.PJ1"])
                    for ns in range(2):
                        A("tensor", lambda e, ns=ns: e.matmul(
                            pPJ[:, 512 * ns:512 * ns + 512],
                            orbS[0:1, :], bvB[0:1, 512 * ns:512 * ns + 512],
                            start=False, stop=True, skip_group_check=True),
                          reads=["c.orb", "r.bvB"], writes=["p.PJ0", "p.PJ1"])
                    A("scalar", lambda e, it=it: e.copy(
                        VS[:, 1024 * it:1024 * it + 1024], pPJ[:, :]),
                      reads=["p.PJ0", "p.PJ1"], writes=["b.V"])
                    for dm in range(NDM):
                        for ns in range(2):
                            A("tensor", lambda e, it=it, dm=dm, ns=ns, gc=gc: e.matmul(
                                pPJ[:, 512 * ns:512 * ns + 512],
                                xnT[:, 2048 * dm + gc + 128 * it:2048 * dm + gc + 128 * it + 128],
                                WkS(1024 * dm + 512 * ns, 512),
                                start=(dm == 0), stop=False, skip_group_check=True),
                              reads=["xnT", "w.k"], writes=["p.PJ0", "p.PJ1"])
                    for ns in range(2):
                        A("tensor", lambda e, ns=ns: e.matmul(
                            pPJ[:, 512 * ns:512 * ns + 512],
                            orbS[0:1, :], bkB[0:1, 512 * ns:512 * ns + 512],
                            start=False, stop=True, skip_group_check=True),
                          reads=["c.orb", "r.bkB"], writes=["p.PJ0", "p.PJ1"])
                    A("vector", lambda e, it=it: e.tensor_scalar(
                        KuS[:, 1024 * it:1024 * it + 1024], pPJ[:, :],
                        kusS[:, it:it + 1], None, ALU.mult),
                      reads=["p.PJ0", "p.PJ1", "c.kus"], writes=["b.Ku"])
                for which, wsl, wtok in (("q", WqS, "w.q"), ("k", WkS, "w.k")):
                    for dt in range(NDM):
                        half = (dt % 2) * 512
                        ptok = "p.PJ%d" % (dt % 2)
                        for dm in range(NDM):
                            A("tensor", lambda e, wsl=wsl, dt=dt, dm=dm, half=half, gc=gc: e.matmul(
                                pPJ[:, half:half + GRP],
                                wsl(1024 * dm + 128 * dt, 128),
                                xnT[:, 2048 * dm + gc:2048 * dm + gc + GRP],
                                start=(dm == 0), stop=(dm == NDM - 1),
                                skip_group_check=True),
                              reads=[wtok, "xnT"], writes=[ptok])
                        if which == "q":
                            if dbg and g == 0 and dt == 0:
                                A("scalar", lambda e: e.copy(sqsbS[:, :], pPJ[:, 0:256]),
                                  reads=[ptok], writes=["b.sqsb"])
                                A("sync", lambda e: e.dma_start(out=d_PJ[:, :], in_=sqsbS[:, :]),
                                  reads=["b.sqsb"], writes=["d.out"], dma=True)
                            A("vector", lambda e, dt=dt, half=half: e.scalar_tensor_tensor(
                                QsS[:, GRP * dt:GRP * dt + GRP],
                                pPJ[:, half:half + GRP],
                                bqTs[:, dt:dt + 1], qdecS[:, :],
                                ALU.add, ALU.mult),
                              reads=[ptok, "r.bqT", "c.qdec"], writes=["b.Qs"])
                        elif which == "k":
                            A("vector", lambda e, dt=dt, half=half: e.scalar_tensor_tensor(
                                KsS[:, GRP * dt:GRP * dt + GRP],
                                pPJ[:, half:half + GRP],
                                bkTs[:, dt:dt + 1], kdecS[:, :],
                                ALU.add, ALU.mult),
                              reads=[ptok, "r.bkT", "c.kdec"], writes=["b.Ks"])


            def stageM2(g):
                gc = GRP * g
                A("scalar", lambda e: e.copy(sb16[:, :], state[:, :]),
                  reads=["b.state"], writes=["b.sb16"])
                for which, wsl, wtok in (("g", WgS, "w.g"),):
                    for dt in range(NDM):
                        half = (dt % 2) * 512
                        ptok = "p.PJ%d" % (dt % 2)
                        for dm in range(NDM):
                            A("tensor", lambda e, wsl=wsl, dt=dt, dm=dm, half=half, gc=gc: e.matmul(
                                pPJ[:, half:half + GRP],
                                wsl(1024 * dm + 128 * dt, 128),
                                xnT[:, 2048 * dm + gc:2048 * dm + gc + GRP],
                                start=(dm == 0), stop=(dm == NDM - 1),
                                skip_group_check=True),
                              reads=[wtok, "xnT"], writes=[ptok])
                        A("scalar", lambda e, dt=dt, half=half: e.activation(
                            GtS[:, GRP * dt:GRP * dt + GRP],
                            pPJ[:, half:half + GRP],
                            AF.Sigmoid, bias=bgTs[:, dt:dt + 1]),
                          reads=[ptok, "r.bgT"], writes=["b.Gt.%d" % dt])
                for hp in range(NP):
                    if hp % 2 == 0:
                        jb0b, jb0t = pSCB, "p.SCB"
                        jb1b, jb1t, jb1c = pSCA, "p.SCA", 256
                    else:
                        jb0b, jb0t = pLATE, "p.LATE"
                        jb1b, jb1t, jb1c = pS, "p.S", 0
                    for hh in range(2):
                        r0 = 64 * hh
                        jb0ap = jb0b[:, 256 * hh:256 * hh + 256]
                        jb1ap = jb1b[:, jb1c + 128 * hh:jb1c + 128 * hh + 128]
                        A("tensor", lambda e, hp=hp, r0=r0, jb0ap=jb0ap: e.matmul(
                            jb0ap,
                            KsS[r0:r0 + 64, GRP * hp:GRP * hp + 128],
                            QsS[r0:r0 + 64, GRP * hp:GRP * hp + GRP],
                            start=True, stop=True, skip_group_check=True),
                          reads=["b.Ks", "b.Qs"], writes=[jb0t])
                        A("tensor", lambda e, hp=hp, r0=r0, jb1ap=jb1ap: e.matmul(
                            jb1ap,
                            KsS[r0:r0 + 64, GRP * hp + 128:GRP * hp + 256],
                            QsS[r0:r0 + 64, GRP * hp + 128:GRP * hp + 256],
                            start=True, stop=True, skip_group_check=True),
                          reads=["b.Ks", "b.Qs"], writes=[jb1t])
                        A("vector", lambda e, hh=hh, jb0ap=jb0ap: e.tensor_tensor(
                            Pm[hh][:, 0:256], jb0ap, maskS[:, 0:256],
                            ALU.mult),
                          reads=[jb0t, "c.mask"], writes=["b.Pm0%d" % hh])
                        A("vector", lambda e, hh=hh, jb1ap=jb1ap: e.tensor_tensor(
                            Pm[2 + hh][:, 0:128], jb1ap, maskS[:, 0:128],
                            ALU.mult),
                          reads=[jb1t, "c.mask"], writes=["b.Pm1%d" % hh])
                    if dbg and g == 0 and hp == 0:
                        A("sync", lambda e: e.dma_start(
                            out=d_Pm[:, :], in_=Pm[0][:, 0:256]),
                          reads=["b.Pm00"], writes=["d.out"], dma=True)
                    for hh in range(2):
                        h = 2 * hp + hh
                        ro = 64 * hh
                        A("tensor", lambda e, h=h, ro=ro, hh=hh: e.matmul(
                            pRT[ro:ro + 64, 0:256] if ro == 0 else pRT[64:128, 0:256],
                            VS[:, 64 * h:64 * h + 64],
                            Pm[hh][:, 0:256],
                            start=True, stop=False, skip_group_check=True,
                            tile_position=(0, ro)),
                          reads=["b.V", "b.Pm0%d" % hh], writes=["p.RT"])
                        A("tensor", lambda e, h=h, ro=ro, hh=hh: e.matmul(
                            pRT[ro:ro + 64, 128:256] if ro == 0 else pRT[64:128, 128:256],
                            VS[:, 1024 + 64 * h:1024 + 64 * h + 64],
                            Pm[2 + hh][:, 0:128],
                            start=False, stop=False, skip_group_check=True,
                            tile_position=(0, ro)),
                          reads=["b.V", "b.Pm1%d" % hh], writes=["p.RT"])
                        A("tensor", lambda e, h=h, ro=ro, hp=hp: e.matmul(
                            pRT[ro:ro + 64, 0:256] if ro == 0 else pRT[64:128, 0:256],
                            sb16[ro:ro + 64, 64 * hp:64 * hp + 64],
                            QsS[ro:ro + 64, GRP * hp:GRP * hp + GRP],
                            start=False, stop=(ro == 64), skip_group_check=True,
                            tile_position=(ro, ro)),
                          reads=["b.sb16", "b.Qs"], writes=["p.RT"])
                    A("scalar", lambda e, hp=hp: e.copy(
                        rsbS[:, GRP * hp:GRP * hp + GRP], pRT[:, 0:256]),
                      reads=["p.RT"], writes=["b.rsb.%d" % hp])
                    for hh in range(2):
                        h = 2 * hp + hh
                        ro = 64 * hh
                        for it in range(2):
                            A("tensor", lambda e, h=h, ro=ro, it=it, hp=hp, g=g: e.matmul(
                                pSU[ro:ro + 64, 64 * hp:64 * hp + 64] if ro == 0
                                else pSU[64:128, 64 * hp:64 * hp + 64],
                                KuS[:, 1024 * it + 64 * h:1024 * it + 64 * h + 64],
                                VS[:, 1024 * it + 64 * h:1024 * it + 64 * h + 64],
                                start=(it == 0), stop=(it == 1),
                                skip_group_check=True, tile_position=(0, ro)),
                              reads=["b.Ku", "b.V"], writes=["p.SU"])
                A("vector", lambda e: e.scalar_tensor_tensor(
                    state[:, :], state[:, :], D256, pSU[:, :], ALU.mult, ALU.add),
                  reads=["b.state", "p.SU"], writes=["b.state"])
                if dbg and g == 0:
                    A("sync", lambda e: e.dma_start(out=d_Wq[:, :], in_=WqS(0, 8192)),
                      reads=["w.q"], writes=["d.out"], dma=True)
                    A("sync", lambda e: e.dma_start(out=d_Wk[:, :], in_=WkS(0, 8192)),
                      reads=["w.k"], writes=["d.out"], dma=True)
                    for dd, sb_, tok in [(d_Qs, QsS, "b.Qs"), (d_Ks, KsS, "b.Ks"),
                                         (d_V, VS, "b.V"), (d_Ku, KuS, "b.Ku"),
                                         (d_Gt, GtS, "b.Gt"), (d_rsb, rsbS, "b.rsb"),
                                         (d_S1, state, "b.state")]:
                        A("sync", lambda e, dd=dd, sb_=sb_: e.dma_start(
                            out=dd[:, :], in_=sb_[:, 0:dd.shape[1]]),
                          reads=[tok], writes=["d.out"], dma=True)

            def stageS(g):
                gc = GRP * g
                for dt in range(NDM):
                    A("tensor", lambda e, dt=dt: e.matmul(
                        pLATE[0:1, 0:256], ocfS[:, :],
                        rsbS[:, GRP * dt:GRP * dt + GRP],
                        start=(dt == 0), stop=(dt == NDM - 1),
                        skip_group_check=True),
                      reads=["c.ocf", "b.rsb.%d" % dt], writes=["p.LATE"])
                A("vector", lambda e: e.tensor_scalar(
                    nmuR, pLATE[0:1, 0:256], -1.0 / DM, None, ALU.mult),
                  reads=["p.LATE"], writes=["b.tA0"])
                AsF = AsS.bitcast(F32)
                for dt in range(NDM):
                    j = dt % 4
                    sqap = AsF[:, 256 * j:256 * j + 256]
                    sqtok = ["b.As.%d" % (2 * j), "b.As.%d" % (2 * j + 1)]
                    A("scalar", lambda e, dt=dt, sqap=sqap: e.activation(
                        sqap, rsbS[:, GRP * dt:GRP * dt + GRP], AF.Square),
                      reads=["b.rsb.%d" % dt], writes=sqtok)
                    A("tensor", lambda e, dt=dt, sqap=sqap: e.matmul(
                        pLATE[0:1, 256:512], ocfS[:, :], sqap,
                        start=(dt == 0), stop=(dt == NDM - 1),
                        skip_group_check=True),
                      reads=["c.ocf"] + sqtok, writes=["p.LATE"])
                A("vector", lambda e: e.tensor_tensor(
                    tR, nmuR, nmuR, ALU.mult),
                  reads=["b.tA0"], writes=["b.sqsb"])
                A("vector", lambda e: e.scalar_tensor_tensor(
                    vR, pLATE[0:1, 256:512], 1.0 / DM, tR,
                    ALU.mult, ALU.subtract),
                  reads=["p.LATE", "b.sqsb"], writes=["b.vR"])
                A("vector", lambda e: e.tensor_scalar(
                    vR, vR, EPS, None, ALU.add),
                  reads=["b.vR"], writes=["b.vR"])
                A("vector", lambda e: e.tensor_scalar(
                    yR.bitcast(I32), vR.bitcast(I32), 1, None,
                    ALU.logical_shift_right), reads=["b.vR"], writes=["b.yR"])
                A("vector", lambda e: e.tensor_scalar(
                    yR.bitcast(I32), yR.bitcast(I32), -1,
                    None, ALU.bitwise_xor), reads=["b.yR"], writes=["b.yR"])
                A("vector", lambda e: e.tensor_scalar(
                    yR.bitcast(I32), yR.bitcast(I32), MAGIC + 1,
                    None, ALU.add), reads=["b.yR"], writes=["b.yR"])
                for _ in range(2):
                    A("vector", lambda e: e.tensor_tensor(
                        tR, yR, yR, ALU.mult),
                      reads=["b.yR"], writes=["b.sqsb"])
                    A("vector", lambda e: e.tensor_tensor(
                        tR, tR, vR, ALU.mult),
                      reads=["b.sqsb", "b.vR"], writes=["b.sqsb"])
                    A("vector", lambda e: e.tensor_scalar(
                        tR, tR, -0.5, 1.5, ALU.mult, ALU.add),
                      reads=["b.sqsb"], writes=["b.sqsb"])
                    A("vector", lambda e: e.tensor_tensor(
                        yR, yR, tR, ALU.mult),
                      reads=["b.yR", "b.sqsb"], writes=["b.yR"])
                A("vector", lambda e: e.tensor_tensor(
                    nmrR, nmuR, yR, ALU.mult),
                  reads=["b.tA0", "b.yR"], writes=["b.nmrR"])

            def stageL(g):
                gc = GRP * g
                A("tensor", lambda e: e.matmul(
                    pLATE[:, 0:256], orfS[0:1, :], yR,
                    start=True, stop=True, skip_group_check=True),
                  reads=["c.orf", "b.yR"], writes=["p.LATE"])
                A("tensor", lambda e: e.matmul(
                    pLATE[:, 256:512], orfS[0:1, :], nmrR,
                    start=True, stop=True, skip_group_check=True),
                  reads=["c.orf", "b.nmrR"], writes=["p.LATE"])
                for dt in range(NDM):
                    tA_ = tAS if dt % 2 == 0 else tA2S
                    tB_ = tBS if dt % 2 == 0 else tB2S
                    tat = "b.tA%d" % (dt % 2)
                    tbt = "b.tB%d" % (dt % 2)
                    A("vector", lambda e, dt=dt, tA_=tA_: e.tensor_tensor(
                        tA_[:, :], rsbS[:, GRP * dt:GRP * dt + GRP], pLATE[:, 0:256],
                        ALU.mult), reads=["b.rsb.%d" % dt, "p.LATE"], writes=[tat])
                    A("vector", lambda e, dt=dt, tA_=tA_, tB_=tB_: e.tensor_tensor(
                        tB_[:, :], tA_[:, :], pLATE[:, 256:512], ALU.add),
                      reads=[tat, "p.LATE"], writes=[tbt])
                    A("vector", lambda e, dt=dt, tA_=tA_, tB_=tB_: e.tensor_scalar(
                        tA_[:, :], tB_[:, :], gmTs[:, dt:dt + 1], btTs[:, dt:dt + 1],
                        ALU.mult, ALU.add),
                      reads=[tbt, "r.gmT", "r.btT"], writes=[tat])
                    A("gpsimd", lambda e, dt=dt, tA_=tA_: e.tensor_tensor(
                        AsS[:, GRP * dt:GRP * dt + GRP], tA_[:, :],
                        GtS[:, GRP * dt:GRP * dt + GRP], ALU.mult),
                      reads=[tat, "b.Gt.%d" % dt], writes=["b.As.%d" % dt])
                if dbg and g == 0:
                    A("sync", lambda e: e.dma_start(out=d_As[:, :], in_=AsS[:, :]),
                      reads=["b.As.%d" % d_ for d_ in range(NDM)], writes=["d.out"], dma=True)
                for it in range(2):
                    for ns in range(2):
                        for dt in range(NDM):
                            A("tensor", lambda e, it=it, dt=dt, ns=ns: e.matmul(
                                pS[:, :],
                                AsS[:, GRP * dt + 128 * it:GRP * dt + 128 * it + 128],
                                WoS(1024 * dt + 512 * ns, 512),
                                start=(dt == 0), stop=False, skip_group_check=True),
                              reads=["b.As.%d" % dt, "w.o"], writes=["p.S"])
                        A("tensor", lambda e, ns=ns: e.matmul(
                            pS[:, :],
                            orbS[0:1, :], boB[0:1, 512 * ns:512 * ns + 512],
                            start=False, stop=True, skip_group_check=True),
                          reads=["c.orb", "r.boB"], writes=["p.S"])
                        A("scalar", lambda e, ns=ns: e.copy(
                            sstgS[:, 512 * ns:512 * ns + 512], pS[:, :]),
                          reads=["p.S"], writes=["b.sstg"])
                    A("gpsimd", lambda e, gc=gc, it=it: e.dma_start(
                        out=s_scr[gc + 128 * it:gc + 128 * it + 128, :],
                        in_=sstgS[:, :]),
                      reads=["b.sstg"], writes=["s_scr.%d.%d" % (g, it)], dma=True)

            if skip_p0:
                A("vector", lambda e: e.memset(state[:, :], 0.0),
                  writes=["b.state"])
            if not skip_p0:
                for o in range(3):
                    for jp in range(4):
                        pr = 4 * o + jp
                        jt0 = 2 * pr
                        buf = xg01 if pr % 2 == 0 else xg23
                        btok = ["xg0", "xg1"] if pr % 2 == 0 else ["xg2", "xg3"]
                        c0 = jt0 % 4
                        A("sync", lambda e, jt0=jt0, buf=buf: e.dma_start(
                            out=buf[:, :],
                            in_=xpre[:, 1024 * jt0:1024 * jt0 + 2048]),
                          writes=btok, dma=True)
                        for t2 in range(2):
                            ln_bn(buf[:, 1024 * t2:1024 * t2 + 1024],
                                  btok[t2], t2)
                        ln_bn_vpe(c0)
                        ln_nr_core(c0, 2)
                        for t2 in range(2):
                            jt = jt0 + t2
                            ln_bn_fin(buf[:, 1024 * t2:1024 * t2 + 1024],
                                      btok[t2], t2, c0 + t2,
                                      XH(1024 * jt, 1024), "xnh.%d" % jt,
                                      scale_col=(cpreS[:, jt:jt + 1], "r.cpre"))
                            for ns in range(2):
                                A("tensor", lambda e, jt=jt, ns=ns: e.matmul(
                                    pMR[0:1, 512 * ns:512 * ns + 512],
                                    cpreB[:, jt:jt + 1],
                                    XH(1024 * jt + 512 * ns, 512),
                                    start=(jt == 0), stop=(jt == NPT - 1),
                                    skip_group_check=True),
                                  reads=["r.cpreB", "xnh.%d" % jt],
                                  writes=["p.MR"])
                    if o == 0:
                        A("gpsimd", lambda e: e.dma_start(
                            out=WkS(0, 8192), in_=Wk.ap()),
                          writes=["w.k"], dma=True)
                        A("gpsimd", lambda e: e.dma_start(
                            out=WvS(0, 8192), in_=Wv.ap()),
                          writes=["w.v"], dma=True)
                    for at in range(NDM):
                        pM = pM0 if at % 2 == 0 else pM1
                        ptok = "p.M%d" % (at % 2)
                        if o > 0:
                            for ns in range(2):
                                A("tensor", lambda e, at=at, ns=ns, pM=pM: e.matmul(
                                    pM[:, 512 * ns:512 * ns + 512],
                                    identS[:, :],
                                    MS(1024 * at + 512 * ns, 512),
                                    start=True, stop=False,
                                    skip_group_check=True),
                                  reads=["b.M.%d" % at, "c.ident"],
                                  writes=[ptok])
                        for j in range(8):
                            jt = 8 * o + j
                            for ns in range(2):
                                A("tensor", lambda e, jt=jt, at=at, ns=ns, pM=pM, o=o: e.matmul(
                                    pM[:, 512 * ns:512 * ns + 512],
                                    XH(1024 * jt + 128 * at, 128),
                                    XH(1024 * jt + 512 * ns, 512),
                                    start=(jt % 8 == 0 and o == 0),
                                    stop=(jt % 8 == 7),
                                    skip_group_check=True),
                                  reads=["xnh.%d" % jt], writes=[ptok])
                        A("scalar", lambda e, at=at, pM=pM: e.copy(
                            MS(1024 * at, 1024), pM[:, :]),
                          reads=[ptok], writes=["b.M.%d" % at])
                A("vector", lambda e: e.tensor_copy(mrowB, pMR[0:1, :]),
                  reads=["p.MR"], writes=["r.mrow", "xnh.9"])
                A("vector", lambda e: e.tensor_copy(mstage[0:1, 0:1024], pMR[0:1, :]),
                  reads=["p.MR"], writes=["r.mst", "xnh.8"])
                for t in range(NDM):
                    A("tensor", lambda e, t=t: e.transpose(
                        pTR[:, 0:128], mstage[:, 128 * t:128 * t + 128],
                        identS[:, :]),
                      reads=["r.mst", "xnh.8", "c.ident"], writes=["p.TR"])
                    A("scalar", lambda e, t=t: e.copy(
                        mcolB[:, t:t + 1], pTR[:, 0:1]),
                      reads=["p.TR"], writes=["r.mcol"])
                for at in range(NDM):
                    pM = pM0 if at % 2 == 0 else pM1
                    ptok = "p.M%d" % (at % 2)
                    for bt in range(NDM):
                        for ns in range(2):
                            A("tensor", lambda e, at=at, bt=bt, ns=ns, pM=pM: e.matmul(
                                pM[:, 512 * ns:512 * ns + 512],
                                MS(1024 * bt + 128 * at, 128),
                                WvS(1024 * bt + 512 * ns, 512),
                                start=(bt == 0), stop=False,
                                skip_group_check=True),
                              reads=["b.M.%d" % bt, "w.v"], writes=[ptok])
                    for ns in range(2):
                        A("tensor", lambda e, at=at, ns=ns, pM=pM: e.matmul(
                            pM[:, 512 * ns:512 * ns + 512],
                            mrowB[0:1, 128 * at:128 * at + 128],
                            bvB[0:1, 512 * ns:512 * ns + 512],
                            start=False, stop=True, skip_group_check=True),
                          reads=["r.mrow", "r.bvB"], writes=[ptok])
                    A("scalar", lambda e, at=at, pM=pM: e.copy(
                        TS(1024 * at, 1024), pM[:, :]),
                      reads=[ptok], writes=["b.T"])
            A("gpsimd", lambda e: e.dma_start(out=WqS(0, 8192), in_=Wq.ap()),
              writes=["w.q"] + ["b.M.%d" % at_ for at_ in range(NDM)],
              dma=True)
            stageE(0, trp=((pTR, "p.TR"), (pTR, "p.TR")))
            if not skip_p0:
                A("vector", lambda e: e.tensor_scalar(
                    bvsB, bvB[0:1, :], csumS[0:1, 0:1], None, ALU.mult),
                  reads=["r.bvB", "r.csum"], writes=["r.bvs", "xnh.7"])
                for at in range(NDM):
                    for ns in range(2):
                        A("tensor", lambda e, at=at, ns=ns: e.matmul(
                            pMR[0:1, 512 * ns:512 * ns + 512],
                            mcolB[:, at:at + 1],
                            WvS(1024 * at + 512 * ns, 512),
                            start=(at == 0), stop=False, skip_group_check=True),
                          reads=["r.mcol", "w.v"], writes=["p.MR"])
                for ns in range(2):
                    A("tensor", lambda e, ns=ns: e.matmul(
                        pMR[0:1, 512 * ns:512 * ns + 512],
                        orbS[0:1, 0:1], bvsB[0:1, 512 * ns:512 * ns + 512],
                        start=False, stop=True, skip_group_check=True),
                      reads=["c.orb", "r.bvs"], writes=["p.MR"])
                A("vector", lambda e: e.tensor_copy(zB, pMR[0:1, :]),
                  reads=["p.MR"], writes=["r.zB", "xnh.6"])
            stageE(1, trp=((pTR, "p.TR"), (pTR, "p.TR")))
            if not skip_p0:
                for hp in range(NP):
                    for hh in range(2):
                        h = 2 * hp + hh
                        ro = 64 * hh
                        for at in range(NDM):
                            A("tensor", lambda e, h=h, ro=ro, at=at: e.matmul(
                                pSP[ro:ro + 64, 0:64] if ro == 0 else pSP[64:128, 0:64],
                                WkS(1024 * at + 64 * h, 64),
                                TS(1024 * at + 64 * h, 64),
                                start=(at == 0), stop=False,
                                skip_group_check=True, tile_position=(0, ro)),
                              reads=["w.k", "b.T"], writes=["p.SP"])
                        A("tensor", lambda e, h=h, ro=ro: e.matmul(
                            pSP[ro:ro + 64, 0:64] if ro == 0 else pSP[64:128, 0:64],
                            bkB[0:1, 64 * h:64 * h + 64],
                            zB[0:1, 64 * h:64 * h + 64],
                            start=False, stop=(ro == 64), skip_group_check=True,
                            tile_position=(0, ro)),
                          reads=["r.bkB", "r.zB"], writes=["p.SP"])
                    A("vector", lambda e, hp=hp: e.tensor_copy(
                        state[:, 64 * hp:64 * hp + 64], pSP[:, 0:64]),
                      reads=["p.SP"], writes=["b.state"])
            if skip_p0:
                A("gpsimd", lambda e: e.dma_start(out=WkS(0, 8192), in_=Wk.ap()),
                  writes=["w.k"], dma=True)
                A("gpsimd", lambda e: e.dma_start(out=WvS(0, 8192), in_=Wv.ap()),
                  writes=["w.v"], dma=True)
            A("gpsimd", lambda e: e.dma_start(out=WgS(0, 8192), in_=Wg.ap()),
              writes=["w.g", "b.T"], dma=True)
            A("gpsimd", lambda e: e.dma_start(out=WoS(0, 8192), in_=Wo.ap()),
              writes=["w.o"], dma=True)
            A("gpsimd", lambda e: e.dma_start(out=w1b[:, :], in_=w1.ap()),
              writes=["w.1hbm"], dma=True)
            A("gpsimd", lambda e: e.dma_start(out=w2b[:, :], in_=w2.ap()),
              writes=["w.2hbm"], dma=True)
            if dbg:
                A("sync", lambda e: e.dma_start(out=d_S0[:, :], in_=state[:, :]),
                  reads=["b.state"], writes=["d.out"], dma=True)
        _p0.close()

        p.psum_scope(["p.M0", "p.M1", "p.MR", "p.TR", "p.SP"])
        _pa = ExitStack()
        pPJ = _pa.enter_context(nc.psum_tensor("pPJ", [128, 1024], F32))
        pSCA = _pa.enter_context(nc.psum_tensor("pSCA", [128, 512], F32))
        pSCB = _pa.enter_context(nc.psum_tensor("pSCB", [128, 512], F32))
        pRT = _pa.enter_context(nc.psum_tensor("pRT", [128, 512], F32))
        pSU = _pa.enter_context(nc.psum_tensor("pSU", [128, 512], F32))
        pLATE = _pa.enter_context(nc.psum_tensor("pLATE", [128, 512], F32))
        pS = _pa.enter_context(nc.psum_tensor("pS", [128, 512], F32))
        pTR2 = pSCA.bitcast(BF16)
        pSCBb = pSCB.bitcast(BF16)
        if True:
            stageM1(0)
            for g in range(NG):
                stageM2(g)
                if g + 2 < NG:
                    stageE_pre(g + 2)
                stageS(g)
                if g + 1 < NG:
                    stageM1(g + 1)
                if g == 6:
                    A("sync", lambda e: e.dma_start(
                        out=w1S(0, 24576), in_=w1b.ap()[:, 0:24576]),
                      reads=["w.1hbm"],
                      writes=["w.1a", "w.k", "w.v", "w.q"]
                             + ["b.M.%d" % at_ for at_ in range(NDM)],
                      dma=True)
                if g == 7:
                    A("sync", lambda e: e.dma_start(
                        out=w1S(24576, 8192), in_=w1b.ap()[:, 24576:32768]),
                      reads=["w.1hbm"], writes=["w.1b", "w.g", "b.T"],
                      dma=True)
                    A("scalar", lambda e: e.dma_start(
                        out=WB2[:, 8192:16384], in_=w2b.ap()[:, 0:8192]),
                      reads=["w.2hbm"],
                      writes=["w.2a", "b.Qs", "b.Ks", "b.V", "b.Ku",
                              "r.zB", "r.bvs"]
                             + ["xnh.%d" % j_ for j_ in range(8)],
                      dma=True)
                if g + 2 < NG:
                    stageE_fin(g + 2)
                stageL(g)
        _pa.close()
        if dbg:
            A("sync", lambda e: e.dma_start(out=d_xnT[:, :], in_=xnT[:, :]),
              reads=["xnT"], writes=["d.out"], dma=True)
            A("sync", lambda e: e.dma_start(out=d_s[:, :], in_=s_scr[:, :]),
              reads=["s_scr.%d.%d" % (g_, i_) for g_ in range(NG) for i_ in range(2)],
              writes=["d.out"], dma=True)
        p.psum_scope(["p.PJ0", "p.PJ1", "p.SCA", "p.SCB", "p.RT", "p.SU",
                      "p.LATE", "p.S"])
        w2tl = (["w.2b", "w.o", "r.mst", "r.mrow", "b.sstg", "b.sqsb",
                 "b.vR", "b.yR", "b.nmrR", "b.tA0", "b.tA1", "b.tB0",
                 "b.tB1", "b.Pm00", "b.Pm01", "b.Pm10", "b.Pm11"]
                + ["b.As.%d" % i_ for i_ in range(NDM)]
                + ["b.Gt.%d" % i_ for i_ in range(NDM)]
                + ["b.rsb.%d" % i_ for i_ in range(NDM)]
                + ["xnh.%d" % j_ for j_ in range(8, NPT)])
        A("scalar", lambda e: e.dma_start(
            out=WB2[:, 16384:32768], in_=w2b.ap()[:, 8192:24576]),
          reads=["w.2hbm"], writes=list(w2tl), dma=True)
        A("scalar", lambda e: e.dma_start(
            out=WB2[:, 0:8192], in_=w2b.ap()[:, 24576:32768]),
          reads=["w.2hbm"], writes=["w.2b", "w.o"], dma=True)

        _p3 = ExitStack()
        hT = _p3.enter_context(nc.sbuf_tensor("hT", [128, 4096], BF16))
        sld0 = _p3.enter_context(nc.sbuf_tensor("sld", [128, 1024], F32))
        sld1 = sld0
        scrA = xg23[:, 1024:2048]
        scrB = xg23[:, 0:1024]
        pH0 = _p3.enter_context(nc.psum_tensor("pH0", [128, 1024], F32))
        pH1 = _p3.enter_context(nc.psum_tensor("pH1", [128, 1024], F32))
        pY0 = _p3.enter_context(nc.psum_tensor("pY0", [128, 1024], F32))
        pY1 = _p3.enter_context(nc.psum_tensor("pY1", [128, 1024], F32))
        if True:
            nout = 0
            for g in range(NG):
                gc = GRP * g
                for half in range(2):
                    for q in range(4):
                        pH = pH0 if q % 2 == 0 else pH1
                        htok = "p.H%d" % (q % 2)
                        for sub in range(4):
                            dht = 16 * half + 4 * q + sub
                            for dm in range(NDM):
                                A("tensor", lambda e, dht=dht, dm=dm, sub=sub, pH=pH, gc=gc: e.matmul(
                                    pH[:, 256 * sub:256 * sub + 256],
                                    w1S(4096 * dm + 128 * dht, 128),
                                    xnT[:, 2048 * dm + gc:2048 * dm + gc + GRP],
                                    start=(dm == 0),
                                    stop=(dm == NDM - 1), skip_group_check=True),
                                  reads=["w.1a" if dm < 6 else "w.1b", "xnT"],
                                  writes=[htok])
                        for sub in range(4):
                            dht = 16 * half + 4 * q + sub
                            lt = 4 * q + sub
                            A("scalar", lambda e, dht=dht, lt=lt, sub=sub, pH=pH: e.activation(
                                hT[:, 256 * lt:256 * lt + 256],
                                pH[:, 256 * sub:256 * sub + 256],
                                AF.Gelu, bias=b1Ts[:, dht:dht + 1]),
                              reads=[htok, "r.b1T"], writes=["b.hT.%d" % lt])
                    if dbg and g == 0 and half == 0:
                        A("sync", lambda e: e.dma_start(out=d_hT[:, :], in_=hT[:, :]),
                          reads=["b.hT.%d" % l_ for l_ in range(16)], writes=["d.out"], dma=True)
                    for it in range(2):
                        pY = pY0 if it % 2 == 0 else pY1
                        ytok = "p.Y%d" % (it % 2)
                        for lt in range(16):
                            dht = 16 * half + lt
                            for ns in range(2):
                                A("tensor", lambda e, lt=lt, dht=dht, ns=ns, it=it, pY=pY, half=half: e.matmul(
                                    pY[:, 512 * ns:512 * ns + 512],
                                    hT[:, 256 * lt + 128 * it:256 * lt + 128 * it + 128],
                                    w2S(1024 * dht + 512 * ns, 512),
                                    start=(lt == 0 and half == 0), stop=False,
                                    skip_group_check=True),
                                  reads=["b.hT.%d" % lt,
                                         "w.2a" if dht < 8 else "w.2b"],
                                  writes=[ytok])
                for it in range(2):
                    pY = pY0 if it % 2 == 0 else pY1
                    ytok = "p.Y%d" % (it % 2)
                    xb = xg0 if it % 2 == 0 else xg1
                    xtok = "xg%d" % (it % 2)
                    sb = sld0 if it % 2 == 0 else sld1
                    stok = "b.sld"
                    A("sync", lambda e, g=g, it=it, xb=xb: e.dma_start(
                        out=xb[:, :],
                        in_=xseg[:, 1024 * (2 * g + it):1024 * (2 * g + it) + 1024]),
                      writes=[xtok], dma=True)
                    A("sync", lambda e, gc=gc, it=it, sb=sb: e.dma_start(
                        out=sb[:, :], in_=s_scr[gc + 128 * it:gc + 128 * it + 128, :]),
                      reads=["s_scr.%d.%d" % (g, it)], writes=[stok], dma=True)
                    for ns in range(2):
                        A("tensor", lambda e, ns=ns, pY=pY: e.matmul(
                            pY[:, 512 * ns:512 * ns + 512],
                            orbS[0:1, :], b2B[0:1, 512 * ns:512 * ns + 512],
                            start=False, stop=True, skip_group_check=True),
                          reads=["c.orb", "r.b2B"], writes=[ytok])
                    A("scalar", lambda e, pY=pY: e.activation(
                        scrA[:, :], pY[:, :], AF.Tanh, scale=0.5),
                      reads=[ytok], writes=["b.scrA"])
                    A("vector", lambda e, pY=pY: e.scalar_tensor_tensor(
                        scrB[:, :], pY[:, :], 0.5, scrA[:, :], ALU.mult, ALU.mult),
                      reads=[ytok, "b.scrA"], writes=["b.scrB"])
                    A("vector", lambda e, pY=pY: e.scalar_tensor_tensor(
                        scrA[:, :], pY[:, :], 0.5, scrB[:, :], ALU.mult, ALU.add),
                      reads=[ytok, "b.scrB"], writes=["b.scrA"])
                    A("gpsimd", lambda e, sb=sb: e.tensor_tensor(
                        scrB[:, :], sb[:, :], scrA[:, :], ALU.mult),
                      reads=[stok, "b.scrA"], writes=["b.scrB"])
                    A("gpsimd", lambda e, sb=sb, xb=xb: e.tensor_tensor(
                        sb[:, :], xb[:, :], scrB[:, :], ALU.add),
                      reads=[xtok, "b.scrB"], writes=[stok])
                    A("gpsimd", lambda e, gc=gc, it=it, sb=sb: e.dma_start(
                        out=out[gc + 128 * it:gc + 128 * it + 128, :], in_=sb[:, :]),
                      reads=[stok], writes=["d.out.%d" % nout], dma=True)
                    nout += 1
            A("scalar", None, reads=["d.out.%d" % i for i in range(nout)])

        _p3.close()
        def SEM(nm):
            return _st.enter_context(nc.semaphore(nm))
        sems = {"c_tensor": SEM("sem_t"), "c_vector": SEM("sem_v"),
                "c_scalar": SEM("sem_s"), "c_gpsimd": SEM("sem_g"),
                "c_sync": SEM("sem_sy")}
        for qq in ("sync", "gpsimd", "scalar"):
            for sl in range(8):
                sems["dma_%s_%d" % (qq, sl)] = SEM("dq_%s_%d" % (qq, sl))
        for sl in range(8):
            sems["gps_%d" % sl] = SEM("gpc_%d" % sl)
        with nc.Block() as block:
            p.emit(nc, block, sems, serial=bool(globals().get("SERIAL", False)))
        _st.close()
    return nc



def _pm(a, nt, n):
    return np.ascontiguousarray(
        np.asarray(a, np.float32).reshape(nt, 128, n)
        .transpose(1, 0, 2).reshape(128, nt * n))


def shared_inputs(inputs):
    f32 = np.float32

    def row(v):
        return np.ascontiguousarray(np.asarray(v, f32).reshape(1, -1))

    def colT(v, nt):
        return np.ascontiguousarray(np.asarray(v, f32).reshape(nt, 128).T)

    return dict(
        Wq=_pm(inputs["L0_attn_Wq"], NDM, DM),
        Wk=_pm(inputs["L0_attn_Wk"], NDM, DM),
        Wv=_pm(inputs["L0_attn_Wv"], NDM, DM),
        Wg=_pm(inputs["L0_attn_Wg"], NDM, DM),
        Wo=_pm(inputs["L0_attn_Wo"], NDM, DM),
        w1=_pm(inputs["L0_ffn_w1"], NDM, DFF),
        w2=_pm(inputs["L0_ffn_w2"], NDH, DM),
        b_v=row(inputs["L0_attn_bv"]), b_k=row(inputs["L0_attn_bk"]),
        b_o=row(inputs["L0_attn_bo"]), b_2=row(inputs["L0_ffn_b2"]),
        bqT=colT(inputs["L0_attn_bq"], NDM), bkT=colT(inputs["L0_attn_bk"], NDM),
        bgT=colT(inputs["L0_attn_bg"], NDM), b1T=colT(inputs["L0_ffn_b1"], NDH),
        gmT=colT(inputs["L0_attn_gamma"], NDM), btT=colT(inputs["L0_attn_beta"], NDM),
    )


def core_inputs(inputs, core, shared=None):
    f32 = np.float32
    x = np.asarray(inputs["L0_attn_x"])
    b, seg = core // 4, core % 4
    t0 = SEG * seg
    xseg = np.ascontiguousarray(x[b, t0:t0 + SEG], f32)
    xpre = np.zeros((PRE, DM), f32)
    lo = max(0, t0 - PRE)
    if t0 > 0:
        xpre[PRE - (t0 - lo):] = x[b, lo:t0]
    w = np.arange(PRE, dtype=np.float64)
    c = D ** (PRE - w)
    c[w < PRE - t0] = 0.0
    csqrt = np.sqrt(c)
    cpre_l = np.ascontiguousarray(csqrt.reshape(NPT, 128).T.astype(f32))
    csum_l = np.array([[c.sum()]], dtype=f32)
    m = dict(shared if shared is not None else shared_inputs(inputs))
    m.update(xseg=_pm(xseg, SEG // 128, DM), xpre=_pm(xpre, NPT, DM),
             cpre=cpre_l, csum=csum_l)
    return m


def kernel(**inputs):
    shared = shared_inputs(inputs)
    in_maps = [core_inputs(inputs, core, shared) for core in range(8)]

    nc = build_nc()
    res = run_bass_kernel_spmd(nc, in_maps, core_ids=list(range(8)))
    shards = [res.results[i]["out"] for i in range(8)]
    full = np.concatenate(shards, axis=0).reshape(2, 8192, 1024)
    return np.ascontiguousarray(full.astype(np.float32))


if __name__ == "__main__":
    rng = np.random.default_rng(0)
    ins = {
        "L0_attn_x": rng.standard_normal((2, 8192, 1024)).astype(np.float32),
    }
    for nm in ["Wq", "Wk", "Wv", "Wg", "Wo"]:
        ins["L0_attn_" + nm] = (rng.standard_normal((1024, 1024)) * 0.02).astype(np.float32)
    for nm in ["bq", "bk", "bv", "bg", "bo", "gamma", "beta"]:
        ins["L0_attn_" + nm] = rng.standard_normal(1024).astype(np.float32) * 0.02
    ins["L0_attn_gamma"] += 1.0
    ins["L0_ffn_w1"] = (rng.standard_normal((1024, 4096)) / 32).astype(np.float32)
    ins["L0_ffn_b1"] = (rng.standard_normal(4096) * 0.02).astype(np.float32)
    ins["L0_ffn_w2"] = (rng.standard_normal((4096, 1024)) / 64).astype(np.float32)
    ins["L0_ffn_b2"] = (rng.standard_normal(1024) * 0.02).astype(np.float32)
    o = kernel(**ins)
    print("kernel ran", o.shape, o.dtype)


# version 23
# speedup vs baseline: 1.2319x; 1.0218x over previous
import numpy as np
import ml_dtypes
import concourse.bass as bass
import concourse.mybir as mybir
from concourse.bass_utils import run_bass_kernel_spmd

F32 = mybir.dt.float32
BF16 = mybir.dt.bfloat16
I32 = mybir.dt.int32
AF = mybir.ActivationFunctionType
ALU = mybir.AluOpType
AX = mybir.AxisListType

D = 0.99
SEG = 2048
PRE = 3072
GRP = 256
NG = SEG // GRP
NPT = PRE // 128
DM = 1024
NDM = DM // 128
NH, DH = 16, 64
NP = NH // 2
DFF = 4096
NDH = DFF // 128
EPS = 1e-5
MAGIC = 0x5f3759df
D256 = float(D ** 256)


class Prog:
    def __init__(self):
        self.ops = []
        self._prev_psums = ()
        self._opened = set()

    def psum_scope(self, prev_tokens):
        self._prev_psums = tuple(prev_tokens)
        self._opened = set()

    def add(self, eng, fn, reads=(), writes=(), dma=False):
        writes = list(writes)
        for t in list(writes):
            if t.startswith("p.") and t not in self._opened:
                self._opened.add(t)
                writes += self._prev_psums
        self.ops.append(dict(eng=eng, fn=fn, reads=tuple(reads),
                             writes=tuple(writes), dma=dma, idx=len(self.ops)))

    def emit(self, nc, block, sems, serial=False):
        ops = self.ops
        NSLOT = 8
        dcount = {}
        gcount = 0
        for op in ops:
            if op["dma"]:
                k = dcount.get(op["eng"], 0)
                dcount[op["eng"]] = k + 1
                op["slot"] = k % NSLOT
                op["semval"] = 16 * (k // NSLOT + 1)
                op["inc"] = 16
            elif op["eng"] == "gpsimd" and op["fn"] is not None:
                k = gcount
                gcount += 1
                op["slot"] = k % NSLOT
                op["semval"] = k // NSLOT + 1
                op["inc"] = 1
        last_w = {}
        readers = {}
        for op in ops:
            deps = set()
            if serial and op["idx"] > 0:
                deps.add(op["idx"] - 1)
            for t in op["reads"]:
                if t in last_w:
                    deps.add(last_w[t])
            for t in op["writes"]:
                if t in last_w:
                    deps.add(last_w[t])
                for r in readers.get(t, ()):
                    deps.add(r)
            deps.discard(op["idx"])
            fd = set()
            for d in deps:
                dop = ops[d]
                if (not dop["dma"] and not op["dma"]
                        and dop["eng"] == "tensor" and op["eng"] == "tensor"):
                    continue
                fd.add(d)
            bysrc = {}
            for d in fd:
                dop = ops[d]
                if dop["dma"] or dop["eng"] == "gpsimd":
                    key = (dop["eng"], dop["dma"], dop.get("slot"))
                else:
                    key = (dop["eng"],)
                if key not in bysrc or d > bysrc[key]:
                    bysrc[key] = d
            op["deps"] = sorted(bysrc.values())
            for t in op["reads"]:
                readers.setdefault(t, []).append(op["idx"])
            for t in op["writes"]:
                last_w[t] = op["idx"]
                readers[t] = []
        need_inc = set()
        for op in ops:
            for d in op["deps"]:
                if not ops[d]["dma"]:
                    need_inc.add(d)
        ccount = {}
        for op in ops:
            if op["dma"] or op["eng"] == "gpsimd":
                pass
            elif op["idx"] in need_inc:
                ccount[op["eng"]] = ccount.get(op["eng"], 0) + 1
                op["semval"] = ccount[op["eng"]]
                op["inc"] = 1
            else:
                op["semval"] = None
                op["inc"] = 0
        hw = {}

        def semkey(dop):
            if dop["dma"]:
                return "dma_%s_%d" % (dop["eng"], dop["slot"])
            if dop["eng"] == "gpsimd":
                return "gps_%d" % dop["slot"]
            return "c_" + dop["eng"]

        def make_section(name):
            def section(e):
                for op in ops:
                    if op["eng"] != name:
                        continue
                    for d in op["deps"]:
                        dop = ops[d]
                        sk = semkey(dop)
                        val = dop["semval"]
                        assert val is not None
                        if hw.get((name, sk), -1) >= val:
                            continue
                        e.wait_ge(sems[sk], val)
                        hw[(name, sk)] = val
                    if op["fn"] is not None:
                        if op["dma"] or op["eng"] == "gpsimd":
                            sk = semkey(op)
                            prev = op["semval"] - (16 if op["dma"] else 1)
                            if prev > 0 and hw.get((name, sk), -1) < prev:
                                e.wait_ge(sems[sk], prev)
                                hw[(name, sk)] = prev
                        ins = op["fn"](e)
                        if op["inc"]:
                            ins.then_inc(sems[semkey(op)], op["inc"])
                    else:
                        assert op["inc"] == 0
            return section

        for name in ["tensor", "vector", "scalar", "gpsimd", "sync"]:
            getattr(block, name)(make_section(name))



def _host_consts():
    c = {}
    c["ident"] = np.eye(128, dtype=ml_dtypes.bfloat16)
    c["ocf"] = np.ones((128, 1), dtype=np.float32)
    c["ocb"] = np.ones((128, 1), dtype=ml_dtypes.bfloat16)
    c["orb"] = np.ones((1, 128), dtype=ml_dtypes.bfloat16)
    c["orf"] = np.ones((1, 128), dtype=np.float32)
    mask = np.ones((128, 256), dtype=np.float32)
    tri = (np.arange(128)[None, :] >= np.arange(128)[:, None]).astype(np.float32)
    mask[:, 0:128] = tri
    c["mask"] = mask
    i = np.arange(GRP, dtype=np.float64)
    c["qdec"] = np.tile((D ** i / 8.0).astype(np.float32), (128, 1))
    c["kdec"] = np.tile((D ** (-i)).astype(np.float32), (128, 1))
    p = np.arange(128, dtype=np.float64)
    kus = np.stack([D ** (256 - (128 * s + p)) for s in range(2)], axis=1)
    c["kus"] = kus.astype(np.float32)
    return c



def build_nc(dbg=False, skip_p0=False):
    nc = bass.Bass("TRN2", target_bir_lowering=False)

    xseg = nc.dram_tensor("xseg", [128, (SEG // 128) * DM], F32, kind="ExternalInput")
    xpre = nc.dram_tensor("xpre", [128, NPT * DM], F32, kind="ExternalInput")
    cpre = nc.dram_tensor("cpre", [128, NPT], F32, kind="ExternalInput")
    csum = nc.dram_tensor("csum", [1, 1], F32, kind="ExternalInput")
    Wq = nc.dram_tensor("Wq", [128, NDM * DM], F32, kind="ExternalInput")
    Wk = nc.dram_tensor("Wk", [128, NDM * DM], F32, kind="ExternalInput")
    Wv = nc.dram_tensor("Wv", [128, NDM * DM], F32, kind="ExternalInput")
    Wg = nc.dram_tensor("Wg", [128, NDM * DM], F32, kind="ExternalInput")
    Wo = nc.dram_tensor("Wo", [128, NDM * DM], F32, kind="ExternalInput")
    w1 = nc.dram_tensor("w1", [128, NDM * DFF], F32, kind="ExternalInput")
    w2 = nc.dram_tensor("w2", [128, NDH * DM], F32, kind="ExternalInput")
    b_v = nc.dram_tensor("b_v", [1, DM], F32, kind="ExternalInput")
    b_k = nc.dram_tensor("b_k", [1, DM], F32, kind="ExternalInput")
    b_o = nc.dram_tensor("b_o", [1, DM], F32, kind="ExternalInput")
    b_2 = nc.dram_tensor("b_2", [1, DM], F32, kind="ExternalInput")
    bqT = nc.dram_tensor("bqT", [128, NDM], F32, kind="ExternalInput")
    bkT = nc.dram_tensor("bkT", [128, NDM], F32, kind="ExternalInput")
    bgT = nc.dram_tensor("bgT", [128, NDM], F32, kind="ExternalInput")
    b1T = nc.dram_tensor("b1T", [128, NDH], F32, kind="ExternalInput")
    gmT = nc.dram_tensor("gmT", [128, NDM], F32, kind="ExternalInput")
    btT = nc.dram_tensor("btT", [128, NDM], F32, kind="ExternalInput")
    out = nc.dram_tensor("out", [SEG, DM], F32, kind="ExternalOutput")
    s_scr = nc.dram_tensor("s_scr", [SEG, DM], F32, kind="Internal")
    w1b = nc.dram_tensor("w1b", [128, NDM * DFF], BF16, kind="Internal")
    w2b = nc.dram_tensor("w2b", [128, NDH * DM], BF16, kind="Internal")
    if dbg:
        d_S0 = nc.dram_tensor("d_S0", [128, 512], F32, kind="ExternalOutput")
        d_S1 = nc.dram_tensor("d_S1", [128, 512], F32, kind="ExternalOutput")
        d_xnT = nc.dram_tensor("d_xnT", [128, 16384], BF16, kind="ExternalOutput")
        d_Qs = nc.dram_tensor("d_Qs", [128, 2048], BF16, kind="ExternalOutput")
        d_Ks = nc.dram_tensor("d_Ks", [128, 2048], BF16, kind="ExternalOutput")
        d_V = nc.dram_tensor("d_V", [128, 2048], BF16, kind="ExternalOutput")
        d_Ku = nc.dram_tensor("d_Ku", [128, 2048], BF16, kind="ExternalOutput")
        d_Gt = nc.dram_tensor("d_Gt", [128, 2048], F32, kind="ExternalOutput")
        d_rsb = nc.dram_tensor("d_rsb", [128, 2048], F32, kind="ExternalOutput")
        d_As = nc.dram_tensor("d_As", [128, 2048], BF16, kind="ExternalOutput")
        d_s = nc.dram_tensor("d_s", [SEG, DM], F32, kind="ExternalOutput")
        d_hT = nc.dram_tensor("d_hT", [128, 4096], BF16, kind="ExternalOutput")
        d_Pm = nc.dram_tensor("d_Pm", [128, 256], BF16, kind="ExternalOutput")
        d_Wq = nc.dram_tensor("d_Wq", [128, 8192], BF16, kind="ExternalOutput")
        d_PJ = nc.dram_tensor("d_PJ", [128, 256], F32, kind="ExternalOutput")
        d_xnb = nc.dram_tensor("d_xnb", [128, 1024], BF16, kind="ExternalOutput")
        d_Wk = nc.dram_tensor("d_Wk", [128, 8192], BF16, kind="ExternalOutput")

    hc = _host_consts()
    ident_d = nc.inline_tensor(hc["ident"], name="identc")
    ocf_d = nc.inline_tensor(hc["ocf"], name="ocfc")
    ocb_d = nc.inline_tensor(hc["ocb"], name="ocbc")
    orb_d = nc.inline_tensor(hc["orb"], name="orbc")
    orf_d = nc.inline_tensor(hc["orf"], name="orfc")
    mask_d = nc.inline_tensor(hc["mask"], name="maskc")
    qdec_d = nc.inline_tensor(hc["qdec"], name="qdecc")
    kdec_d = nc.inline_tensor(hc["kdec"], name="kdecc")
    kus_d = nc.inline_tensor(hc["kus"], name="kusc")

    p = Prog()
    A = p.add

    from contextlib import ExitStack
    _st = ExitStack()
    def SB(name, shape, dtype):
        return _st.enter_context(nc.sbuf_tensor(name, shape, dtype))
    if True:
        WB1 = SB("WB1", [128, 32768], BF16)
        WB2 = SB("WB2", [128, 32768], BF16)
        xnT = SB("xnT", [128, 16384], BF16)
        xg01 = SB("xg01", [128, 2048], F32)
        xg23 = SB("xg23", [128, 2048], F32)
        xg0 = xg01[:, 0:1024]
        xg1 = xg01[:, 1024:2048]
        state = SB("state", [128, 512], F32)
        sb16 = SB("sb16", [128, 512], BF16)
        xnb = SB("xnb", [128, 1024], BF16)
        tA2F = SB("tA2F", [128, 256], F32)
        tB2F = SB("tB2F", [128, 256], F32)
        identS = SB("identS", [128, 128], BF16)
        ocfS = SB("ocfS", [128, 1], F32)
        ocbS = SB("ocbS", [128, 1], BF16)
        orbS = SB("orbS", [1, 128], BF16)
        orfS = SB("orfS", [1, 128], F32)
        maskS = SB("maskS", [128, 256], F32)
        qdecS = SB("qdecS", [128, GRP], F32)
        kdecS = SB("kdecS", [128, GRP], F32)
        kusS = SB("kusS", [128, 2], F32)
        cpreS = SB("cpreS", [128, NPT], F32)
        cpreB = SB("cpreB", [128, NPT], BF16)
        csumS = SB("csumS", [1, 1], F32)
        bvB = SB("bvB", [1, DM], BF16)
        bkB = SB("bkB", [1, DM], BF16)
        boB = SB("boB", [1, DM], BF16)
        b2B = SB("b2B", [1, DM], BF16)
        bqTs = SB("bqTs", [128, NDM], F32)
        bkTs = SB("bkTs", [128, NDM], F32)
        bgTs = SB("bgTs", [128, NDM], F32)
        b1Ts = SB("b1Ts", [128, NDH], F32)
        gmTs = SB("gmTs", [128, NDM], F32)
        btTs = SB("btTs", [128, NDM], F32)
        mcolB = SB("mcolB", [128, NDM], BF16)
        vpe = SB("vpe", [128, 4], F32)
        nrY = SB("nrY", [128, 4], F32)
        nrT = SB("nrT", [128, 4], F32)
        bnst = SB("bnst", [128, 24], F32)
        mvc = SB("mvc", [128, 4], F32)
    if True:
        def wb1(c0, n):
            return WB1[:, c0:c0 + n]
        def wb2(c0, n):
            return WB2[:, c0:c0 + n]
        WkS = lambda c0, n: WB1[:, 0 + c0: 0 + c0 + n]
        WvS = lambda c0, n: WB1[:, 8192 + c0: 8192 + c0 + n]
        MS = lambda c0, n: WB1[:, 16384 + c0: 16384 + c0 + n]
        TS = lambda c0, n: WB1[:, 24576 + c0: 24576 + c0 + n]
        WqS = lambda c0, n: WB1[:, 16384 + c0: 16384 + c0 + n]
        WgS = lambda c0, n: WB1[:, 24576 + c0: 24576 + c0 + n]
        WoS = lambda c0, n: WB2[:, 0 + c0: 0 + c0 + n]
        XH = lambda c0, n: WB2[:, 8192 + c0: 8192 + c0 + n]
        zB = WB2[0:1, 8192 + 1024 * 6:8192 + 1024 * 6 + 1024]
        bvsB = WB2[0:1, 8192 + 1024 * 7:8192 + 1024 * 7 + 1024]
        mstage = WB2[:, 8192 + 1024 * 8:8192 + 1024 * 8 + 1024]
        mrowB = WB2[0:1, 8192 + 1024 * 9:8192 + 1024 * 9 + 1024]
        QsS = WB2[:, 8192:10240]
        KsS = WB2[:, 10240:12288]
        VS = WB2[:, 12288:14336]
        KuS = WB2[:, 14336:16384]
        AsS = WB2[:, 16384:18432]
        Pm = [WB2[:, 18432:18688], WB2[:, 18688:18944],
              WB2[:, 18944:19072], WB2[:, 19072:19200]]
        SPARE = WB2[:, 19200:32768].bitcast(F32)
        GtS = SPARE[:, 0:2048]
        rsbS = SPARE[:, 2048:4096]
        sstgS = SPARE[:, 4096:5120]
        sqsbS = SPARE[:, 5120:5376]
        tAS = SPARE[:, 5376:5632]
        tBS = SPARE[:, 5632:5888]
        tA2S = tA2F[:, :]
        tB2S = tB2F[:, :]
        vR = SPARE[0:1, 5888:6144]
        yR = SPARE[0:1, 6144:6400]
        nmrR = SPARE[0:1, 6400:6656]
        nmuR = tAS[0:1, 0:256]
        tR = sqsbS[0:1, 0:256]
        WB2_A_TOKENS = ("b.Qs", "b.Ks", "b.V", "b.Ku", "b.As", "b.Pm0", "b.Pm1",
                        "b.Gt", "b.rsb", "b.sstg", "b.sqsb", "b.tA", "b.tB")

        w1S = lambda c0, n: WB1[:, c0:c0 + n]
        w2S = lambda c0, n: WB2[:, c0:c0 + n]

        for nm, dst, src in [
            ("c.ident", identS, ident_d), ("c.ocf", ocfS, ocf_d),
            ("c.ocb", ocbS, ocb_d),
            ("c.orb", orbS, orb_d),
            ("c.orf", orfS, orf_d), ("c.mask", maskS, mask_d),
            ("c.qdec", qdecS, qdec_d), ("c.kdec", kdecS, kdec_d),
            ("c.kus", kusS, kus_d), ("r.cpre", cpreS, cpre),
            ("r.csum", csumS, csum),
            ("r.bqT", bqTs, bqT), ("r.bkT", bkTs, bkT),
            ("r.bgT", bgTs, bgT), ("r.b1T", b1Ts, b1T),
            ("r.gmT", gmTs, gmT), ("r.btT", btTs, btT),
        ]:
            A("sync", lambda e, d=dst, s=src: e.dma_start(out=d[:, :], in_=s[:, :]),
              writes=[nm], dma=True)
        for nm, dst, src in [
            ("r.cpreB", cpreB, cpre), ("r.bvB", bvB, b_v), ("r.bkB", bkB, b_k),
            ("r.boB", boB, b_o), ("r.b2B", b2B, b_2),
        ]:
            A("gpsimd", lambda e, d=dst, s=src: e.dma_start(out=d[:, :], in_=s[:, :]),
              writes=[nm], dma=True)

        def ln_stats(xbuf, xtok, col):
            cs = (col, col + 1)
            sx = col // 2
            A("scalar", lambda e: e.activation(
                xnb[:, :], xbuf[:, :], AF.Copy,
                accum_out=nsum[:, cs[0]:cs[1]]),
              reads=[xtok], writes=["b.xnb", "b.nsum.%d" % sx])
            A("vector", lambda e: e.tensor_scalar(
                nmu[:, cs[0]:cs[1]], nsum[:, cs[0]:cs[1]], -1.0 / DM, None,
                ALU.mult),
              reads=["b.nsum.%d" % sx], writes=["b.nmu.%d" % sx])
            A("scalar", lambda e: e.activation(
                xnb[:, :], xbuf[:, :], AF.Square, bias=nmu[:, cs[0]:cs[1]],
                accum_out=v1c[:, cs[0]:cs[1]]),
              reads=[xtok, "b.nmu.%d" % sx], writes=["b.xnb", "b.v1c.%d" % sx])

        def ln_nr(c0, ncols):
            cs = (c0, c0 + ncols)
            sx = c0 // 2
            A("vector", lambda e: e.tensor_scalar(
                vpe[:, cs[0]:cs[1]], v1c[:, cs[0]:cs[1]], 1.0 / DM, EPS,
                ALU.mult, ALU.add),
              reads=["b.v1c.%d" % sx], writes=["b.vpe.%d" % sx])
            ln_nr_core(c0, ncols)

        def ln_nr_core(c0, ncols):
            cs = (c0, c0 + ncols)
            sx = c0 // 2
            A("vector", lambda e: e.tensor_scalar(
                nrY.bitcast(I32)[:, cs[0]:cs[1]], vpe.bitcast(I32)[:, cs[0]:cs[1]],
                1, None, ALU.logical_shift_right),
              reads=["b.vpe.%d" % sx], writes=["b.nrY.%d" % sx])
            A("vector", lambda e: e.tensor_scalar(
                nrY.bitcast(I32)[:, cs[0]:cs[1]], nrY.bitcast(I32)[:, cs[0]:cs[1]],
                -1, None, ALU.bitwise_xor),
              reads=["b.nrY.%d" % sx], writes=["b.nrY.%d" % sx])
            A("vector", lambda e: e.tensor_scalar(
                nrY.bitcast(I32)[:, cs[0]:cs[1]], nrY.bitcast(I32)[:, cs[0]:cs[1]],
                MAGIC + 1, None, ALU.add),
              reads=["b.nrY.%d" % sx], writes=["b.nrY.%d" % sx])
            for _ in range(2):
                A("vector", lambda e: e.tensor_tensor(
                    nrT[:, cs[0]:cs[1]], nrY[:, cs[0]:cs[1]], nrY[:, cs[0]:cs[1]],
                    ALU.mult), reads=["b.nrY.%d" % sx], writes=["b.nrT.%d" % sx])
                A("vector", lambda e: e.tensor_tensor(
                    nrT[:, cs[0]:cs[1]], nrT[:, cs[0]:cs[1]], vpe[:, cs[0]:cs[1]],
                    ALU.mult),
                  reads=["b.nrT.%d" % sx, "b.vpe.%d" % sx],
                  writes=["b.nrT.%d" % sx])
                A("vector", lambda e: e.tensor_scalar(
                    nrT[:, cs[0]:cs[1]], nrT[:, cs[0]:cs[1]], -0.5, 1.5,
                    ALU.mult, ALU.add),
                  reads=["b.nrT.%d" % sx], writes=["b.nrT.%d" % sx])
                A("vector", lambda e: e.tensor_tensor(
                    nrY[:, cs[0]:cs[1]], nrY[:, cs[0]:cs[1]], nrT[:, cs[0]:cs[1]],
                    ALU.mult),
                  reads=["b.nrY.%d" % sx, "b.nrT.%d" % sx],
                  writes=["b.nrY.%d" % sx])

        def ln_finish(xbuf, xtok, col, out_ap, out_tok, scale_col=None, extra=()):
            cs = (col, col + 1)
            sx = col // 2
            rstd_tok = "b.nrY.%d" % sx
            rstd_ap = nrY
            if scale_col is not None:
                ap, rtok = scale_col
                A("vector", lambda e: e.tensor_tensor(
                    nrT[:, cs[0]:cs[1]], nrY[:, cs[0]:cs[1]], ap, ALU.mult),
                  reads=["b.nrY.%d" % sx, rtok], writes=["b.nrT.%d" % sx])
                rstd_tok = "b.nrT.%d" % sx
                rstd_ap = nrT
            A("vector", lambda e: e.tensor_scalar(
                out_ap, xbuf[:, :], nmu[:, cs[0]:cs[1]], rstd_ap[:, cs[0]:cs[1]],
                ALU.add, ALU.mult),
              reads=[xtok, "b.nmu.%d" % sx, rstd_tok] + list(extra),
              writes=[out_tok])

        def ln_bn(xbuf, xtok, t2):
            A("vector", lambda e: e.bn_stats(
                bnst[:, 12 * t2:12 * t2 + 6], xbuf[:, 0:512]),
              reads=[xtok], writes=["b.bn.%d" % t2])
            A("vector", lambda e: e.bn_stats(
                bnst[:, 12 * t2 + 6:12 * t2 + 12], xbuf[:, 512:1024]),
              reads=[xtok], writes=["b.bn.%d" % t2])
            A("vector", lambda e: e.bn_aggr(
                mvc[:, 2 * t2:2 * t2 + 2], bnst[:, 12 * t2:12 * t2 + 12]),
              reads=["b.bn.%d" % t2], writes=["b.mv.%d" % t2])

        def ln_bn_vpe(c0):
            mvv = mvc.rearrange("p (t two) -> p t two", two=2)[:, :, 1:2]
            A("vector", lambda e: e.tensor_scalar(
                vpe[:, c0:c0 + 2], mvv, EPS, None, ALU.add),
              reads=["b.mv.0", "b.mv.1"], writes=["b.vpe.%d" % (c0 // 2)])

        def ln_bn_fin(xbuf, xtok, t2, col, out_ap, out_tok, scale_col=None,
                      extra=()):
            cs = (col, col + 1)
            sx = col // 2
            rstd_tok = "b.nrY.%d" % sx
            rstd_ap = nrY
            if scale_col is not None:
                ap, rtok = scale_col
                A("vector", lambda e: e.tensor_tensor(
                    nrT[:, cs[0]:cs[1]], nrY[:, cs[0]:cs[1]], ap, ALU.mult),
                  reads=["b.nrY.%d" % sx, rtok], writes=["b.nrT.%d" % sx])
                rstd_tok = "b.nrT.%d" % sx
                rstd_ap = nrT
            A("vector", lambda e, rstd_ap=rstd_ap: e.tensor_scalar(
                out_ap, xbuf[:, :], mvc[:, 2 * t2:2 * t2 + 1],
                rstd_ap[:, cs[0]:cs[1]], ALU.subtract, ALU.mult),
              reads=[xtok, "b.mv.%d" % t2, rstd_tok] + list(extra),
              writes=[out_tok])

        _p0 = ExitStack()
        pM0 = _p0.enter_context(nc.psum_tensor("pM0", [128, 1024], F32))
        pM1 = _p0.enter_context(nc.psum_tensor("pM1", [128, 1024], F32))
        pMR = _p0.enter_context(nc.psum_tensor("pMR", [1, 1024], F32))
        pTR = _p0.enter_context(nc.psum_tensor("pTR", [128, 512], BF16))
        pSP = _p0.enter_context(nc.psum_tensor("pSP", [128, 64], F32))
        if True:
            def stageE_pre(g):
                A("sync", lambda e, g=g: e.dma_start(
                    out=xg01[:, :], in_=xseg[:, 2048 * g:2048 * g + 2048]),
                  writes=["xg0", "xg1"], dma=True)
                for t in range(2):
                    xb = xg0 if t == 0 else xg1
                    ln_bn(xb, "xg%d" % t, t)
                ln_bn_vpe(0)
                ln_nr_core(0, 2)

            def stageE_fin(g, trp=None):
                gc = GRP * g
                trp = trp if trp is not None else (
                    (pTR2[:, 0:512], "p.SCA"), (pSCBb[:, 0:512], "p.SCB"))
                for t in range(2):
                    xb = xg0 if t == 0 else xg1
                    xtok = "xg%d" % t
                    ln_bn_fin(xb, xtok, t, t, xnb[:, :], "b.xnb")
                    if dbg and g == 0 and t == 0:
                        A("sync", lambda e: e.dma_start(out=d_xnb[:, :], in_=xnb[:, :]),
                          reads=["b.xnb"], writes=["d.out"], dma=True)
                    for c4 in range(2):
                        pT, pTtok = trp[(2 * t + c4) % 2]
                        for bb in range(4):
                            A("tensor", lambda e, c4=c4, bb=bb, pT=pT: e.transpose(
                                pT[:, 128 * bb:128 * bb + 128],
                                xnb[:, 512 * c4 + 128 * bb:512 * c4 + 128 * bb + 128],
                                identS[:, :]),
                              reads=["b.xnb", "c.ident"], writes=[pTtok])
                        A("scalar", lambda e, c4=c4, gc=gc, t=t, pT=pT: e.copy(
                            xnT[:, :].rearrange("p (t n) -> p t n", n=2048)
                               [:, 4 * c4:4 * c4 + 4, gc + 128 * t:gc + 128 * t + 128],
                            pT[:, 0:512].rearrange("p (t n) -> p t n", n=128)),
                          reads=[pTtok], writes=["xnT"])

            def stageE(g, trp=None):
                stageE_pre(g)
                stageE_fin(g, trp=trp)

            def stageM1(g):
                gc = GRP * g
                for it in range(2):
                    for dm in range(NDM):
                        for ns in range(2):
                            A("tensor", lambda e, it=it, dm=dm, ns=ns, gc=gc: e.matmul(
                                pPJ[:, 512 * ns:512 * ns + 512],
                                xnT[:, 2048 * dm + gc + 128 * it:2048 * dm + gc + 128 * it + 128],
                                WvS(1024 * dm + 512 * ns, 512),
                                start=(dm == 0), stop=False, skip_group_check=True),
                              reads=["xnT", "w.v"], writes=["p.PJ0", "p.PJ1"])
                    for ns in range(2):
                        A("tensor", lambda e, ns=ns: e.matmul(
                            pPJ[:, 512 * ns:512 * ns + 512],
                            orbS[0:1, :], bvB[0:1, 512 * ns:512 * ns + 512],
                            start=False, stop=True, skip_group_check=True),
                          reads=["c.orb", "r.bvB"], writes=["p.PJ0", "p.PJ1"])
                    A("scalar", lambda e, it=it: e.copy(
                        VS[:, 1024 * it:1024 * it + 1024], pPJ[:, :]),
                      reads=["p.PJ0", "p.PJ1"], writes=["b.V"])
                    for dm in range(NDM):
                        for ns in range(2):
                            A("tensor", lambda e, it=it, dm=dm, ns=ns, gc=gc: e.matmul(
                                pPJ[:, 512 * ns:512 * ns + 512],
                                xnT[:, 2048 * dm + gc + 128 * it:2048 * dm + gc + 128 * it + 128],
                                WkS(1024 * dm + 512 * ns, 512),
                                start=(dm == 0), stop=False, skip_group_check=True),
                              reads=["xnT", "w.k"], writes=["p.PJ0", "p.PJ1"])
                    for ns in range(2):
                        A("tensor", lambda e, ns=ns: e.matmul(
                            pPJ[:, 512 * ns:512 * ns + 512],
                            orbS[0:1, :], bkB[0:1, 512 * ns:512 * ns + 512],
                            start=False, stop=True, skip_group_check=True),
                          reads=["c.orb", "r.bkB"], writes=["p.PJ0", "p.PJ1"])
                    A("vector", lambda e, it=it: e.tensor_scalar(
                        KuS[:, 1024 * it:1024 * it + 1024], pPJ[:, :],
                        kusS[:, it:it + 1], None, ALU.mult),
                      reads=["p.PJ0", "p.PJ1", "c.kus"], writes=["b.Ku"])
                for which, wsl, wtok in (("q", WqS, "w.q"), ("k", WkS, "w.k")):
                    for dt in range(NDM):
                        half = (dt % 2) * 512
                        ptok = "p.PJ%d" % (dt % 2)
                        for dm in range(NDM):
                            A("tensor", lambda e, wsl=wsl, dt=dt, dm=dm, half=half, gc=gc: e.matmul(
                                pPJ[:, half:half + GRP],
                                wsl(1024 * dm + 128 * dt, 128),
                                xnT[:, 2048 * dm + gc:2048 * dm + gc + GRP],
                                start=(dm == 0), stop=(dm == NDM - 1),
                                skip_group_check=True),
                              reads=[wtok, "xnT"], writes=[ptok])
                        if which == "q":
                            if dbg and g == 0 and dt == 0:
                                A("scalar", lambda e: e.copy(sqsbS[:, :], pPJ[:, 0:256]),
                                  reads=[ptok], writes=["b.sqsb"])
                                A("sync", lambda e: e.dma_start(out=d_PJ[:, :], in_=sqsbS[:, :]),
                                  reads=["b.sqsb"], writes=["d.out"], dma=True)
                            A("vector", lambda e, dt=dt, half=half: e.scalar_tensor_tensor(
                                QsS[:, GRP * dt:GRP * dt + GRP],
                                pPJ[:, half:half + GRP],
                                bqTs[:, dt:dt + 1], qdecS[:, :],
                                ALU.add, ALU.mult),
                              reads=[ptok, "r.bqT", "c.qdec"], writes=["b.Qs"])
                        elif which == "k":
                            A("vector", lambda e, dt=dt, half=half: e.scalar_tensor_tensor(
                                KsS[:, GRP * dt:GRP * dt + GRP],
                                pPJ[:, half:half + GRP],
                                bkTs[:, dt:dt + 1], kdecS[:, :],
                                ALU.add, ALU.mult),
                              reads=[ptok, "r.bkT", "c.kdec"], writes=["b.Ks"])


            def stageM2(g):
                gc = GRP * g
                A("scalar", lambda e: e.copy(sb16[:, :], state[:, :]),
                  reads=["b.state"], writes=["b.sb16"])
                for which, wsl, wtok in (("g", WgS, "w.g"),):
                    for dt in range(NDM):
                        half = (dt % 2) * 512
                        ptok = "p.PJ%d" % (dt % 2)
                        for dm in range(NDM):
                            A("tensor", lambda e, wsl=wsl, dt=dt, dm=dm, half=half, gc=gc: e.matmul(
                                pPJ[:, half:half + GRP],
                                wsl(1024 * dm + 128 * dt, 128),
                                xnT[:, 2048 * dm + gc:2048 * dm + gc + GRP],
                                start=(dm == 0), stop=(dm == NDM - 1),
                                skip_group_check=True),
                              reads=[wtok, "xnT"], writes=[ptok])
                        A("scalar", lambda e, dt=dt, half=half: e.activation(
                            GtS[:, GRP * dt:GRP * dt + GRP],
                            pPJ[:, half:half + GRP],
                            AF.Sigmoid, bias=bgTs[:, dt:dt + 1]),
                          reads=[ptok, "r.bgT"], writes=["b.Gt.%d" % dt])
                for hp in range(NP):
                    if hp % 2 == 0:
                        jb0b, jb0t = pSCB, "p.SCB"
                        jb1b, jb1t, jb1c = pSCA, "p.SCA", 256
                    else:
                        jb0b, jb0t = pLATE, "p.LATE"
                        jb1b, jb1t, jb1c = pS, "p.S", 0
                    for hh in range(2):
                        r0 = 64 * hh
                        jb0ap = jb0b[:, 256 * hh:256 * hh + 256]
                        jb1ap = jb1b[:, jb1c + 128 * hh:jb1c + 128 * hh + 128]
                        A("tensor", lambda e, hp=hp, r0=r0, jb0ap=jb0ap: e.matmul(
                            jb0ap,
                            KsS[r0:r0 + 64, GRP * hp:GRP * hp + 128],
                            QsS[r0:r0 + 64, GRP * hp:GRP * hp + GRP],
                            start=True, stop=True, skip_group_check=True),
                          reads=["b.Ks", "b.Qs"], writes=[jb0t])
                        A("tensor", lambda e, hp=hp, r0=r0, jb1ap=jb1ap: e.matmul(
                            jb1ap,
                            KsS[r0:r0 + 64, GRP * hp + 128:GRP * hp + 256],
                            QsS[r0:r0 + 64, GRP * hp + 128:GRP * hp + 256],
                            start=True, stop=True, skip_group_check=True),
                          reads=["b.Ks", "b.Qs"], writes=[jb1t])
                        A("vector", lambda e, hh=hh, jb0ap=jb0ap: e.tensor_tensor(
                            Pm[hh][:, 0:256], jb0ap, maskS[:, 0:256],
                            ALU.mult),
                          reads=[jb0t, "c.mask"], writes=["b.Pm0%d" % hh])
                        A("vector", lambda e, hh=hh, jb1ap=jb1ap: e.tensor_tensor(
                            Pm[2 + hh][:, 0:128], jb1ap, maskS[:, 0:128],
                            ALU.mult),
                          reads=[jb1t, "c.mask"], writes=["b.Pm1%d" % hh])
                    if dbg and g == 0 and hp == 0:
                        A("sync", lambda e: e.dma_start(
                            out=d_Pm[:, :], in_=Pm[0][:, 0:256]),
                          reads=["b.Pm00"], writes=["d.out"], dma=True)
                    for hh in range(2):
                        h = 2 * hp + hh
                        ro = 64 * hh
                        A("tensor", lambda e, h=h, ro=ro, hh=hh: e.matmul(
                            pRT[ro:ro + 64, 0:256] if ro == 0 else pRT[64:128, 0:256],
                            VS[:, 64 * h:64 * h + 64],
                            Pm[hh][:, 0:256],
                            start=True, stop=False, skip_group_check=True,
                            tile_position=(0, ro)),
                          reads=["b.V", "b.Pm0%d" % hh], writes=["p.RT"])
                        A("tensor", lambda e, h=h, ro=ro, hh=hh: e.matmul(
                            pRT[ro:ro + 64, 128:256] if ro == 0 else pRT[64:128, 128:256],
                            VS[:, 1024 + 64 * h:1024 + 64 * h + 64],
                            Pm[2 + hh][:, 0:128],
                            start=False, stop=False, skip_group_check=True,
                            tile_position=(0, ro)),
                          reads=["b.V", "b.Pm1%d" % hh], writes=["p.RT"])
                        A("tensor", lambda e, h=h, ro=ro, hp=hp: e.matmul(
                            pRT[ro:ro + 64, 0:256] if ro == 0 else pRT[64:128, 0:256],
                            sb16[ro:ro + 64, 64 * hp:64 * hp + 64],
                            QsS[ro:ro + 64, GRP * hp:GRP * hp + GRP],
                            start=False, stop=(ro == 64), skip_group_check=True,
                            tile_position=(ro, ro)),
                          reads=["b.sb16", "b.Qs"], writes=["p.RT"])
                    A("scalar", lambda e, hp=hp: e.copy(
                        rsbS[:, GRP * hp:GRP * hp + GRP], pRT[:, 0:256]),
                      reads=["p.RT"], writes=["b.rsb.%d" % hp])
                    for hh in range(2):
                        h = 2 * hp + hh
                        ro = 64 * hh
                        for it in range(2):
                            A("tensor", lambda e, h=h, ro=ro, it=it, hp=hp, g=g: e.matmul(
                                pSU[ro:ro + 64, 64 * hp:64 * hp + 64] if ro == 0
                                else pSU[64:128, 64 * hp:64 * hp + 64],
                                KuS[:, 1024 * it + 64 * h:1024 * it + 64 * h + 64],
                                VS[:, 1024 * it + 64 * h:1024 * it + 64 * h + 64],
                                start=(it == 0), stop=(it == 1),
                                skip_group_check=True, tile_position=(0, ro)),
                              reads=["b.Ku", "b.V"], writes=["p.SU"])
                A("vector", lambda e: e.scalar_tensor_tensor(
                    state[:, :], state[:, :], D256, pSU[:, :], ALU.mult, ALU.add),
                  reads=["b.state", "p.SU"], writes=["b.state"])
                if dbg and g == 0:
                    A("sync", lambda e: e.dma_start(out=d_Wq[:, :], in_=WqS(0, 8192)),
                      reads=["w.q"], writes=["d.out"], dma=True)
                    A("sync", lambda e: e.dma_start(out=d_Wk[:, :], in_=WkS(0, 8192)),
                      reads=["w.k"], writes=["d.out"], dma=True)
                    for dd, sb_, tok in [(d_Qs, QsS, "b.Qs"), (d_Ks, KsS, "b.Ks"),
                                         (d_V, VS, "b.V"), (d_Ku, KuS, "b.Ku"),
                                         (d_Gt, GtS, "b.Gt"), (d_rsb, rsbS, "b.rsb"),
                                         (d_S1, state, "b.state")]:
                        A("sync", lambda e, dd=dd, sb_=sb_: e.dma_start(
                            out=dd[:, :], in_=sb_[:, 0:dd.shape[1]]),
                          reads=[tok], writes=["d.out"], dma=True)

            def stageS(g):
                gc = GRP * g
                for dt in range(NDM):
                    A("tensor", lambda e, dt=dt: e.matmul(
                        pLATE[0:1, 0:256], ocfS[:, :],
                        rsbS[:, GRP * dt:GRP * dt + GRP],
                        start=(dt == 0), stop=(dt == NDM - 1),
                        skip_group_check=True),
                      reads=["c.ocf", "b.rsb.%d" % dt], writes=["p.LATE"])
                A("vector", lambda e: e.tensor_scalar(
                    nmuR, pLATE[0:1, 0:256], -1.0 / DM, None, ALU.mult),
                  reads=["p.LATE"], writes=["b.tA0"])
                for dt in range(NDM):
                    sqap = AsS[:, 256 * dt:256 * dt + 256]
                    A("scalar", lambda e, dt=dt, sqap=sqap: e.activation(
                        sqap, rsbS[:, GRP * dt:GRP * dt + GRP], AF.Square),
                      reads=["b.rsb.%d" % dt], writes=["b.As.%d" % dt])
                    A("tensor", lambda e, dt=dt, sqap=sqap: e.matmul(
                        pLATE[0:1, 256:512], ocbS[:, :], sqap,
                        start=(dt == 0), stop=(dt == NDM - 1),
                        skip_group_check=True),
                      reads=["c.ocb", "b.As.%d" % dt], writes=["p.LATE"])
                A("vector", lambda e: e.tensor_tensor(
                    tR, nmuR, nmuR, ALU.mult),
                  reads=["b.tA0"], writes=["b.sqsb"])
                A("vector", lambda e: e.scalar_tensor_tensor(
                    vR, pLATE[0:1, 256:512], 1.0 / DM, tR,
                    ALU.mult, ALU.subtract),
                  reads=["p.LATE", "b.sqsb"], writes=["b.vR"])
                A("vector", lambda e: e.tensor_scalar(
                    vR, vR, EPS, None, ALU.add),
                  reads=["b.vR"], writes=["b.vR"])
                A("vector", lambda e: e.tensor_scalar(
                    yR.bitcast(I32), vR.bitcast(I32), 1, None,
                    ALU.logical_shift_right), reads=["b.vR"], writes=["b.yR"])
                A("vector", lambda e: e.tensor_scalar(
                    yR.bitcast(I32), yR.bitcast(I32), -1,
                    None, ALU.bitwise_xor), reads=["b.yR"], writes=["b.yR"])
                A("vector", lambda e: e.tensor_scalar(
                    yR.bitcast(I32), yR.bitcast(I32), MAGIC + 1,
                    None, ALU.add), reads=["b.yR"], writes=["b.yR"])
                for _ in range(2):
                    A("vector", lambda e: e.tensor_tensor(
                        tR, yR, yR, ALU.mult),
                      reads=["b.yR"], writes=["b.sqsb"])
                    A("vector", lambda e: e.tensor_tensor(
                        tR, tR, vR, ALU.mult),
                      reads=["b.sqsb", "b.vR"], writes=["b.sqsb"])
                    A("vector", lambda e: e.tensor_scalar(
                        tR, tR, -0.5, 1.5, ALU.mult, ALU.add),
                      reads=["b.sqsb"], writes=["b.sqsb"])
                    A("vector", lambda e: e.tensor_tensor(
                        yR, yR, tR, ALU.mult),
                      reads=["b.yR", "b.sqsb"], writes=["b.yR"])
                A("vector", lambda e: e.tensor_tensor(
                    nmrR, nmuR, yR, ALU.mult),
                  reads=["b.tA0", "b.yR"], writes=["b.nmrR"])

            def stageL(g):
                gc = GRP * g
                A("tensor", lambda e: e.matmul(
                    pLATE[:, 0:256], orfS[0:1, :], yR,
                    start=True, stop=True, skip_group_check=True),
                  reads=["c.orf", "b.yR"], writes=["p.LATE"])
                A("tensor", lambda e: e.matmul(
                    pLATE[:, 256:512], orfS[0:1, :], nmrR,
                    start=True, stop=True, skip_group_check=True),
                  reads=["c.orf", "b.nmrR"], writes=["p.LATE"])
                for dt in range(NDM):
                    tA_ = tAS if dt % 2 == 0 else tA2S
                    tB_ = tBS if dt % 2 == 0 else tB2S
                    tat = "b.tA%d" % (dt % 2)
                    tbt = "b.tB%d" % (dt % 2)
                    A("vector", lambda e, dt=dt, tA_=tA_: e.tensor_tensor(
                        tA_[:, :], rsbS[:, GRP * dt:GRP * dt + GRP], pLATE[:, 0:256],
                        ALU.mult), reads=["b.rsb.%d" % dt, "p.LATE"], writes=[tat])
                    A("vector", lambda e, dt=dt, tA_=tA_, tB_=tB_: e.tensor_tensor(
                        tB_[:, :], tA_[:, :], pLATE[:, 256:512], ALU.add),
                      reads=[tat, "p.LATE"], writes=[tbt])
                    A("vector", lambda e, dt=dt, tA_=tA_, tB_=tB_: e.tensor_scalar(
                        tA_[:, :], tB_[:, :], gmTs[:, dt:dt + 1], btTs[:, dt:dt + 1],
                        ALU.mult, ALU.add),
                      reads=[tbt, "r.gmT", "r.btT"], writes=[tat])
                    A("gpsimd", lambda e, dt=dt, tA_=tA_: e.tensor_tensor(
                        AsS[:, GRP * dt:GRP * dt + GRP], tA_[:, :],
                        GtS[:, GRP * dt:GRP * dt + GRP], ALU.mult),
                      reads=[tat, "b.Gt.%d" % dt], writes=["b.As.%d" % dt])
                if dbg and g == 0:
                    A("sync", lambda e: e.dma_start(out=d_As[:, :], in_=AsS[:, :]),
                      reads=["b.As.%d" % d_ for d_ in range(NDM)], writes=["d.out"], dma=True)
                for it in range(2):
                    for ns in range(2):
                        for dt in range(NDM):
                            A("tensor", lambda e, it=it, dt=dt, ns=ns: e.matmul(
                                pS[:, :],
                                AsS[:, GRP * dt + 128 * it:GRP * dt + 128 * it + 128],
                                WoS(1024 * dt + 512 * ns, 512),
                                start=(dt == 0), stop=False, skip_group_check=True),
                              reads=["b.As.%d" % dt, "w.o"], writes=["p.S"])
                        A("tensor", lambda e, ns=ns: e.matmul(
                            pS[:, :],
                            orbS[0:1, :], boB[0:1, 512 * ns:512 * ns + 512],
                            start=False, stop=True, skip_group_check=True),
                          reads=["c.orb", "r.boB"], writes=["p.S"])
                        A("scalar", lambda e, ns=ns: e.copy(
                            sstgS[:, 512 * ns:512 * ns + 512], pS[:, :]),
                          reads=["p.S"], writes=["b.sstg"])
                    A("gpsimd", lambda e, gc=gc, it=it: e.dma_start(
                        out=s_scr[gc + 128 * it:gc + 128 * it + 128, :],
                        in_=sstgS[:, :]),
                      reads=["b.sstg"], writes=["s_scr.%d.%d" % (g, it)], dma=True)

            if skip_p0:
                A("vector", lambda e: e.memset(state[:, :], 0.0),
                  writes=["b.state"])
            if not skip_p0:
                for o in range(3):
                    for jp in range(4):
                        pr = 4 * o + jp
                        jt0 = 2 * pr
                        buf = xg01 if pr % 2 == 0 else xg23
                        btok = ["xg0", "xg1"] if pr % 2 == 0 else ["xg2", "xg3"]
                        c0 = jt0 % 4
                        A("sync", lambda e, jt0=jt0, buf=buf: e.dma_start(
                            out=buf[:, :],
                            in_=xpre[:, 1024 * jt0:1024 * jt0 + 2048]),
                          writes=btok, dma=True)
                        for t2 in range(2):
                            ln_bn(buf[:, 1024 * t2:1024 * t2 + 1024],
                                  btok[t2], t2)
                        ln_bn_vpe(c0)
                        ln_nr_core(c0, 2)
                        for t2 in range(2):
                            jt = jt0 + t2
                            ln_bn_fin(buf[:, 1024 * t2:1024 * t2 + 1024],
                                      btok[t2], t2, c0 + t2,
                                      XH(1024 * jt, 1024), "xnh.%d" % jt,
                                      scale_col=(cpreS[:, jt:jt + 1], "r.cpre"))
                            for ns in range(2):
                                A("tensor", lambda e, jt=jt, ns=ns: e.matmul(
                                    pMR[0:1, 512 * ns:512 * ns + 512],
                                    cpreB[:, jt:jt + 1],
                                    XH(1024 * jt + 512 * ns, 512),
                                    start=(jt == 0), stop=(jt == NPT - 1),
                                    skip_group_check=True),
                                  reads=["r.cpreB", "xnh.%d" % jt],
                                  writes=["p.MR"])
                    if o == 0:
                        A("gpsimd", lambda e: e.dma_start(
                            out=WkS(0, 8192), in_=Wk.ap()),
                          writes=["w.k"], dma=True)
                        A("gpsimd", lambda e: e.dma_start(
                            out=WvS(0, 8192), in_=Wv.ap()),
                          writes=["w.v"], dma=True)
                    for at in range(NDM):
                        pM = pM0 if at % 2 == 0 else pM1
                        ptok = "p.M%d" % (at % 2)
                        if o > 0:
                            for ns in range(2):
                                A("tensor", lambda e, at=at, ns=ns, pM=pM: e.matmul(
                                    pM[:, 512 * ns:512 * ns + 512],
                                    identS[:, :],
                                    MS(1024 * at + 512 * ns, 512),
                                    start=True, stop=False,
                                    skip_group_check=True),
                                  reads=["b.M.%d" % at, "c.ident"],
                                  writes=[ptok])
                        for j in range(8):
                            jt = 8 * o + j
                            for ns in range(2):
                                A("tensor", lambda e, jt=jt, at=at, ns=ns, pM=pM, o=o: e.matmul(
                                    pM[:, 512 * ns:512 * ns + 512],
                                    XH(1024 * jt + 128 * at, 128),
                                    XH(1024 * jt + 512 * ns, 512),
                                    start=(jt % 8 == 0 and o == 0),
                                    stop=(jt % 8 == 7),
                                    skip_group_check=True),
                                  reads=["xnh.%d" % jt], writes=[ptok])
                        A("scalar", lambda e, at=at, pM=pM: e.copy(
                            MS(1024 * at, 1024), pM[:, :]),
                          reads=[ptok], writes=["b.M.%d" % at])
                A("vector", lambda e: e.tensor_copy(mrowB, pMR[0:1, :]),
                  reads=["p.MR"], writes=["r.mrow", "xnh.9"])
                A("vector", lambda e: e.tensor_copy(mstage[0:1, 0:1024], pMR[0:1, :]),
                  reads=["p.MR"], writes=["r.mst", "xnh.8"])
                for t in range(NDM):
                    A("tensor", lambda e, t=t: e.transpose(
                        pTR[:, 0:128], mstage[:, 128 * t:128 * t + 128],
                        identS[:, :]),
                      reads=["r.mst", "xnh.8", "c.ident"], writes=["p.TR"])
                    A("scalar", lambda e, t=t: e.copy(
                        mcolB[:, t:t + 1], pTR[:, 0:1]),
                      reads=["p.TR"], writes=["r.mcol"])
                for at in range(NDM):
                    pM = pM0 if at % 2 == 0 else pM1
                    ptok = "p.M%d" % (at % 2)
                    for bt in range(NDM):
                        for ns in range(2):
                            A("tensor", lambda e, at=at, bt=bt, ns=ns, pM=pM: e.matmul(
                                pM[:, 512 * ns:512 * ns + 512],
                                MS(1024 * bt + 128 * at, 128),
                                WvS(1024 * bt + 512 * ns, 512),
                                start=(bt == 0), stop=False,
                                skip_group_check=True),
                              reads=["b.M.%d" % bt, "w.v"], writes=[ptok])
                    for ns in range(2):
                        A("tensor", lambda e, at=at, ns=ns, pM=pM: e.matmul(
                            pM[:, 512 * ns:512 * ns + 512],
                            mrowB[0:1, 128 * at:128 * at + 128],
                            bvB[0:1, 512 * ns:512 * ns + 512],
                            start=False, stop=True, skip_group_check=True),
                          reads=["r.mrow", "r.bvB"], writes=[ptok])
                    A("scalar", lambda e, at=at, pM=pM: e.copy(
                        TS(1024 * at, 1024), pM[:, :]),
                      reads=[ptok], writes=["b.T"])
            A("gpsimd", lambda e: e.dma_start(out=WqS(0, 8192), in_=Wq.ap()),
              writes=["w.q"] + ["b.M.%d" % at_ for at_ in range(NDM)],
              dma=True)
            stageE(0, trp=((pTR, "p.TR"), (pTR, "p.TR")))
            if not skip_p0:
                A("vector", lambda e: e.tensor_scalar(
                    bvsB, bvB[0:1, :], csumS[0:1, 0:1], None, ALU.mult),
                  reads=["r.bvB", "r.csum"], writes=["r.bvs", "xnh.7"])
                for at in range(NDM):
                    for ns in range(2):
                        A("tensor", lambda e, at=at, ns=ns: e.matmul(
                            pMR[0:1, 512 * ns:512 * ns + 512],
                            mcolB[:, at:at + 1],
                            WvS(1024 * at + 512 * ns, 512),
                            start=(at == 0), stop=False, skip_group_check=True),
                          reads=["r.mcol", "w.v"], writes=["p.MR"])
                for ns in range(2):
                    A("tensor", lambda e, ns=ns: e.matmul(
                        pMR[0:1, 512 * ns:512 * ns + 512],
                        orbS[0:1, 0:1], bvsB[0:1, 512 * ns:512 * ns + 512],
                        start=False, stop=True, skip_group_check=True),
                      reads=["c.orb", "r.bvs"], writes=["p.MR"])
                A("vector", lambda e: e.tensor_copy(zB, pMR[0:1, :]),
                  reads=["p.MR"], writes=["r.zB", "xnh.6"])
            stageE(1, trp=((pTR, "p.TR"), (pTR, "p.TR")))
            if not skip_p0:
                for hp in range(NP):
                    for hh in range(2):
                        h = 2 * hp + hh
                        ro = 64 * hh
                        for at in range(NDM):
                            A("tensor", lambda e, h=h, ro=ro, at=at: e.matmul(
                                pSP[ro:ro + 64, 0:64] if ro == 0 else pSP[64:128, 0:64],
                                WkS(1024 * at + 64 * h, 64),
                                TS(1024 * at + 64 * h, 64),
                                start=(at == 0), stop=False,
                                skip_group_check=True, tile_position=(0, ro)),
                              reads=["w.k", "b.T"], writes=["p.SP"])
                        A("tensor", lambda e, h=h, ro=ro: e.matmul(
                            pSP[ro:ro + 64, 0:64] if ro == 0 else pSP[64:128, 0:64],
                            bkB[0:1, 64 * h:64 * h + 64],
                            zB[0:1, 64 * h:64 * h + 64],
                            start=False, stop=(ro == 64), skip_group_check=True,
                            tile_position=(0, ro)),
                          reads=["r.bkB", "r.zB"], writes=["p.SP"])
                    A("vector", lambda e, hp=hp: e.tensor_copy(
                        state[:, 64 * hp:64 * hp + 64], pSP[:, 0:64]),
                      reads=["p.SP"], writes=["b.state"])
            if skip_p0:
                A("gpsimd", lambda e: e.dma_start(out=WkS(0, 8192), in_=Wk.ap()),
                  writes=["w.k"], dma=True)
                A("gpsimd", lambda e: e.dma_start(out=WvS(0, 8192), in_=Wv.ap()),
                  writes=["w.v"], dma=True)
            A("gpsimd", lambda e: e.dma_start(out=WgS(0, 8192), in_=Wg.ap()),
              writes=["w.g", "b.T"], dma=True)
            A("gpsimd", lambda e: e.dma_start(out=WoS(0, 8192), in_=Wo.ap()),
              writes=["w.o"], dma=True)
            A("gpsimd", lambda e: e.dma_start(out=w1b[:, :], in_=w1.ap()),
              writes=["w.1hbm"], dma=True)
            A("gpsimd", lambda e: e.dma_start(out=w2b[:, :], in_=w2.ap()),
              writes=["w.2hbm"], dma=True)
            if dbg:
                A("sync", lambda e: e.dma_start(out=d_S0[:, :], in_=state[:, :]),
                  reads=["b.state"], writes=["d.out"], dma=True)
        _p0.close()

        p.psum_scope(["p.M0", "p.M1", "p.MR", "p.TR", "p.SP"])
        _pa = ExitStack()
        pPJ = _pa.enter_context(nc.psum_tensor("pPJ", [128, 1024], F32))
        pSCA = _pa.enter_context(nc.psum_tensor("pSCA", [128, 512], F32))
        pSCB = _pa.enter_context(nc.psum_tensor("pSCB", [128, 512], F32))
        pRT = _pa.enter_context(nc.psum_tensor("pRT", [128, 512], F32))
        pSU = _pa.enter_context(nc.psum_tensor("pSU", [128, 512], F32))
        pLATE = _pa.enter_context(nc.psum_tensor("pLATE", [128, 512], F32))
        pS = _pa.enter_context(nc.psum_tensor("pS", [128, 512], F32))
        pTR2 = pSCA.bitcast(BF16)
        pSCBb = pSCB.bitcast(BF16)
        if True:
            stageM1(0)
            for g in range(NG):
                stageM2(g)
                if g + 2 < NG:
                    stageE_pre(g + 2)
                stageS(g)
                if g + 1 < NG:
                    stageM1(g + 1)
                if g + 2 < NG:
                    stageE_fin(g + 2)
                stageL(g)
        _pa.close()
        if dbg:
            A("sync", lambda e: e.dma_start(out=d_xnT[:, :], in_=xnT[:, :]),
              reads=["xnT"], writes=["d.out"], dma=True)
            A("sync", lambda e: e.dma_start(out=d_s[:, :], in_=s_scr[:, :]),
              reads=["s_scr.%d.%d" % (g_, i_) for g_ in range(NG) for i_ in range(2)],
              writes=["d.out"], dma=True)
        p.psum_scope(["p.PJ0", "p.PJ1", "p.SCA", "p.SCB", "p.RT", "p.SU",
                      "p.LATE", "p.S"])
        A("sync", lambda e: e.dma_start(out=w1S(0, 32768), in_=w1b.ap()),
          reads=["w.1hbm"],
          writes=["w.1", "w.k", "w.v", "w.q", "w.g"]
                 + ["b.M.%d" % at_ for at_ in range(NDM)] + ["b.T"], dma=True)
        A("scalar", lambda e: e.dma_start(out=w2S(0, 32768), in_=w2b.ap()),
          reads=["w.2hbm"],
          writes=("w.2", "w.o", "r.mst", "r.mrow", "r.zB", "r.bvs")
                 + tuple("xnh.%d" % j_ for j_ in range(NPT)) + WB2_A_TOKENS,
          dma=True)

        _p3 = ExitStack()
        hT = _p3.enter_context(nc.sbuf_tensor("hT", [128, 4096], BF16))
        sld0 = _p3.enter_context(nc.sbuf_tensor("sld", [128, 1024], F32))
        sld1 = sld0
        scrA = xg23[:, 1024:2048]
        scrB = xg23[:, 0:1024]
        pH0 = _p3.enter_context(nc.psum_tensor("pH0", [128, 1024], F32))
        pH1 = _p3.enter_context(nc.psum_tensor("pH1", [128, 1024], F32))
        pY0 = _p3.enter_context(nc.psum_tensor("pY0", [128, 1024], F32))
        pY1 = _p3.enter_context(nc.psum_tensor("pY1", [128, 1024], F32))
        if True:
            nout = 0
            for g in range(NG):
                gc = GRP * g
                for half in range(2):
                    for q in range(4):
                        pH = pH0 if q % 2 == 0 else pH1
                        htok = "p.H%d" % (q % 2)
                        for sub in range(4):
                            dht = 16 * half + 4 * q + sub
                            for dm in range(NDM):
                                A("tensor", lambda e, dht=dht, dm=dm, sub=sub, pH=pH, gc=gc: e.matmul(
                                    pH[:, 256 * sub:256 * sub + 256],
                                    w1S(4096 * dm + 128 * dht, 128),
                                    xnT[:, 2048 * dm + gc:2048 * dm + gc + GRP],
                                    start=(dm == 0),
                                    stop=(dm == NDM - 1), skip_group_check=True),
                                  reads=["w.1", "xnT"], writes=[htok])
                        for sub in range(4):
                            dht = 16 * half + 4 * q + sub
                            lt = 4 * q + sub
                            A("scalar", lambda e, dht=dht, lt=lt, sub=sub, pH=pH: e.activation(
                                hT[:, 256 * lt:256 * lt + 256],
                                pH[:, 256 * sub:256 * sub + 256],
                                AF.Gelu, bias=b1Ts[:, dht:dht + 1]),
                              reads=[htok, "r.b1T"], writes=["b.hT.%d" % lt])
                    if dbg and g == 0 and half == 0:
                        A("sync", lambda e: e.dma_start(out=d_hT[:, :], in_=hT[:, :]),
                          reads=["b.hT.%d" % l_ for l_ in range(16)], writes=["d.out"], dma=True)
                    for it in range(2):
                        pY = pY0 if it % 2 == 0 else pY1
                        ytok = "p.Y%d" % (it % 2)
                        for lt in range(16):
                            dht = 16 * half + lt
                            for ns in range(2):
                                A("tensor", lambda e, lt=lt, dht=dht, ns=ns, it=it, pY=pY, half=half: e.matmul(
                                    pY[:, 512 * ns:512 * ns + 512],
                                    hT[:, 256 * lt + 128 * it:256 * lt + 128 * it + 128],
                                    w2S(1024 * dht + 512 * ns, 512),
                                    start=(lt == 0 and half == 0), stop=False,
                                    skip_group_check=True),
                                  reads=["b.hT.%d" % lt, "w.2"], writes=[ytok])
                for it in range(2):
                    pY = pY0 if it % 2 == 0 else pY1
                    ytok = "p.Y%d" % (it % 2)
                    xb = xg0 if it % 2 == 0 else xg1
                    xtok = "xg%d" % (it % 2)
                    sb = sld0 if it % 2 == 0 else sld1
                    stok = "b.sld"
                    A("sync", lambda e, g=g, it=it, xb=xb: e.dma_start(
                        out=xb[:, :],
                        in_=xseg[:, 1024 * (2 * g + it):1024 * (2 * g + it) + 1024]),
                      writes=[xtok], dma=True)
                    A("sync", lambda e, gc=gc, it=it, sb=sb: e.dma_start(
                        out=sb[:, :], in_=s_scr[gc + 128 * it:gc + 128 * it + 128, :]),
                      reads=["s_scr.%d.%d" % (g, it)], writes=[stok], dma=True)
                    for ns in range(2):
                        A("tensor", lambda e, ns=ns, pY=pY: e.matmul(
                            pY[:, 512 * ns:512 * ns + 512],
                            orbS[0:1, :], b2B[0:1, 512 * ns:512 * ns + 512],
                            start=False, stop=True, skip_group_check=True),
                          reads=["c.orb", "r.b2B"], writes=[ytok])
                    A("scalar", lambda e, pY=pY: e.activation(
                        scrA[:, :], pY[:, :], AF.Tanh, scale=0.5),
                      reads=[ytok], writes=["b.scrA"])
                    A("vector", lambda e, pY=pY: e.scalar_tensor_tensor(
                        scrB[:, :], pY[:, :], 0.5, scrA[:, :], ALU.mult, ALU.mult),
                      reads=[ytok, "b.scrA"], writes=["b.scrB"])
                    A("vector", lambda e, pY=pY: e.scalar_tensor_tensor(
                        scrA[:, :], pY[:, :], 0.5, scrB[:, :], ALU.mult, ALU.add),
                      reads=[ytok, "b.scrB"], writes=["b.scrA"])
                    A("gpsimd", lambda e, sb=sb: e.tensor_tensor(
                        scrB[:, :], sb[:, :], scrA[:, :], ALU.mult),
                      reads=[stok, "b.scrA"], writes=["b.scrB"])
                    A("gpsimd", lambda e, sb=sb, xb=xb: e.tensor_tensor(
                        sb[:, :], xb[:, :], scrB[:, :], ALU.add),
                      reads=[xtok, "b.scrB"], writes=[stok])
                    A("gpsimd", lambda e, gc=gc, it=it, sb=sb: e.dma_start(
                        out=out[gc + 128 * it:gc + 128 * it + 128, :], in_=sb[:, :]),
                      reads=[stok], writes=["d.out.%d" % nout], dma=True)
                    nout += 1
            A("scalar", None, reads=["d.out.%d" % i for i in range(nout)])

        _p3.close()
        def SEM(nm):
            return _st.enter_context(nc.semaphore(nm))
        sems = {"c_tensor": SEM("sem_t"), "c_vector": SEM("sem_v"),
                "c_scalar": SEM("sem_s"), "c_gpsimd": SEM("sem_g"),
                "c_sync": SEM("sem_sy")}
        for qq in ("sync", "gpsimd", "scalar"):
            for sl in range(8):
                sems["dma_%s_%d" % (qq, sl)] = SEM("dq_%s_%d" % (qq, sl))
        for sl in range(8):
            sems["gps_%d" % sl] = SEM("gpc_%d" % sl)
        with nc.Block() as block:
            p.emit(nc, block, sems, serial=bool(globals().get("SERIAL", False)))
        _st.close()
    return nc



def _pm(a, nt, n):
    return np.ascontiguousarray(
        np.asarray(a, np.float32).reshape(nt, 128, n)
        .transpose(1, 0, 2).reshape(128, nt * n))


def shared_inputs(inputs):
    f32 = np.float32

    def row(v):
        return np.ascontiguousarray(np.asarray(v, f32).reshape(1, -1))

    def colT(v, nt):
        return np.ascontiguousarray(np.asarray(v, f32).reshape(nt, 128).T)

    return dict(
        Wq=_pm(inputs["L0_attn_Wq"], NDM, DM),
        Wk=_pm(inputs["L0_attn_Wk"], NDM, DM),
        Wv=_pm(inputs["L0_attn_Wv"], NDM, DM),
        Wg=_pm(inputs["L0_attn_Wg"], NDM, DM),
        Wo=_pm(inputs["L0_attn_Wo"], NDM, DM),
        w1=_pm(inputs["L0_ffn_w1"], NDM, DFF),
        w2=_pm(inputs["L0_ffn_w2"], NDH, DM),
        b_v=row(inputs["L0_attn_bv"]), b_k=row(inputs["L0_attn_bk"]),
        b_o=row(inputs["L0_attn_bo"]), b_2=row(inputs["L0_ffn_b2"]),
        bqT=colT(inputs["L0_attn_bq"], NDM), bkT=colT(inputs["L0_attn_bk"], NDM),
        bgT=colT(inputs["L0_attn_bg"], NDM), b1T=colT(inputs["L0_ffn_b1"], NDH),
        gmT=colT(inputs["L0_attn_gamma"], NDM), btT=colT(inputs["L0_attn_beta"], NDM),
    )


def core_inputs(inputs, core, shared=None):
    f32 = np.float32
    x = np.asarray(inputs["L0_attn_x"])
    b, seg = core // 4, core % 4
    t0 = SEG * seg
    xseg = np.ascontiguousarray(x[b, t0:t0 + SEG], f32)
    xpre = np.zeros((PRE, DM), f32)
    lo = max(0, t0 - PRE)
    if t0 > 0:
        xpre[PRE - (t0 - lo):] = x[b, lo:t0]
    w = np.arange(PRE, dtype=np.float64)
    c = D ** (PRE - w)
    c[w < PRE - t0] = 0.0
    csqrt = np.sqrt(c)
    cpre_l = np.ascontiguousarray(csqrt.reshape(NPT, 128).T.astype(f32))
    csum_l = np.array([[c.sum()]], dtype=f32)
    m = dict(shared if shared is not None else shared_inputs(inputs))
    m.update(xseg=_pm(xseg, SEG // 128, DM), xpre=_pm(xpre, NPT, DM),
             cpre=cpre_l, csum=csum_l)
    return m


def kernel(**inputs):
    shared = shared_inputs(inputs)
    in_maps = [core_inputs(inputs, core, shared) for core in range(8)]

    nc = build_nc()
    res = run_bass_kernel_spmd(nc, in_maps, core_ids=list(range(8)))
    shards = [res.results[i]["out"] for i in range(8)]
    full = np.concatenate(shards, axis=0).reshape(2, 8192, 1024)
    return np.ascontiguousarray(full.astype(np.float32))


if __name__ == "__main__":
    rng = np.random.default_rng(0)
    ins = {
        "L0_attn_x": rng.standard_normal((2, 8192, 1024)).astype(np.float32),
    }
    for nm in ["Wq", "Wk", "Wv", "Wg", "Wo"]:
        ins["L0_attn_" + nm] = (rng.standard_normal((1024, 1024)) * 0.02).astype(np.float32)
    for nm in ["bq", "bk", "bv", "bg", "bo", "gamma", "beta"]:
        ins["L0_attn_" + nm] = rng.standard_normal(1024).astype(np.float32) * 0.02
    ins["L0_attn_gamma"] += 1.0
    ins["L0_ffn_w1"] = (rng.standard_normal((1024, 4096)) / 32).astype(np.float32)
    ins["L0_ffn_b1"] = (rng.standard_normal(4096) * 0.02).astype(np.float32)
    ins["L0_ffn_w2"] = (rng.standard_normal((4096, 1024)) / 64).astype(np.float32)
    ins["L0_ffn_b2"] = (rng.standard_normal(1024) * 0.02).astype(np.float32)
    o = kernel(**ins)
    print("kernel ran", o.shape, o.dtype)


# version 24 (final)
# speedup vs baseline: 1.2735x; 1.0338x over previous
import numpy as np
import ml_dtypes
import concourse.bass as bass
import concourse.mybir as mybir
from concourse.bass_utils import run_bass_kernel_spmd

F32 = mybir.dt.float32
BF16 = mybir.dt.bfloat16
I32 = mybir.dt.int32
AF = mybir.ActivationFunctionType
ALU = mybir.AluOpType
AX = mybir.AxisListType

D = 0.99
SEG = 2048
PRE = 3072
GRP = 256
NG = SEG // GRP
NPT = PRE // 128
DM = 1024
NDM = DM // 128
NH, DH = 16, 64
NP = NH // 2
DFF = 4096
NDH = DFF // 128
EPS = 1e-5
MAGIC = 0x5f3759df
D256 = float(D ** 256)


class Prog:
    def __init__(self):
        self.ops = []
        self._prev_psums = ()
        self._opened = set()

    def psum_scope(self, prev_tokens):
        self._prev_psums = tuple(prev_tokens)
        self._opened = set()

    def add(self, eng, fn, reads=(), writes=(), dma=False):
        writes = list(writes)
        for t in list(writes):
            if t.startswith("p.") and t not in self._opened:
                self._opened.add(t)
                writes += self._prev_psums
        self.ops.append(dict(eng=eng, fn=fn, reads=tuple(reads),
                             writes=tuple(writes), dma=dma, idx=len(self.ops)))

    def emit(self, nc, block, sems, serial=False):
        ops = self.ops
        NSLOT = 8
        dcount = {}
        gcount = 0
        for op in ops:
            if op["dma"]:
                k = dcount.get(op["eng"], 0)
                dcount[op["eng"]] = k + 1
                op["slot"] = k % NSLOT
                op["semval"] = 16 * (k // NSLOT + 1)
                op["inc"] = 16
            elif op["eng"] == "gpsimd" and op["fn"] is not None:
                k = gcount
                gcount += 1
                op["slot"] = k % NSLOT
                op["semval"] = k // NSLOT + 1
                op["inc"] = 1
        last_w = {}
        readers = {}
        for op in ops:
            deps = set()
            if serial and op["idx"] > 0:
                deps.add(op["idx"] - 1)
            for t in op["reads"]:
                if t in last_w:
                    deps.add(last_w[t])
            for t in op["writes"]:
                if t in last_w:
                    deps.add(last_w[t])
                for r in readers.get(t, ()):
                    deps.add(r)
            deps.discard(op["idx"])
            fd = set()
            for d in deps:
                dop = ops[d]
                if (not dop["dma"] and not op["dma"]
                        and dop["eng"] == "tensor" and op["eng"] == "tensor"):
                    continue
                fd.add(d)
            bysrc = {}
            for d in fd:
                dop = ops[d]
                if dop["dma"] or dop["eng"] == "gpsimd":
                    key = (dop["eng"], dop["dma"], dop.get("slot"))
                else:
                    key = (dop["eng"],)
                if key not in bysrc or d > bysrc[key]:
                    bysrc[key] = d
            op["deps"] = sorted(bysrc.values())
            for t in op["reads"]:
                readers.setdefault(t, []).append(op["idx"])
            for t in op["writes"]:
                last_w[t] = op["idx"]
                readers[t] = []
        need_inc = set()
        for op in ops:
            for d in op["deps"]:
                if not ops[d]["dma"]:
                    need_inc.add(d)
        ccount = {}
        for op in ops:
            if op["dma"] or op["eng"] == "gpsimd":
                pass
            elif op["idx"] in need_inc:
                ccount[op["eng"]] = ccount.get(op["eng"], 0) + 1
                op["semval"] = ccount[op["eng"]]
                op["inc"] = 1
            else:
                op["semval"] = None
                op["inc"] = 0
        hw = {}

        def semkey(dop):
            if dop["dma"]:
                return "dma_%s_%d" % (dop["eng"], dop["slot"])
            if dop["eng"] == "gpsimd":
                return "gps_%d" % dop["slot"]
            return "c_" + dop["eng"]

        def make_section(name):
            def section(e):
                for op in ops:
                    if op["eng"] != name:
                        continue
                    for d in op["deps"]:
                        dop = ops[d]
                        sk = semkey(dop)
                        val = dop["semval"]
                        assert val is not None
                        if hw.get((name, sk), -1) >= val:
                            continue
                        e.wait_ge(sems[sk], val)
                        hw[(name, sk)] = val
                    if op["fn"] is not None:
                        if op["dma"] or op["eng"] == "gpsimd":
                            sk = semkey(op)
                            prev = op["semval"] - (16 if op["dma"] else 1)
                            if prev > 0 and hw.get((name, sk), -1) < prev:
                                e.wait_ge(sems[sk], prev)
                                hw[(name, sk)] = prev
                        ins = op["fn"](e)
                        if op["inc"]:
                            ins.then_inc(sems[semkey(op)], op["inc"])
                    else:
                        assert op["inc"] == 0
            return section

        for name in ["tensor", "vector", "scalar", "gpsimd", "sync"]:
            getattr(block, name)(make_section(name))



def _host_consts():
    c = {}
    c["ident"] = np.eye(128, dtype=ml_dtypes.bfloat16)
    c["ocf"] = np.ones((128, 1), dtype=np.float32)
    c["ocb"] = np.ones((128, 1), dtype=ml_dtypes.bfloat16)
    c["orb"] = np.ones((1, 128), dtype=ml_dtypes.bfloat16)
    c["orf"] = np.ones((1, 128), dtype=np.float32)
    mask = np.ones((128, 256), dtype=np.float32)
    tri = (np.arange(128)[None, :] >= np.arange(128)[:, None]).astype(np.float32)
    mask[:, 0:128] = tri
    c["mask"] = mask
    i = np.arange(GRP, dtype=np.float64)
    c["qdec"] = np.tile((D ** i / 8.0).astype(np.float32), (128, 1))
    c["kdec"] = np.tile((D ** (-i)).astype(np.float32), (128, 1))
    p = np.arange(128, dtype=np.float64)
    kus = np.stack([D ** (256 - (128 * s + p)) for s in range(2)], axis=1)
    c["kus"] = kus.astype(np.float32)
    return c



def build_nc(dbg=False, skip_p0=False):
    nc = bass.Bass("TRN2", target_bir_lowering=False)

    xseg = nc.dram_tensor("xseg", [128, (SEG // 128) * DM], F32, kind="ExternalInput")
    xpre = nc.dram_tensor("xpre", [128, NPT * DM], F32, kind="ExternalInput")
    cpre = nc.dram_tensor("cpre", [128, NPT], F32, kind="ExternalInput")
    csum = nc.dram_tensor("csum", [1, 1], F32, kind="ExternalInput")
    Wq = nc.dram_tensor("Wq", [128, NDM * DM], F32, kind="ExternalInput")
    Wk = nc.dram_tensor("Wk", [128, NDM * DM], F32, kind="ExternalInput")
    Wv = nc.dram_tensor("Wv", [128, NDM * DM], F32, kind="ExternalInput")
    Wg = nc.dram_tensor("Wg", [128, NDM * DM], F32, kind="ExternalInput")
    Wo = nc.dram_tensor("Wo", [128, NDM * DM], F32, kind="ExternalInput")
    w1 = nc.dram_tensor("w1", [128, NDM * DFF], F32, kind="ExternalInput")
    w2 = nc.dram_tensor("w2", [128, NDH * DM], F32, kind="ExternalInput")
    b_v = nc.dram_tensor("b_v", [1, DM], F32, kind="ExternalInput")
    b_k = nc.dram_tensor("b_k", [1, DM], F32, kind="ExternalInput")
    b_o = nc.dram_tensor("b_o", [1, DM], F32, kind="ExternalInput")
    b_2 = nc.dram_tensor("b_2", [1, DM], F32, kind="ExternalInput")
    bqT = nc.dram_tensor("bqT", [128, NDM], F32, kind="ExternalInput")
    bkT = nc.dram_tensor("bkT", [128, NDM], F32, kind="ExternalInput")
    bgT = nc.dram_tensor("bgT", [128, NDM], F32, kind="ExternalInput")
    b1T = nc.dram_tensor("b1T", [128, NDH], F32, kind="ExternalInput")
    gmT = nc.dram_tensor("gmT", [128, NDM], F32, kind="ExternalInput")
    btT = nc.dram_tensor("btT", [128, NDM], F32, kind="ExternalInput")
    out = nc.dram_tensor("out", [SEG, DM], F32, kind="ExternalOutput")
    s_scr = nc.dram_tensor("s_scr", [SEG, DM], F32, kind="Internal")
    w1b = nc.dram_tensor("w1b", [128, NDM * DFF], BF16, kind="Internal")
    w2b = nc.dram_tensor("w2b", [128, NDH * DM], BF16, kind="Internal")
    if dbg:
        d_S0 = nc.dram_tensor("d_S0", [128, 512], F32, kind="ExternalOutput")
        d_S1 = nc.dram_tensor("d_S1", [128, 512], F32, kind="ExternalOutput")
        d_xnT = nc.dram_tensor("d_xnT", [128, 16384], BF16, kind="ExternalOutput")
        d_Qs = nc.dram_tensor("d_Qs", [128, 2048], BF16, kind="ExternalOutput")
        d_Ks = nc.dram_tensor("d_Ks", [128, 2048], BF16, kind="ExternalOutput")
        d_V = nc.dram_tensor("d_V", [128, 2048], BF16, kind="ExternalOutput")
        d_Ku = nc.dram_tensor("d_Ku", [128, 2048], BF16, kind="ExternalOutput")
        d_Gt = nc.dram_tensor("d_Gt", [128, 2048], F32, kind="ExternalOutput")
        d_rsb = nc.dram_tensor("d_rsb", [128, 2048], F32, kind="ExternalOutput")
        d_As = nc.dram_tensor("d_As", [128, 2048], BF16, kind="ExternalOutput")
        d_s = nc.dram_tensor("d_s", [SEG, DM], F32, kind="ExternalOutput")
        d_hT = nc.dram_tensor("d_hT", [128, 4096], BF16, kind="ExternalOutput")
        d_Pm = nc.dram_tensor("d_Pm", [128, 256], BF16, kind="ExternalOutput")
        d_Wq = nc.dram_tensor("d_Wq", [128, 8192], BF16, kind="ExternalOutput")
        d_PJ = nc.dram_tensor("d_PJ", [128, 256], F32, kind="ExternalOutput")
        d_xnb = nc.dram_tensor("d_xnb", [128, 1024], BF16, kind="ExternalOutput")
        d_Wk = nc.dram_tensor("d_Wk", [128, 8192], BF16, kind="ExternalOutput")

    hc = _host_consts()
    ident_d = nc.inline_tensor(hc["ident"], name="identc")
    ocf_d = nc.inline_tensor(hc["ocf"], name="ocfc")
    ocb_d = nc.inline_tensor(hc["ocb"], name="ocbc")
    orb_d = nc.inline_tensor(hc["orb"], name="orbc")
    orf_d = nc.inline_tensor(hc["orf"], name="orfc")
    mask_d = nc.inline_tensor(hc["mask"], name="maskc")
    qdec_d = nc.inline_tensor(hc["qdec"], name="qdecc")
    kdec_d = nc.inline_tensor(hc["kdec"], name="kdecc")
    kus_d = nc.inline_tensor(hc["kus"], name="kusc")

    p = Prog()
    A = p.add

    from contextlib import ExitStack
    _st = ExitStack()
    def SB(name, shape, dtype):
        return _st.enter_context(nc.sbuf_tensor(name, shape, dtype))
    if True:
        WB1 = SB("WB1", [128, 32768], BF16)
        WB2 = SB("WB2", [128, 32768], BF16)
        xnT = SB("xnT", [128, 16384], BF16)
        xg01 = SB("xg01", [128, 2048], F32)
        xg23 = SB("xg23", [128, 2048], F32)
        xg0 = xg01[:, 0:1024]
        xg1 = xg01[:, 1024:2048]
        state = SB("state", [128, 512], F32)
        sb16 = SB("sb16", [128, 512], BF16)
        xnb = SB("xnb", [128, 1024], BF16)
        tA2F = SB("tA2F", [128, 256], F32)
        tB2F = SB("tB2F", [128, 256], F32)
        identS = SB("identS", [128, 128], BF16)
        ocfS = SB("ocfS", [128, 1], F32)
        ocbS = SB("ocbS", [128, 1], BF16)
        orbS = SB("orbS", [1, 128], BF16)
        orfS = SB("orfS", [1, 128], F32)
        maskS = SB("maskS", [128, 256], F32)
        qdecS = SB("qdecS", [128, GRP], F32)
        kdecS = SB("kdecS", [128, GRP], F32)
        kusS = SB("kusS", [128, 2], F32)
        cpreS = SB("cpreS", [128, NPT], F32)
        cpreB = SB("cpreB", [128, NPT], BF16)
        csumS = SB("csumS", [1, 1], F32)
        bvB = SB("bvB", [1, DM], BF16)
        bkB = SB("bkB", [1, DM], BF16)
        boB = SB("boB", [1, DM], BF16)
        b2B = SB("b2B", [1, DM], BF16)
        bqTs = SB("bqTs", [128, NDM], F32)
        bkTs = SB("bkTs", [128, NDM], F32)
        bgTs = SB("bgTs", [128, NDM], F32)
        b1Ts = SB("b1Ts", [128, NDH], F32)
        gmTs = SB("gmTs", [128, NDM], F32)
        btTs = SB("btTs", [128, NDM], F32)
        mcolB = SB("mcolB", [128, NDM], BF16)
        vpe = SB("vpe", [128, 4], F32)
        nrY = SB("nrY", [128, 4], F32)
        nrT = SB("nrT", [128, 4], F32)
        bnst = SB("bnst", [128, 24], F32)
        mvc = SB("mvc", [128, 4], F32)
    if True:
        def wb1(c0, n):
            return WB1[:, c0:c0 + n]
        def wb2(c0, n):
            return WB2[:, c0:c0 + n]
        WkS = lambda c0, n: WB1[:, 0 + c0: 0 + c0 + n]
        WvS = lambda c0, n: WB1[:, 8192 + c0: 8192 + c0 + n]
        MS = lambda c0, n: WB1[:, 16384 + c0: 16384 + c0 + n]
        TS = lambda c0, n: WB1[:, 24576 + c0: 24576 + c0 + n]
        WqS = lambda c0, n: WB1[:, 16384 + c0: 16384 + c0 + n]
        WgS = lambda c0, n: WB1[:, 24576 + c0: 24576 + c0 + n]
        WoS = lambda c0, n: WB2[:, 0 + c0: 0 + c0 + n]
        XH = lambda c0, n: WB2[:, 8192 + c0: 8192 + c0 + n]
        zB = WB2[0:1, 8192 + 1024 * 6:8192 + 1024 * 6 + 1024]
        bvsB = WB2[0:1, 8192 + 1024 * 7:8192 + 1024 * 7 + 1024]
        mstage = WB2[:, 8192 + 1024 * 8:8192 + 1024 * 8 + 1024]
        mrowB = WB2[0:1, 8192 + 1024 * 9:8192 + 1024 * 9 + 1024]
        QsS = WB2[:, 8192:10240]
        KsS = WB2[:, 10240:12288]
        VS = WB2[:, 12288:14336]
        KuS = WB2[:, 14336:16384]
        AsS = WB2[:, 16384:18432]
        Pm = [WB2[:, 18432:18688], WB2[:, 18688:18944],
              WB2[:, 18944:19072], WB2[:, 19072:19200]]
        SPARE = WB2[:, 19200:32768].bitcast(F32)
        GtS = SPARE[:, 0:2048]
        rsbS = SPARE[:, 2048:4096]
        sstgS = SPARE[:, 4096:5120]
        sqsbS = SPARE[:, 5120:5376]
        tAS = SPARE[:, 5376:5632]
        tBS = SPARE[:, 5632:5888]
        tA2S = tA2F[:, :]
        tB2S = tB2F[:, :]
        vR = SPARE[0:1, 5888:6144]
        yR = SPARE[0:1, 6144:6400]
        nmrR = SPARE[0:1, 6400:6656]
        nmuR = tAS[0:1, 0:256]
        tR = sqsbS[0:1, 0:256]
        WB2_A_TOKENS = ("b.Qs", "b.Ks", "b.V", "b.Ku", "b.As", "b.Pm0", "b.Pm1",
                        "b.Gt", "b.rsb", "b.sstg", "b.sqsb", "b.tA", "b.tB")

        w1S = lambda c0, n: WB1[:, c0:c0 + n]
        w2S = lambda c0, n: WB2[:, c0:c0 + n]

        for nm, dst, src in [
            ("c.ident", identS, ident_d), ("c.ocf", ocfS, ocf_d),
            ("c.ocb", ocbS, ocb_d),
            ("c.orb", orbS, orb_d),
            ("c.orf", orfS, orf_d), ("c.mask", maskS, mask_d),
            ("c.qdec", qdecS, qdec_d), ("c.kdec", kdecS, kdec_d),
            ("c.kus", kusS, kus_d), ("r.cpre", cpreS, cpre),
            ("r.csum", csumS, csum),
            ("r.bqT", bqTs, bqT), ("r.bkT", bkTs, bkT),
            ("r.bgT", bgTs, bgT), ("r.b1T", b1Ts, b1T),
            ("r.gmT", gmTs, gmT), ("r.btT", btTs, btT),
        ]:
            A("sync", lambda e, d=dst, s=src: e.dma_start(out=d[:, :], in_=s[:, :]),
              writes=[nm], dma=True)
        for nm, dst, src in [
            ("r.cpreB", cpreB, cpre), ("r.bvB", bvB, b_v), ("r.bkB", bkB, b_k),
            ("r.boB", boB, b_o), ("r.b2B", b2B, b_2),
        ]:
            A("gpsimd", lambda e, d=dst, s=src: e.dma_start(out=d[:, :], in_=s[:, :]),
              writes=[nm], dma=True)

        def ln_stats(xbuf, xtok, col):
            cs = (col, col + 1)
            sx = col // 2
            A("scalar", lambda e: e.activation(
                xnb[:, :], xbuf[:, :], AF.Copy,
                accum_out=nsum[:, cs[0]:cs[1]]),
              reads=[xtok], writes=["b.xnb", "b.nsum.%d" % sx])
            A("vector", lambda e: e.tensor_scalar(
                nmu[:, cs[0]:cs[1]], nsum[:, cs[0]:cs[1]], -1.0 / DM, None,
                ALU.mult),
              reads=["b.nsum.%d" % sx], writes=["b.nmu.%d" % sx])
            A("scalar", lambda e: e.activation(
                xnb[:, :], xbuf[:, :], AF.Square, bias=nmu[:, cs[0]:cs[1]],
                accum_out=v1c[:, cs[0]:cs[1]]),
              reads=[xtok, "b.nmu.%d" % sx], writes=["b.xnb", "b.v1c.%d" % sx])

        def ln_nr(c0, ncols):
            cs = (c0, c0 + ncols)
            sx = c0 // 2
            A("vector", lambda e: e.tensor_scalar(
                vpe[:, cs[0]:cs[1]], v1c[:, cs[0]:cs[1]], 1.0 / DM, EPS,
                ALU.mult, ALU.add),
              reads=["b.v1c.%d" % sx], writes=["b.vpe.%d" % sx])
            ln_nr_core(c0, ncols)

        def ln_nr_core(c0, ncols):
            cs = (c0, c0 + ncols)
            sx = c0 // 2
            A("vector", lambda e: e.tensor_scalar(
                nrY.bitcast(I32)[:, cs[0]:cs[1]], vpe.bitcast(I32)[:, cs[0]:cs[1]],
                1, None, ALU.logical_shift_right),
              reads=["b.vpe.%d" % sx], writes=["b.nrY.%d" % sx])
            A("vector", lambda e: e.tensor_scalar(
                nrY.bitcast(I32)[:, cs[0]:cs[1]], nrY.bitcast(I32)[:, cs[0]:cs[1]],
                -1, None, ALU.bitwise_xor),
              reads=["b.nrY.%d" % sx], writes=["b.nrY.%d" % sx])
            A("vector", lambda e: e.tensor_scalar(
                nrY.bitcast(I32)[:, cs[0]:cs[1]], nrY.bitcast(I32)[:, cs[0]:cs[1]],
                MAGIC + 1, None, ALU.add),
              reads=["b.nrY.%d" % sx], writes=["b.nrY.%d" % sx])
            for _ in range(2):
                A("vector", lambda e: e.tensor_tensor(
                    nrT[:, cs[0]:cs[1]], nrY[:, cs[0]:cs[1]], nrY[:, cs[0]:cs[1]],
                    ALU.mult), reads=["b.nrY.%d" % sx], writes=["b.nrT.%d" % sx])
                A("vector", lambda e: e.tensor_tensor(
                    nrT[:, cs[0]:cs[1]], nrT[:, cs[0]:cs[1]], vpe[:, cs[0]:cs[1]],
                    ALU.mult),
                  reads=["b.nrT.%d" % sx, "b.vpe.%d" % sx],
                  writes=["b.nrT.%d" % sx])
                A("vector", lambda e: e.tensor_scalar(
                    nrT[:, cs[0]:cs[1]], nrT[:, cs[0]:cs[1]], -0.5, 1.5,
                    ALU.mult, ALU.add),
                  reads=["b.nrT.%d" % sx], writes=["b.nrT.%d" % sx])
                A("vector", lambda e: e.tensor_tensor(
                    nrY[:, cs[0]:cs[1]], nrY[:, cs[0]:cs[1]], nrT[:, cs[0]:cs[1]],
                    ALU.mult),
                  reads=["b.nrY.%d" % sx, "b.nrT.%d" % sx],
                  writes=["b.nrY.%d" % sx])

        def ln_finish(xbuf, xtok, col, out_ap, out_tok, scale_col=None, extra=()):
            cs = (col, col + 1)
            sx = col // 2
            rstd_tok = "b.nrY.%d" % sx
            rstd_ap = nrY
            if scale_col is not None:
                ap, rtok = scale_col
                A("vector", lambda e: e.tensor_tensor(
                    nrT[:, cs[0]:cs[1]], nrY[:, cs[0]:cs[1]], ap, ALU.mult),
                  reads=["b.nrY.%d" % sx, rtok], writes=["b.nrT.%d" % sx])
                rstd_tok = "b.nrT.%d" % sx
                rstd_ap = nrT
            A("vector", lambda e: e.tensor_scalar(
                out_ap, xbuf[:, :], nmu[:, cs[0]:cs[1]], rstd_ap[:, cs[0]:cs[1]],
                ALU.add, ALU.mult),
              reads=[xtok, "b.nmu.%d" % sx, rstd_tok] + list(extra),
              writes=[out_tok])

        def ln_bn(xbuf, xtok, t2):
            A("vector", lambda e: e.bn_stats(
                bnst[:, 12 * t2:12 * t2 + 6], xbuf[:, 0:512]),
              reads=[xtok], writes=["b.bn.%d" % t2])
            A("vector", lambda e: e.bn_stats(
                bnst[:, 12 * t2 + 6:12 * t2 + 12], xbuf[:, 512:1024]),
              reads=[xtok], writes=["b.bn.%d" % t2])
            A("vector", lambda e: e.bn_aggr(
                mvc[:, 2 * t2:2 * t2 + 2], bnst[:, 12 * t2:12 * t2 + 12]),
              reads=["b.bn.%d" % t2], writes=["b.mv.%d" % t2])

        def ln_bn_vpe(c0):
            mvv = mvc.rearrange("p (t two) -> p t two", two=2)[:, :, 1:2]
            A("vector", lambda e: e.tensor_scalar(
                vpe[:, c0:c0 + 2], mvv, EPS, None, ALU.add),
              reads=["b.mv.0", "b.mv.1"], writes=["b.vpe.%d" % (c0 // 2)])

        def ln_bn_fin(xbuf, xtok, t2, col, out_ap, out_tok, scale_col=None,
                      extra=()):
            cs = (col, col + 1)
            sx = col // 2
            rstd_tok = "b.nrY.%d" % sx
            rstd_ap = nrY
            if scale_col is not None:
                ap, rtok = scale_col
                A("vector", lambda e: e.tensor_tensor(
                    nrT[:, cs[0]:cs[1]], nrY[:, cs[0]:cs[1]], ap, ALU.mult),
                  reads=["b.nrY.%d" % sx, rtok], writes=["b.nrT.%d" % sx])
                rstd_tok = "b.nrT.%d" % sx
                rstd_ap = nrT
            A("vector", lambda e, rstd_ap=rstd_ap: e.tensor_scalar(
                out_ap, xbuf[:, :], mvc[:, 2 * t2:2 * t2 + 1],
                rstd_ap[:, cs[0]:cs[1]], ALU.subtract, ALU.mult),
              reads=[xtok, "b.mv.%d" % t2, rstd_tok] + list(extra),
              writes=[out_tok])

        _p0 = ExitStack()
        pM0 = _p0.enter_context(nc.psum_tensor("pM0", [128, 1024], F32))
        pM1 = _p0.enter_context(nc.psum_tensor("pM1", [128, 1024], F32))
        pMR = _p0.enter_context(nc.psum_tensor("pMR", [1, 1024], F32))
        pTR = _p0.enter_context(nc.psum_tensor("pTR", [128, 512], BF16))
        pSP = _p0.enter_context(nc.psum_tensor("pSP", [128, 64], F32))
        if True:
            def stageE_pre(g):
                A("sync", lambda e, g=g: e.dma_start(
                    out=xg01[:, :], in_=xseg[:, 2048 * g:2048 * g + 2048]),
                  writes=["xg0", "xg1"], dma=True)
                for t in range(2):
                    xb = xg0 if t == 0 else xg1
                    ln_bn(xb, "xg%d" % t, t)
                ln_bn_vpe(0)
                ln_nr_core(0, 2)

            def stageE_fin(g, trp=None):
                gc = GRP * g
                trp = trp if trp is not None else (
                    (pTR2[:, 0:512], "p.SCA"), (pSCBb[:, 0:512], "p.SCB"))
                for t in range(2):
                    xb = xg0 if t == 0 else xg1
                    xtok = "xg%d" % t
                    ln_bn_fin(xb, xtok, t, t, xnb[:, :], "b.xnb")
                    if dbg and g == 0 and t == 0:
                        A("sync", lambda e: e.dma_start(out=d_xnb[:, :], in_=xnb[:, :]),
                          reads=["b.xnb"], writes=["d.out"], dma=True)
                    for c4 in range(2):
                        pT, pTtok = trp[(2 * t + c4) % 2]
                        for bb in range(4):
                            A("tensor", lambda e, c4=c4, bb=bb, pT=pT: e.transpose(
                                pT[:, 128 * bb:128 * bb + 128],
                                xnb[:, 512 * c4 + 128 * bb:512 * c4 + 128 * bb + 128],
                                identS[:, :]),
                              reads=["b.xnb", "c.ident"], writes=[pTtok])
                        A("scalar", lambda e, c4=c4, gc=gc, t=t, pT=pT: e.copy(
                            xnT[:, :].rearrange("p (t n) -> p t n", n=2048)
                               [:, 4 * c4:4 * c4 + 4, gc + 128 * t:gc + 128 * t + 128],
                            pT[:, 0:512].rearrange("p (t n) -> p t n", n=128)),
                          reads=[pTtok], writes=["xnT"])

            def stageE(g, trp=None):
                stageE_pre(g)
                stageE_fin(g, trp=trp)

            def stageM1(g):
                gc = GRP * g
                for it in range(2):
                    for dm in range(NDM):
                        for ns in range(2):
                            A("tensor", lambda e, it=it, dm=dm, ns=ns, gc=gc: e.matmul(
                                pPJ[:, 512 * ns:512 * ns + 512],
                                xnT[:, 2048 * dm + gc + 128 * it:2048 * dm + gc + 128 * it + 128],
                                WvS(1024 * dm + 512 * ns, 512),
                                start=(dm == 0), stop=False, skip_group_check=True),
                              reads=["xnT", "w.v"], writes=["p.PJ0", "p.PJ1"])
                    for ns in range(2):
                        A("tensor", lambda e, ns=ns: e.matmul(
                            pPJ[:, 512 * ns:512 * ns + 512],
                            orbS[0:1, :], bvB[0:1, 512 * ns:512 * ns + 512],
                            start=False, stop=True, skip_group_check=True),
                          reads=["c.orb", "r.bvB"], writes=["p.PJ0", "p.PJ1"])
                    A("scalar", lambda e, it=it: e.copy(
                        VS[:, 1024 * it:1024 * it + 1024], pPJ[:, :]),
                      reads=["p.PJ0", "p.PJ1"], writes=["b.V"])
                for which, wsl, wtok in (("q", WqS, "w.q"), ("k", WkS, "w.k")):
                    for dt in range(NDM):
                        half = (dt % 2) * 512
                        ptok = "p.PJ%d" % (dt % 2)
                        for dm in range(NDM):
                            A("tensor", lambda e, wsl=wsl, dt=dt, dm=dm, half=half, gc=gc: e.matmul(
                                pPJ[:, half:half + GRP],
                                wsl(1024 * dm + 128 * dt, 128),
                                xnT[:, 2048 * dm + gc:2048 * dm + gc + GRP],
                                start=(dm == 0), stop=(dm == NDM - 1),
                                skip_group_check=True),
                              reads=[wtok, "xnT"], writes=[ptok])
                        if which == "q":
                            if dbg and g == 0 and dt == 0:
                                A("scalar", lambda e: e.copy(sqsbS[:, :], pPJ[:, 0:256]),
                                  reads=[ptok], writes=["b.sqsb"])
                                A("sync", lambda e: e.dma_start(out=d_PJ[:, :], in_=sqsbS[:, :]),
                                  reads=["b.sqsb"], writes=["d.out"], dma=True)
                            A("vector", lambda e, dt=dt, half=half: e.scalar_tensor_tensor(
                                QsS[:, GRP * dt:GRP * dt + GRP],
                                pPJ[:, half:half + GRP],
                                bqTs[:, dt:dt + 1], qdecS[:, :],
                                ALU.add, ALU.mult),
                              reads=[ptok, "r.bqT", "c.qdec"], writes=["b.Qs"])
                        elif which == "k":
                            A("vector", lambda e, dt=dt, half=half: e.scalar_tensor_tensor(
                                KsS[:, GRP * dt:GRP * dt + GRP],
                                pPJ[:, half:half + GRP],
                                bkTs[:, dt:dt + 1], kdecS[:, :],
                                ALU.add, ALU.mult),
                              reads=[ptok, "r.bkT", "c.kdec"], writes=["b.Ks"])
                for it in range(2):
                    for q in range(2):
                        pT, pTtok = ((pTR2[:, 0:512], "p.SCA"),
                                     (pSCBb[:, 0:512], "p.SCB"))[q]
                        for hq in range(4):
                            hp = 4 * q + hq
                            A("tensor", lambda e, it=it, hp=hp, hq=hq, pT=pT: e.transpose(
                                pT[:, 128 * hq:128 * hq + 128],
                                KsS[:, 256 * hp + 128 * it:256 * hp + 128 * it + 128],
                                identS[:, :]),
                              reads=["b.Ks", "c.ident"], writes=[pTtok])
                        A("scalar", lambda e, it=it, q=q, pT=pT: e.activation(
                            KuS[:, 1024 * it + 512 * q:1024 * it + 512 * q + 512],
                            pT[:, 0:512], AF.Identity, scale=float(D256)),
                          reads=[pTtok], writes=["b.Ku"])


            def stageM2(g):
                gc = GRP * g
                A("scalar", lambda e: e.copy(sb16[:, :], state[:, :]),
                  reads=["b.state"], writes=["b.sb16"])
                for which, wsl, wtok in (("g", WgS, "w.g"),):
                    for dt in range(NDM):
                        half = (dt % 2) * 512
                        ptok = "p.PJ%d" % (dt % 2)
                        for dm in range(NDM):
                            A("tensor", lambda e, wsl=wsl, dt=dt, dm=dm, half=half, gc=gc: e.matmul(
                                pPJ[:, half:half + GRP],
                                wsl(1024 * dm + 128 * dt, 128),
                                xnT[:, 2048 * dm + gc:2048 * dm + gc + GRP],
                                start=(dm == 0), stop=(dm == NDM - 1),
                                skip_group_check=True),
                              reads=[wtok, "xnT"], writes=[ptok])
                        A("scalar", lambda e, dt=dt, half=half: e.activation(
                            GtS[:, GRP * dt:GRP * dt + GRP],
                            pPJ[:, half:half + GRP],
                            AF.Sigmoid, bias=bgTs[:, dt:dt + 1]),
                          reads=[ptok, "r.bgT"], writes=["b.Gt.%d" % dt])
                for hp in range(NP):
                    if hp % 2 == 0:
                        jb0b, jb0t = pSCB, "p.SCB"
                        jb1b, jb1t, jb1c = pSCA, "p.SCA", 256
                    else:
                        jb0b, jb0t = pLATE, "p.LATE"
                        jb1b, jb1t, jb1c = pS, "p.S", 0
                    for hh in range(2):
                        r0 = 64 * hh
                        jb0ap = jb0b[:, 256 * hh:256 * hh + 256]
                        jb1ap = jb1b[:, jb1c + 128 * hh:jb1c + 128 * hh + 128]
                        A("tensor", lambda e, hp=hp, r0=r0, jb0ap=jb0ap: e.matmul(
                            jb0ap,
                            KsS[r0:r0 + 64, GRP * hp:GRP * hp + 128],
                            QsS[r0:r0 + 64, GRP * hp:GRP * hp + GRP],
                            start=True, stop=True, skip_group_check=True),
                          reads=["b.Ks", "b.Qs"], writes=[jb0t])
                        A("tensor", lambda e, hp=hp, r0=r0, jb1ap=jb1ap: e.matmul(
                            jb1ap,
                            KsS[r0:r0 + 64, GRP * hp + 128:GRP * hp + 256],
                            QsS[r0:r0 + 64, GRP * hp + 128:GRP * hp + 256],
                            start=True, stop=True, skip_group_check=True),
                          reads=["b.Ks", "b.Qs"], writes=[jb1t])
                        A("vector", lambda e, hh=hh, jb0ap=jb0ap: e.tensor_tensor(
                            Pm[hh][:, 0:256], jb0ap, maskS[:, 0:256],
                            ALU.mult),
                          reads=[jb0t, "c.mask"], writes=["b.Pm0%d" % hh])
                        A("vector", lambda e, hh=hh, jb1ap=jb1ap: e.tensor_tensor(
                            Pm[2 + hh][:, 0:128], jb1ap, maskS[:, 0:128],
                            ALU.mult),
                          reads=[jb1t, "c.mask"], writes=["b.Pm1%d" % hh])
                    if dbg and g == 0 and hp == 0:
                        A("sync", lambda e: e.dma_start(
                            out=d_Pm[:, :], in_=Pm[0][:, 0:256]),
                          reads=["b.Pm00"], writes=["d.out"], dma=True)
                    for hh in range(2):
                        h = 2 * hp + hh
                        ro = 64 * hh
                        A("tensor", lambda e, h=h, ro=ro, hh=hh: e.matmul(
                            pRT[ro:ro + 64, 0:256] if ro == 0 else pRT[64:128, 0:256],
                            VS[:, 64 * h:64 * h + 64],
                            Pm[hh][:, 0:256],
                            start=True, stop=False, skip_group_check=True,
                            tile_position=(0, ro)),
                          reads=["b.V", "b.Pm0%d" % hh], writes=["p.RT"])
                        A("tensor", lambda e, h=h, ro=ro, hh=hh: e.matmul(
                            pRT[ro:ro + 64, 128:256] if ro == 0 else pRT[64:128, 128:256],
                            VS[:, 1024 + 64 * h:1024 + 64 * h + 64],
                            Pm[2 + hh][:, 0:128],
                            start=False, stop=False, skip_group_check=True,
                            tile_position=(0, ro)),
                          reads=["b.V", "b.Pm1%d" % hh], writes=["p.RT"])
                        A("tensor", lambda e, h=h, ro=ro, hp=hp: e.matmul(
                            pRT[ro:ro + 64, 0:256] if ro == 0 else pRT[64:128, 0:256],
                            sb16[ro:ro + 64, 64 * hp:64 * hp + 64],
                            QsS[ro:ro + 64, GRP * hp:GRP * hp + GRP],
                            start=False, stop=(ro == 64), skip_group_check=True,
                            tile_position=(ro, ro)),
                          reads=["b.sb16", "b.Qs"], writes=["p.RT"])
                    A("scalar", lambda e, hp=hp: e.copy(
                        rsbS[:, GRP * hp:GRP * hp + GRP], pRT[:, 0:256]),
                      reads=["p.RT"], writes=["b.rsb.%d" % hp])
                    for hh in range(2):
                        h = 2 * hp + hh
                        ro = 64 * hh
                        for it in range(2):
                            A("tensor", lambda e, h=h, ro=ro, it=it, hp=hp, g=g: e.matmul(
                                pSU[ro:ro + 64, 64 * hp:64 * hp + 64] if ro == 0
                                else pSU[64:128, 64 * hp:64 * hp + 64],
                                KuS[:, 1024 * it + 64 * h:1024 * it + 64 * h + 64],
                                VS[:, 1024 * it + 64 * h:1024 * it + 64 * h + 64],
                                start=(it == 0), stop=(it == 1),
                                skip_group_check=True, tile_position=(0, ro)),
                              reads=["b.Ku", "b.V"], writes=["p.SU"])
                A("vector", lambda e: e.scalar_tensor_tensor(
                    state[:, :], state[:, :], D256, pSU[:, :], ALU.mult, ALU.add),
                  reads=["b.state", "p.SU"], writes=["b.state"])
                if dbg and g == 0:
                    A("sync", lambda e: e.dma_start(out=d_Wq[:, :], in_=WqS(0, 8192)),
                      reads=["w.q"], writes=["d.out"], dma=True)
                    A("sync", lambda e: e.dma_start(out=d_Wk[:, :], in_=WkS(0, 8192)),
                      reads=["w.k"], writes=["d.out"], dma=True)
                    for dd, sb_, tok in [(d_Qs, QsS, "b.Qs"), (d_Ks, KsS, "b.Ks"),
                                         (d_V, VS, "b.V"), (d_Ku, KuS, "b.Ku"),
                                         (d_Gt, GtS, "b.Gt"), (d_rsb, rsbS, "b.rsb"),
                                         (d_S1, state, "b.state")]:
                        A("sync", lambda e, dd=dd, sb_=sb_: e.dma_start(
                            out=dd[:, :], in_=sb_[:, 0:dd.shape[1]]),
                          reads=[tok], writes=["d.out"], dma=True)

            def stageS(g):
                gc = GRP * g
                for dt in range(NDM):
                    A("tensor", lambda e, dt=dt: e.matmul(
                        pLATE[0:1, 0:256], ocfS[:, :],
                        rsbS[:, GRP * dt:GRP * dt + GRP],
                        start=(dt == 0), stop=(dt == NDM - 1),
                        skip_group_check=True),
                      reads=["c.ocf", "b.rsb.%d" % dt], writes=["p.LATE"])
                A("vector", lambda e: e.tensor_scalar(
                    nmuR, pLATE[0:1, 0:256], -1.0 / DM, None, ALU.mult),
                  reads=["p.LATE"], writes=["b.tA0"])
                for dt in range(NDM):
                    sqap = AsS[:, 256 * dt:256 * dt + 256]
                    A("scalar", lambda e, dt=dt, sqap=sqap: e.activation(
                        sqap, rsbS[:, GRP * dt:GRP * dt + GRP], AF.Square),
                      reads=["b.rsb.%d" % dt], writes=["b.As.%d" % dt])
                    A("tensor", lambda e, dt=dt, sqap=sqap: e.matmul(
                        pLATE[0:1, 256:512], ocbS[:, :], sqap,
                        start=(dt == 0), stop=(dt == NDM - 1),
                        skip_group_check=True),
                      reads=["c.ocb", "b.As.%d" % dt], writes=["p.LATE"])
                A("vector", lambda e: e.tensor_tensor(
                    tR, nmuR, nmuR, ALU.mult),
                  reads=["b.tA0"], writes=["b.sqsb"])
                A("vector", lambda e: e.scalar_tensor_tensor(
                    vR, pLATE[0:1, 256:512], 1.0 / DM, tR,
                    ALU.mult, ALU.subtract),
                  reads=["p.LATE", "b.sqsb"], writes=["b.vR"])
                A("vector", lambda e: e.tensor_scalar(
                    vR, vR, EPS, None, ALU.add),
                  reads=["b.vR"], writes=["b.vR"])
                A("vector", lambda e: e.tensor_scalar(
                    yR.bitcast(I32), vR.bitcast(I32), 1, None,
                    ALU.logical_shift_right), reads=["b.vR"], writes=["b.yR"])
                A("vector", lambda e: e.tensor_scalar(
                    yR.bitcast(I32), yR.bitcast(I32), -1,
                    None, ALU.bitwise_xor), reads=["b.yR"], writes=["b.yR"])
                A("vector", lambda e: e.tensor_scalar(
                    yR.bitcast(I32), yR.bitcast(I32), MAGIC + 1,
                    None, ALU.add), reads=["b.yR"], writes=["b.yR"])
                for _ in range(2):
                    A("vector", lambda e: e.tensor_tensor(
                        tR, yR, yR, ALU.mult),
                      reads=["b.yR"], writes=["b.sqsb"])
                    A("vector", lambda e: e.tensor_tensor(
                        tR, tR, vR, ALU.mult),
                      reads=["b.sqsb", "b.vR"], writes=["b.sqsb"])
                    A("vector", lambda e: e.tensor_scalar(
                        tR, tR, -0.5, 1.5, ALU.mult, ALU.add),
                      reads=["b.sqsb"], writes=["b.sqsb"])
                    A("vector", lambda e: e.tensor_tensor(
                        yR, yR, tR, ALU.mult),
                      reads=["b.yR", "b.sqsb"], writes=["b.yR"])
                A("vector", lambda e: e.tensor_tensor(
                    nmrR, nmuR, yR, ALU.mult),
                  reads=["b.tA0", "b.yR"], writes=["b.nmrR"])

            def stageL(g):
                gc = GRP * g
                A("tensor", lambda e: e.matmul(
                    pLATE[:, 0:256], orfS[0:1, :], yR,
                    start=True, stop=True, skip_group_check=True),
                  reads=["c.orf", "b.yR"], writes=["p.LATE"])
                A("tensor", lambda e: e.matmul(
                    pLATE[:, 256:512], orfS[0:1, :], nmrR,
                    start=True, stop=True, skip_group_check=True),
                  reads=["c.orf", "b.nmrR"], writes=["p.LATE"])
                for dt in range(NDM):
                    tA_ = tAS if dt % 2 == 0 else tA2S
                    tB_ = tBS if dt % 2 == 0 else tB2S
                    tat = "b.tA%d" % (dt % 2)
                    tbt = "b.tB%d" % (dt % 2)
                    A("vector", lambda e, dt=dt, tA_=tA_: e.tensor_tensor(
                        tA_[:, :], rsbS[:, GRP * dt:GRP * dt + GRP], pLATE[:, 0:256],
                        ALU.mult), reads=["b.rsb.%d" % dt, "p.LATE"], writes=[tat])
                    A("vector", lambda e, dt=dt, tA_=tA_, tB_=tB_: e.tensor_tensor(
                        tB_[:, :], tA_[:, :], pLATE[:, 256:512], ALU.add),
                      reads=[tat, "p.LATE"], writes=[tbt])
                    A("vector", lambda e, dt=dt, tA_=tA_, tB_=tB_: e.tensor_scalar(
                        tA_[:, :], tB_[:, :], gmTs[:, dt:dt + 1], btTs[:, dt:dt + 1],
                        ALU.mult, ALU.add),
                      reads=[tbt, "r.gmT", "r.btT"], writes=[tat])
                    A("gpsimd", lambda e, dt=dt, tA_=tA_: e.tensor_tensor(
                        AsS[:, GRP * dt:GRP * dt + GRP], tA_[:, :],
                        GtS[:, GRP * dt:GRP * dt + GRP], ALU.mult),
                      reads=[tat, "b.Gt.%d" % dt], writes=["b.As.%d" % dt])
                if dbg and g == 0:
                    A("sync", lambda e: e.dma_start(out=d_As[:, :], in_=AsS[:, :]),
                      reads=["b.As.%d" % d_ for d_ in range(NDM)], writes=["d.out"], dma=True)
                for it in range(2):
                    for ns in range(2):
                        for dt in range(NDM):
                            A("tensor", lambda e, it=it, dt=dt, ns=ns: e.matmul(
                                pS[:, :],
                                AsS[:, GRP * dt + 128 * it:GRP * dt + 128 * it + 128],
                                WoS(1024 * dt + 512 * ns, 512),
                                start=(dt == 0), stop=False, skip_group_check=True),
                              reads=["b.As.%d" % dt, "w.o"], writes=["p.S"])
                        A("tensor", lambda e, ns=ns: e.matmul(
                            pS[:, :],
                            orbS[0:1, :], boB[0:1, 512 * ns:512 * ns + 512],
                            start=False, stop=True, skip_group_check=True),
                          reads=["c.orb", "r.boB"], writes=["p.S"])
                        A("scalar", lambda e, ns=ns: e.copy(
                            sstgS[:, 512 * ns:512 * ns + 512], pS[:, :]),
                          reads=["p.S"], writes=["b.sstg"])
                    A("gpsimd", lambda e, gc=gc, it=it: e.dma_start(
                        out=s_scr[gc + 128 * it:gc + 128 * it + 128, :],
                        in_=sstgS[:, :]),
                      reads=["b.sstg"], writes=["s_scr.%d.%d" % (g, it)], dma=True)

            if skip_p0:
                A("vector", lambda e: e.memset(state[:, :], 0.0),
                  writes=["b.state"])
            if not skip_p0:
                for o in range(3):
                    for jp in range(4):
                        pr = 4 * o + jp
                        jt0 = 2 * pr
                        buf = xg01 if pr % 2 == 0 else xg23
                        btok = ["xg0", "xg1"] if pr % 2 == 0 else ["xg2", "xg3"]
                        c0 = jt0 % 4
                        A("sync", lambda e, jt0=jt0, buf=buf: e.dma_start(
                            out=buf[:, :],
                            in_=xpre[:, 1024 * jt0:1024 * jt0 + 2048]),
                          writes=btok, dma=True)
                        for t2 in range(2):
                            ln_bn(buf[:, 1024 * t2:1024 * t2 + 1024],
                                  btok[t2], t2)
                        ln_bn_vpe(c0)
                        ln_nr_core(c0, 2)
                        for t2 in range(2):
                            jt = jt0 + t2
                            ln_bn_fin(buf[:, 1024 * t2:1024 * t2 + 1024],
                                      btok[t2], t2, c0 + t2,
                                      XH(1024 * jt, 1024), "xnh.%d" % jt,
                                      scale_col=(cpreS[:, jt:jt + 1], "r.cpre"))
                            for ns in range(2):
                                A("tensor", lambda e, jt=jt, ns=ns: e.matmul(
                                    pMR[0:1, 512 * ns:512 * ns + 512],
                                    cpreB[:, jt:jt + 1],
                                    XH(1024 * jt + 512 * ns, 512),
                                    start=(jt == 0), stop=(jt == NPT - 1),
                                    skip_group_check=True),
                                  reads=["r.cpreB", "xnh.%d" % jt],
                                  writes=["p.MR"])
                    if o == 0:
                        A("gpsimd", lambda e: e.dma_start(
                            out=WkS(0, 8192), in_=Wk.ap()),
                          writes=["w.k"], dma=True)
                        A("gpsimd", lambda e: e.dma_start(
                            out=WvS(0, 8192), in_=Wv.ap()),
                          writes=["w.v"], dma=True)
                    for at in range(NDM):
                        pM = pM0 if at % 2 == 0 else pM1
                        ptok = "p.M%d" % (at % 2)
                        if o > 0:
                            for ns in range(2):
                                A("tensor", lambda e, at=at, ns=ns, pM=pM: e.matmul(
                                    pM[:, 512 * ns:512 * ns + 512],
                                    identS[:, :],
                                    MS(1024 * at + 512 * ns, 512),
                                    start=True, stop=False,
                                    skip_group_check=True),
                                  reads=["b.M.%d" % at, "c.ident"],
                                  writes=[ptok])
                        for j in range(8):
                            jt = 8 * o + j
                            for ns in range(2):
                                A("tensor", lambda e, jt=jt, at=at, ns=ns, pM=pM, o=o: e.matmul(
                                    pM[:, 512 * ns:512 * ns + 512],
                                    XH(1024 * jt + 128 * at, 128),
                                    XH(1024 * jt + 512 * ns, 512),
                                    start=(jt % 8 == 0 and o == 0),
                                    stop=(jt % 8 == 7),
                                    skip_group_check=True),
                                  reads=["xnh.%d" % jt], writes=[ptok])
                        A("scalar", lambda e, at=at, pM=pM: e.copy(
                            MS(1024 * at, 1024), pM[:, :]),
                          reads=[ptok], writes=["b.M.%d" % at])
                A("vector", lambda e: e.tensor_copy(mrowB, pMR[0:1, :]),
                  reads=["p.MR"], writes=["r.mrow", "xnh.9"])
                A("vector", lambda e: e.tensor_copy(mstage[0:1, 0:1024], pMR[0:1, :]),
                  reads=["p.MR"], writes=["r.mst", "xnh.8"])
                for t in range(NDM):
                    A("tensor", lambda e, t=t: e.transpose(
                        pTR[:, 0:128], mstage[:, 128 * t:128 * t + 128],
                        identS[:, :]),
                      reads=["r.mst", "xnh.8", "c.ident"], writes=["p.TR"])
                    A("scalar", lambda e, t=t: e.copy(
                        mcolB[:, t:t + 1], pTR[:, 0:1]),
                      reads=["p.TR"], writes=["r.mcol"])
                for at in range(NDM):
                    pM = pM0 if at % 2 == 0 else pM1
                    ptok = "p.M%d" % (at % 2)
                    for bt in range(NDM):
                        for ns in range(2):
                            A("tensor", lambda e, at=at, bt=bt, ns=ns, pM=pM: e.matmul(
                                pM[:, 512 * ns:512 * ns + 512],
                                MS(1024 * bt + 128 * at, 128),
                                WvS(1024 * bt + 512 * ns, 512),
                                start=(bt == 0), stop=False,
                                skip_group_check=True),
                              reads=["b.M.%d" % bt, "w.v"], writes=[ptok])
                    for ns in range(2):
                        A("tensor", lambda e, at=at, ns=ns, pM=pM: e.matmul(
                            pM[:, 512 * ns:512 * ns + 512],
                            mrowB[0:1, 128 * at:128 * at + 128],
                            bvB[0:1, 512 * ns:512 * ns + 512],
                            start=False, stop=True, skip_group_check=True),
                          reads=["r.mrow", "r.bvB"], writes=[ptok])
                    A("scalar", lambda e, at=at, pM=pM: e.copy(
                        TS(1024 * at, 1024), pM[:, :]),
                      reads=[ptok], writes=["b.T"])
            A("gpsimd", lambda e: e.dma_start(out=WqS(0, 8192), in_=Wq.ap()),
              writes=["w.q"] + ["b.M.%d" % at_ for at_ in range(NDM)],
              dma=True)
            stageE(0, trp=((pTR, "p.TR"), (pTR, "p.TR")))
            if not skip_p0:
                A("vector", lambda e: e.tensor_scalar(
                    bvsB, bvB[0:1, :], csumS[0:1, 0:1], None, ALU.mult),
                  reads=["r.bvB", "r.csum"], writes=["r.bvs", "xnh.7"])
                for at in range(NDM):
                    for ns in range(2):
                        A("tensor", lambda e, at=at, ns=ns: e.matmul(
                            pMR[0:1, 512 * ns:512 * ns + 512],
                            mcolB[:, at:at + 1],
                            WvS(1024 * at + 512 * ns, 512),
                            start=(at == 0), stop=False, skip_group_check=True),
                          reads=["r.mcol", "w.v"], writes=["p.MR"])
                for ns in range(2):
                    A("tensor", lambda e, ns=ns: e.matmul(
                        pMR[0:1, 512 * ns:512 * ns + 512],
                        orbS[0:1, 0:1], bvsB[0:1, 512 * ns:512 * ns + 512],
                        start=False, stop=True, skip_group_check=True),
                      reads=["c.orb", "r.bvs"], writes=["p.MR"])
                A("vector", lambda e: e.tensor_copy(zB, pMR[0:1, :]),
                  reads=["p.MR"], writes=["r.zB", "xnh.6"])
            stageE(1, trp=((pTR, "p.TR"), (pTR, "p.TR")))
            if not skip_p0:
                for hp in range(NP):
                    for hh in range(2):
                        h = 2 * hp + hh
                        ro = 64 * hh
                        for at in range(NDM):
                            A("tensor", lambda e, h=h, ro=ro, at=at: e.matmul(
                                pSP[ro:ro + 64, 0:64] if ro == 0 else pSP[64:128, 0:64],
                                WkS(1024 * at + 64 * h, 64),
                                TS(1024 * at + 64 * h, 64),
                                start=(at == 0), stop=False,
                                skip_group_check=True, tile_position=(0, ro)),
                              reads=["w.k", "b.T"], writes=["p.SP"])
                        A("tensor", lambda e, h=h, ro=ro: e.matmul(
                            pSP[ro:ro + 64, 0:64] if ro == 0 else pSP[64:128, 0:64],
                            bkB[0:1, 64 * h:64 * h + 64],
                            zB[0:1, 64 * h:64 * h + 64],
                            start=False, stop=(ro == 64), skip_group_check=True,
                            tile_position=(0, ro)),
                          reads=["r.bkB", "r.zB"], writes=["p.SP"])
                    A("vector", lambda e, hp=hp: e.tensor_copy(
                        state[:, 64 * hp:64 * hp + 64], pSP[:, 0:64]),
                      reads=["p.SP"], writes=["b.state"])
            if skip_p0:
                A("gpsimd", lambda e: e.dma_start(out=WkS(0, 8192), in_=Wk.ap()),
                  writes=["w.k"], dma=True)
                A("gpsimd", lambda e: e.dma_start(out=WvS(0, 8192), in_=Wv.ap()),
                  writes=["w.v"], dma=True)
            A("gpsimd", lambda e: e.dma_start(out=WgS(0, 8192), in_=Wg.ap()),
              writes=["w.g", "b.T"], dma=True)
            A("gpsimd", lambda e: e.dma_start(out=WoS(0, 8192), in_=Wo.ap()),
              writes=["w.o"], dma=True)
            A("gpsimd", lambda e: e.dma_start(out=w1b[:, :], in_=w1.ap()),
              writes=["w.1hbm"], dma=True)
            A("gpsimd", lambda e: e.dma_start(out=w2b[:, :], in_=w2.ap()),
              writes=["w.2hbm"], dma=True)
            if dbg:
                A("sync", lambda e: e.dma_start(out=d_S0[:, :], in_=state[:, :]),
                  reads=["b.state"], writes=["d.out"], dma=True)
        _p0.close()

        p.psum_scope(["p.M0", "p.M1", "p.MR", "p.TR", "p.SP"])
        _pa = ExitStack()
        pPJ = _pa.enter_context(nc.psum_tensor("pPJ", [128, 1024], F32))
        pSCA = _pa.enter_context(nc.psum_tensor("pSCA", [128, 512], F32))
        pSCB = _pa.enter_context(nc.psum_tensor("pSCB", [128, 512], F32))
        pRT = _pa.enter_context(nc.psum_tensor("pRT", [128, 512], F32))
        pSU = _pa.enter_context(nc.psum_tensor("pSU", [128, 512], F32))
        pLATE = _pa.enter_context(nc.psum_tensor("pLATE", [128, 512], F32))
        pS = _pa.enter_context(nc.psum_tensor("pS", [128, 512], F32))
        pTR2 = pSCA.bitcast(BF16)
        pSCBb = pSCB.bitcast(BF16)
        if True:
            stageM1(0)
            for g in range(NG):
                stageM2(g)
                if g + 2 < NG:
                    stageE_pre(g + 2)
                stageS(g)
                if g + 1 < NG:
                    stageM1(g + 1)
                if g + 2 < NG:
                    stageE_fin(g + 2)
                stageL(g)
        _pa.close()
        if dbg:
            A("sync", lambda e: e.dma_start(out=d_xnT[:, :], in_=xnT[:, :]),
              reads=["xnT"], writes=["d.out"], dma=True)
            A("sync", lambda e: e.dma_start(out=d_s[:, :], in_=s_scr[:, :]),
              reads=["s_scr.%d.%d" % (g_, i_) for g_ in range(NG) for i_ in range(2)],
              writes=["d.out"], dma=True)
        p.psum_scope(["p.PJ0", "p.PJ1", "p.SCA", "p.SCB", "p.RT", "p.SU",
                      "p.LATE", "p.S"])
        A("sync", lambda e: e.dma_start(out=w1S(0, 32768), in_=w1b.ap()),
          reads=["w.1hbm"],
          writes=["w.1", "w.k", "w.v", "w.q", "w.g"]
                 + ["b.M.%d" % at_ for at_ in range(NDM)] + ["b.T"], dma=True)
        A("scalar", lambda e: e.dma_start(out=w2S(0, 32768), in_=w2b.ap()),
          reads=["w.2hbm"],
          writes=("w.2", "w.o", "r.mst", "r.mrow", "r.zB", "r.bvs")
                 + tuple("xnh.%d" % j_ for j_ in range(NPT)) + WB2_A_TOKENS,
          dma=True)

        _p3 = ExitStack()
        hT = _p3.enter_context(nc.sbuf_tensor("hT", [128, 4096], BF16))
        sld0 = _p3.enter_context(nc.sbuf_tensor("sld", [128, 1024], F32))
        sld1 = sld0
        scrA = xg23[:, 1024:2048]
        scrB = xg23[:, 0:1024]
        pH0 = _p3.enter_context(nc.psum_tensor("pH0", [128, 1024], F32))
        pH1 = _p3.enter_context(nc.psum_tensor("pH1", [128, 1024], F32))
        pY0 = _p3.enter_context(nc.psum_tensor("pY0", [128, 1024], F32))
        pY1 = _p3.enter_context(nc.psum_tensor("pY1", [128, 1024], F32))
        if True:
            nout = 0
            for g in range(NG):
                gc = GRP * g
                for half in range(2):
                    for q in range(4):
                        pH = pH0 if q % 2 == 0 else pH1
                        htok = "p.H%d" % (q % 2)
                        for sub in range(4):
                            dht = 16 * half + 4 * q + sub
                            for dm in range(NDM):
                                A("tensor", lambda e, dht=dht, dm=dm, sub=sub, pH=pH, gc=gc: e.matmul(
                                    pH[:, 256 * sub:256 * sub + 256],
                                    w1S(4096 * dm + 128 * dht, 128),
                                    xnT[:, 2048 * dm + gc:2048 * dm + gc + GRP],
                                    start=(dm == 0),
                                    stop=(dm == NDM - 1), skip_group_check=True),
                                  reads=["w.1", "xnT"], writes=[htok])
                        for sub in range(4):
                            dht = 16 * half + 4 * q + sub
                            lt = 4 * q + sub
                            A("scalar", lambda e, dht=dht, lt=lt, sub=sub, pH=pH: e.activation(
                                hT[:, 256 * lt:256 * lt + 256],
                                pH[:, 256 * sub:256 * sub + 256],
                                AF.Gelu, bias=b1Ts[:, dht:dht + 1]),
                              reads=[htok, "r.b1T"], writes=["b.hT.%d" % lt])
                    if dbg and g == 0 and half == 0:
                        A("sync", lambda e: e.dma_start(out=d_hT[:, :], in_=hT[:, :]),
                          reads=["b.hT.%d" % l_ for l_ in range(16)], writes=["d.out"], dma=True)
                    for it in range(2):
                        pY = pY0 if it % 2 == 0 else pY1
                        ytok = "p.Y%d" % (it % 2)
                        for lt in range(16):
                            dht = 16 * half + lt
                            for ns in range(2):
                                A("tensor", lambda e, lt=lt, dht=dht, ns=ns, it=it, pY=pY, half=half: e.matmul(
                                    pY[:, 512 * ns:512 * ns + 512],
                                    hT[:, 256 * lt + 128 * it:256 * lt + 128 * it + 128],
                                    w2S(1024 * dht + 512 * ns, 512),
                                    start=(lt == 0 and half == 0), stop=False,
                                    skip_group_check=True),
                                  reads=["b.hT.%d" % lt, "w.2"], writes=[ytok])
                for it in range(2):
                    pY = pY0 if it % 2 == 0 else pY1
                    ytok = "p.Y%d" % (it % 2)
                    xb = xg0 if it % 2 == 0 else xg1
                    xtok = "xg%d" % (it % 2)
                    sb = sld0 if it % 2 == 0 else sld1
                    stok = "b.sld"
                    A("sync", lambda e, g=g, it=it, xb=xb: e.dma_start(
                        out=xb[:, :],
                        in_=xseg[:, 1024 * (2 * g + it):1024 * (2 * g + it) + 1024]),
                      writes=[xtok], dma=True)
                    A("sync", lambda e, gc=gc, it=it, sb=sb: e.dma_start(
                        out=sb[:, :], in_=s_scr[gc + 128 * it:gc + 128 * it + 128, :]),
                      reads=["s_scr.%d.%d" % (g, it)], writes=[stok], dma=True)
                    for ns in range(2):
                        A("tensor", lambda e, ns=ns, pY=pY: e.matmul(
                            pY[:, 512 * ns:512 * ns + 512],
                            orbS[0:1, :], b2B[0:1, 512 * ns:512 * ns + 512],
                            start=False, stop=True, skip_group_check=True),
                          reads=["c.orb", "r.b2B"], writes=[ytok])
                    A("scalar", lambda e, pY=pY: e.activation(
                        scrA[:, :], pY[:, :], AF.Tanh, scale=0.5),
                      reads=[ytok], writes=["b.scrA"])
                    A("vector", lambda e, pY=pY: e.scalar_tensor_tensor(
                        scrB[:, :], pY[:, :], 0.5, scrA[:, :], ALU.mult, ALU.mult),
                      reads=[ytok, "b.scrA"], writes=["b.scrB"])
                    A("vector", lambda e, pY=pY: e.scalar_tensor_tensor(
                        scrA[:, :], pY[:, :], 0.5, scrB[:, :], ALU.mult, ALU.add),
                      reads=[ytok, "b.scrB"], writes=["b.scrA"])
                    A("gpsimd", lambda e, sb=sb: e.tensor_tensor(
                        scrB[:, :], sb[:, :], scrA[:, :], ALU.mult),
                      reads=[stok, "b.scrA"], writes=["b.scrB"])
                    A("gpsimd", lambda e, sb=sb, xb=xb: e.tensor_tensor(
                        sb[:, :], xb[:, :], scrB[:, :], ALU.add),
                      reads=[xtok, "b.scrB"], writes=[stok])
                    A("gpsimd", lambda e, gc=gc, it=it, sb=sb: e.dma_start(
                        out=out[gc + 128 * it:gc + 128 * it + 128, :], in_=sb[:, :]),
                      reads=[stok], writes=["d.out.%d" % nout], dma=True)
                    nout += 1
            A("scalar", None, reads=["d.out.%d" % i for i in range(nout)])

        _p3.close()
        def SEM(nm):
            return _st.enter_context(nc.semaphore(nm))
        sems = {"c_tensor": SEM("sem_t"), "c_vector": SEM("sem_v"),
                "c_scalar": SEM("sem_s"), "c_gpsimd": SEM("sem_g"),
                "c_sync": SEM("sem_sy")}
        for qq in ("sync", "gpsimd", "scalar"):
            for sl in range(8):
                sems["dma_%s_%d" % (qq, sl)] = SEM("dq_%s_%d" % (qq, sl))
        for sl in range(8):
            sems["gps_%d" % sl] = SEM("gpc_%d" % sl)
        with nc.Block() as block:
            p.emit(nc, block, sems, serial=bool(globals().get("SERIAL", False)))
        _st.close()
    return nc



def _pm(a, nt, n):
    return np.ascontiguousarray(
        np.asarray(a, np.float32).reshape(nt, 128, n)
        .transpose(1, 0, 2).reshape(128, nt * n))


def shared_inputs(inputs):
    f32 = np.float32

    def row(v):
        return np.ascontiguousarray(np.asarray(v, f32).reshape(1, -1))

    def colT(v, nt):
        return np.ascontiguousarray(np.asarray(v, f32).reshape(nt, 128).T)

    return dict(
        Wq=_pm(inputs["L0_attn_Wq"], NDM, DM),
        Wk=_pm(inputs["L0_attn_Wk"], NDM, DM),
        Wv=_pm(inputs["L0_attn_Wv"], NDM, DM),
        Wg=_pm(inputs["L0_attn_Wg"], NDM, DM),
        Wo=_pm(inputs["L0_attn_Wo"], NDM, DM),
        w1=_pm(inputs["L0_ffn_w1"], NDM, DFF),
        w2=_pm(inputs["L0_ffn_w2"], NDH, DM),
        b_v=row(inputs["L0_attn_bv"]), b_k=row(inputs["L0_attn_bk"]),
        b_o=row(inputs["L0_attn_bo"]), b_2=row(inputs["L0_ffn_b2"]),
        bqT=colT(inputs["L0_attn_bq"], NDM), bkT=colT(inputs["L0_attn_bk"], NDM),
        bgT=colT(inputs["L0_attn_bg"], NDM), b1T=colT(inputs["L0_ffn_b1"], NDH),
        gmT=colT(inputs["L0_attn_gamma"], NDM), btT=colT(inputs["L0_attn_beta"], NDM),
    )


def core_inputs(inputs, core, shared=None):
    f32 = np.float32
    x = np.asarray(inputs["L0_attn_x"])
    b, seg = core // 4, core % 4
    t0 = SEG * seg
    xseg = np.ascontiguousarray(x[b, t0:t0 + SEG], f32)
    xpre = np.zeros((PRE, DM), f32)
    lo = max(0, t0 - PRE)
    if t0 > 0:
        xpre[PRE - (t0 - lo):] = x[b, lo:t0]
    w = np.arange(PRE, dtype=np.float64)
    c = D ** (PRE - w)
    c[w < PRE - t0] = 0.0
    csqrt = np.sqrt(c)
    cpre_l = np.ascontiguousarray(csqrt.reshape(NPT, 128).T.astype(f32))
    csum_l = np.array([[c.sum()]], dtype=f32)
    m = dict(shared if shared is not None else shared_inputs(inputs))
    m.update(xseg=_pm(xseg, SEG // 128, DM), xpre=_pm(xpre, NPT, DM),
             cpre=cpre_l, csum=csum_l)
    return m


def kernel(**inputs):
    shared = shared_inputs(inputs)
    in_maps = [core_inputs(inputs, core, shared) for core in range(8)]

    nc = build_nc()
    res = run_bass_kernel_spmd(nc, in_maps, core_ids=list(range(8)))
    shards = [res.results[i]["out"] for i in range(8)]
    full = np.concatenate(shards, axis=0).reshape(2, 8192, 1024)
    return np.ascontiguousarray(full.astype(np.float32))


if __name__ == "__main__":
    rng = np.random.default_rng(0)
    ins = {
        "L0_attn_x": rng.standard_normal((2, 8192, 1024)).astype(np.float32),
    }
    for nm in ["Wq", "Wk", "Wv", "Wg", "Wo"]:
        ins["L0_attn_" + nm] = (rng.standard_normal((1024, 1024)) * 0.02).astype(np.float32)
    for nm in ["bq", "bk", "bv", "bg", "bo", "gamma", "beta"]:
        ins["L0_attn_" + nm] = rng.standard_normal(1024).astype(np.float32) * 0.02
    ins["L0_attn_gamma"] += 1.0
    ins["L0_ffn_w1"] = (rng.standard_normal((1024, 4096)) / 32).astype(np.float32)
    ins["L0_ffn_b1"] = (rng.standard_normal(4096) * 0.02).astype(np.float32)
    ins["L0_ffn_w2"] = (rng.standard_normal((4096, 1024)) / 64).astype(np.float32)
    ins["L0_ffn_b2"] = (rng.standard_normal(1024) * 0.02).astype(np.float32)
    o = kernel(**ins)
    print("kernel ran", o.shape, o.dtype)
